# version 04
# baseline (speedup 1.0000x reference)
import contextlib
import numpy as np
import concourse.bass as bass
import concourse.mybir as mybir
from concourse.bass_utils import run_bass_kernel_spmd

F32 = mybir.dt.float32
BF16 = mybir.dt.bfloat16
U8 = mybir.dt.uint8
I32 = mybir.dt.int32
AF = mybir.ActivationFunctionType
ALU = mybir.AluOpType
AX = mybir.AxisListType

B, S, H, D, NI = 2, 8192, 4096, 1024, 11008
TOWN = 2048
NCORE = 8
SOFT_SCALE = 0.03125
EPS = 1e-5
DQ_SLABS = [(0, 2048), (2048, 2048), (4096, 2048), (6144, 2048), (8192, 2048), (10240, 768)]
DQ_SLABS_D = [(0, 2048), (2048, 2048)]


class SemT:
    def __init__(self, nc, es, name):
        self.h = es.enter_context(nc.semaphore(name))
        self.tot = 0

    def inc(self, instr, n=16):
        instr.then_inc(self.h, n)
        self.tot += n
        return instr


class Ctx:
    def __init__(self, nc, es):
        self.nc = nc
        self.es = es
        self.bar = SemT(nc, es, "bar")
        self.sems = {}
        self.chain_sems = {}

    def sem(self, name):
        s = SemT(self.nc, self.es, name)
        self.sems[name] = s
        return s

    def chain(self, eng, ins):
        key = ins.ins.engine if hasattr(ins, "ins") else None
        name = f"ch_{key}"
        if name not in self.chain_sems:
            self.chain_sems[name] = SemT(self.nc, self.es, name)
        s = self.chain_sems[name]
        s.inc(ins, 1)
        eng.wait_ge(s.h, s.tot)


ALL_ENGINES = ("tensor", "vector", "scalar", "gpsimd", "sync")


def run_phase(ctx, name, emitters):
    nc = ctx.nc
    base = ctx.bar.tot
    ctx.bar.tot += len(ALL_ENGINES)
    tgt = ctx.bar.tot

    with nc.Block(name) as blk:
        for ename in ALL_ENGINES:
            fn = emitters.get(ename)

            def body(eng, fn=fn):
                if fn is not None:
                    fn(eng)
                eng.drain().then_inc(ctx.bar.h, 1)
                eng.wait_ge(ctx.bar.h, tgt)
            getattr(blk, ename)(body)




def emit_P0(ctx, io, bufs_es):
    nc = ctx.nc
    E = bufs_es.enter_context
    W = 2048
    s_pi = [E(nc.sbuf_tensor(f"p_pi{i}", [128, W], I32)) for i in range(2)]
    s_sb = [E(nc.sbuf_tensor(f"p_sb{i}", [128, W], BF16)) for i in range(2)]
    s_zb = [E(nc.sbuf_tensor(f"p_zb{i}", [128, W], BF16)) for i in range(2)]
    s_ti = E(nc.sbuf_tensor("p_ti", [128, W // 2], I32))
    s_wh = [E(nc.sbuf_tensor(f"p_wh{i}", [128, W], BF16)) for i in range(2)]
    s_wl = [E(nc.sbuf_tensor(f"p_wl{i}", [128, W], BF16)) for i in range(2)]

    ld = [ctx.sem("p_ld0"), ctx.sem("p_ld1")]
    dq = ctx.sem("p_dq")
    wo = [ctx.sem("p_wo0"), ctx.sem("p_wo1")]

    jobs = [
        (io["i_pg"], io["i_sbg"], io["i_zbg"], io["sc_wg"], 16, 16, DQ_SLABS, False),
        (io["i_pu"], io["i_sbu"], io["i_zbu"], io["sc_wu"], 16, 16, DQ_SLABS, False),
        (io["i_pd"], io["i_sbd"], io["i_zbd"], io["sc_wd"], 43, 43, DQ_SLABS_D, True),
    ]
    slabs = []
    for (pk, sbv, zbv, outw, njt, lofs, slablist, is_down) in jobs:
        for jt in range(njt):
            for (c0, w) in slablist:
                slabs.append((pk, sbv, zbv, outw, jt, lofs, c0, w, is_down))

    def emit_wout(g, si):
        (pk, sbv, zbv, outw, jt, lofs, c0, w, is_down) = slabs[si]
        nt, t0 = w // 128, c0 // 128
        g.wait_ge(dq.h, si + 1)
        for src, rt in ((s_wh[si % 2], jt), (s_wl[si % 2], lofs + jt)):
            dst = outw[t0:t0 + nt, rt, :, :].rearrange("a p c -> p a c")
            wo[si % 2].inc(g.dma_start(out=dst,
                               in_=src[:, 0:w].rearrange("p (a c) -> p a c", c=128)), 16)

    def body_gpsimd(g):
        for si, (pk, sbv, zbv, outw, jt, lofs, c0, w, is_down) in enumerate(slabs):
            if si >= 2:
                g.wait_ge(dq.h, si - 1)
            r0 = jt * 128
            g0 = jt * 4
            ld[si % 2].inc(g.dma_start(out=s_pi[si % 2][:, 0:w], in_=pk[r0:r0 + 128, c0:c0 + w]), 16)
            ld[si % 2].inc(g.dma_start(
                out=s_sb[si % 2][:, 0:w],
                in_=sbv[g0:g0 + 4, c0:c0 + w].unsqueeze(1).broadcast_to([4, 32, w])), 16)
            ld[si % 2].inc(g.dma_start(
                out=s_zb[si % 2][:, 0:w],
                in_=zbv[g0:g0 + 4, c0:c0 + w].unsqueeze(1).broadcast_to([4, 32, w])), 16)
            if si >= 1:
                emit_wout(g, si - 1)
        emit_wout(g, len(slabs) - 1)
        g.wait_ge(wo[0].h, wo[0].tot)
        g.wait_ge(wo[1].h, wo[1].tot)

    def body_vector(v):
        for si, (pk, sbv, zbv, outw, jt, lofs, c0, w, is_down) in enumerate(slabs):
            v.wait_ge(ld[si % 2].h, (si // 2 + 1) * 48)
            if si >= 2:
                v.wait_ge(wo[si % 2].h, (si // 2) * 32)
            b = si % 2
            hw = w // 2
            for piece in range(2):
                p0, p1 = piece * hw, piece * hw + hw
                _v = v.tensor_scalar(out=s_ti[:, 0:hw], in0=s_pi[b][:, p0:p1], scalar1=4,
                                scalar2=None, op0=ALU.logical_shift_right)
                ctx.chain(v, _v)
                _v = v.tensor_tensor(out=s_wh[b][:, p0:p1], in0=s_ti[:, 0:hw],
                                in1=s_zb[b][:, p0:p1], op=ALU.subtract)
                ctx.chain(v, _v)
                _v = v.tensor_tensor(out=s_wh[b][:, p0:p1], in0=s_wh[b][:, p0:p1],
                                in1=s_sb[b][:, p0:p1], op=ALU.mult)
                _v = v.tensor_scalar(out=s_ti[:, 0:hw], in0=s_pi[b][:, p0:p1], scalar1=15,
                                scalar2=None, op0=ALU.bitwise_and)
                ctx.chain(v, _v)
                _v = v.tensor_tensor(out=s_wl[b][:, p0:p1], in0=s_ti[:, 0:hw],
                                in1=s_zb[b][:, p0:p1], op=ALU.subtract)
                ctx.chain(v, _v)
                ins = v.tensor_tensor(out=s_wl[b][:, p0:p1], in0=s_wl[b][:, p0:p1],
                                      in1=s_sb[b][:, p0:p1], op=ALU.mult)
                if piece == 1:
                    dq.inc(ins, 1)
                else:
                    ctx.chain(v, ins)

    return {"gpsimd": body_gpsimd, "vector": body_vector}




def emit_A(ctx, io, bufs_es):
    nc = ctx.nc
    E = bufs_es.enter_context
    s_w = E(nc.sbuf_tensor("a_w", [128, 32 * 1024], BF16))
    s_wsl = [E(nc.sbuf_tensor(f"a_wsl{i}", [128, 2048], F32)) for i in range(2)]
    s_xf = E(nc.sbuf_tensor("a_xf", [128, 32 * 256], F32))
    s_xb = [E(nc.sbuf_tensor(f"a_xb{i}", [128, 32 * 256], BF16)) for i in range(2)]
    s_st = [E(nc.sbuf_tensor(f"a_st{i}", [128, 2048], BF16)) for i in range(2)]
    s_bqT = E(nc.sbuf_tensor("a_bqT", [128, 16], F32))
    ps_mm = [E(nc.psum_tensor(f"a_ps{i}", [128, 2048], F32)) for i in range(2)]

    ld_b = ctx.sem("a_ldb")
    ld_w = [ctx.sem("a_ldw0"), ctx.sem("a_ldw1")]
    ld_x = ctx.sem("a_ldx")
    cv_w = ctx.sem("a_cvw")
    cv_x = ctx.sem("a_cvx")
    mm_d = ctx.sem("a_mmd")
    ev_d = ctx.sem("a_evd")
    st_o = [ctx.sem("a_sto0"), ctx.sem("a_sto1")]

    passes = [
        ("q", 0, 8, io["i_xqT"], io["sc_qT"], 0),
        ("k", 1024, 32, io["i_xT"], io["sc_kT"], 8),
        ("v", 2048, 32, io["i_xT"], None, None),
    ]
    nsl_cum = [0, 8, 40, 72]

    def body_sync(sync):
        ld_b.inc(sync.dma_start(out=s_bqT[:, :], in_=io["i_bqT"]), 16)
        for pi, (pname, wofs, nsl, xsrc, outT, bcol) in enumerate(passes):
            for i in range(16):
                gsi = pi * 16 + i
                if gsi >= 2:
                    sync.wait_ge(cv_w.h, gsi - 1)
                ld_w[gsi % 2].inc(sync.dma_start(
                    out=s_wsl[gsi % 2][:, :].rearrange("p (a f) -> p a f", a=2),
                    in_=io["i_wqkv"].rearrange("(a p) f -> p a f", p=128)[
                        :, 2 * i:2 * i + 2, wofs:wofs + 1024]), 16)
            for i in range(nsl):
                gxi = nsl_cum[pi] + i
                if gxi >= 1:
                    sync.wait_ge(cv_x.h, gxi)
                c0 = i * 256
                ld_x.inc(sync.dma_start(
                    out=s_xf[:, :],
                    in_=xsrc[:, c0:c0 + 256].rearrange("(a p) t -> p a t", p=128)), 16)

    def body_scalar(sc):
        for pi, (pname, wofs, nsl, xsrc, outT, bcol) in enumerate(passes):
            if pi >= 1:
                sc.wait_ge(mm_d.h, nsl_cum[pi])
            for i in range(16):
                gsi = pi * 16 + i
                sc.wait_ge(ld_w[gsi % 2].h, (gsi // 2 + 1) * 16)
                cv_w.inc(sc.activation(out=s_w[:, i * 2048:(i + 1) * 2048],
                              in_=s_wsl[gsi % 2][:, :], func=AF.Copy), 1)
            def emit_evac(i):
                gxi = nsl_cum[pi] + i
                sc.wait_ge(mm_d.h, gxi + 1)
                if gxi >= 2:
                    sc.wait_ge(st_o[gxi % 2].h, (gxi // 2) * 16)
                if pname == "v":
                    ins = sc.activation(out=s_st[i % 2][:, :], in_=ps_mm[i % 2][:, :],
                                           func=AF.Copy)
                    ev_d.inc(ins, 1)
                else:
                    for dt in range(8):
                        ins = sc.activation(out=s_st[i % 2][:, dt * 256:(dt + 1) * 256],
                                      in_=ps_mm[i % 2][:, dt * 256:(dt + 1) * 256],
                                      func=AF.Identity,
                                      bias=s_bqT[:, bcol + dt: bcol + dt + 1])
                    ev_d.inc(ins, 1)
                sc.wait_ge(ev_d.h, ev_d.tot)
                c0 = i * 256
                if pname == "v":
                    st_o[gxi % 2].inc(sc.dma_start(
                        out=io["sc_v"][c0:c0 + 256, :].rearrange("(a p) d -> p a d", p=128),
                        in_=s_st[i % 2][:, :].rearrange("p (a d) -> p a d", a=2)), 16)
                else:
                    st_o[gxi % 2].inc(sc.dma_start(
                        out=outT[:, :, c0:c0 + 256].rearrange("a p t -> p a t"),
                        in_=s_st[i % 2][:, :].rearrange("p (a t) -> p a t", a=8)), 16)

            for i in range(nsl):
                gxi = nsl_cum[pi] + i
                sc.wait_ge(ld_x.h, (gxi + 1) * 16)
                if gxi >= 2:
                    sc.wait_ge(mm_d.h, gxi - 1)
                cv_x.inc(sc.activation(out=s_xb[i % 2][:, :], in_=s_xf[:, :], func=AF.Copy), 1)
                if i >= 1:
                    emit_evac(i - 1)
            emit_evac(nsl - 1)
        sc.wait_ge(st_o[0].h, st_o[0].tot)
        sc.wait_ge(st_o[1].h, st_o[1].tot)

    def body_tensor(t):
        t.wait_ge(ld_b.h, 16)
        for pi, (pname, wofs, nsl, xsrc, outT, bcol) in enumerate(passes):
            for i in range(nsl):
                gxi = nsl_cum[pi] + i
                t.wait_ge(cv_w.h, (pi + 1) * 16)
                t.wait_ge(cv_x.h, gxi + 1)
                if gxi >= 2:
                    t.wait_ge(ev_d.h, gxi - 1)
                ps = ps_mm[i % 2]
                xb = s_xb[i % 2]
                if pname == "v":
                    for tsub in range(2):
                        for a in range(32):
                            for dc in range(2):
                                o0 = tsub * 1024 + dc * 512
                                _i = t.matmul(ps[:, o0:o0 + 512],
                                         xb[:, a * 256 + tsub * 128: a * 256 + tsub * 128 + 128],
                                         s_w[:, a * 1024 + dc * 512: a * 1024 + (dc + 1) * 512],
                                         start=(a == 0), stop=(a == 31))
                else:
                    for dt in range(8):
                        for a in range(32):
                            _i = t.matmul(ps[:, dt * 256:(dt + 1) * 256],
                                     s_w[:, a * 1024 + dt * 128: a * 1024 + dt * 128 + 128],
                                     xb[:, a * 256:(a + 1) * 256],
                                     start=(a == 0), stop=(a == 31))
                mm_d.inc(_i, 1)

    return {"sync": body_sync, "scalar": body_scalar, "tensor": body_tensor}





def emit_B1(ctx, io, bufs_es):
    nc = ctx.nc
    E = bufs_es.enter_context
    s_kT = E(nc.sbuf_tensor("b_kT", [128, 8 * 8192], BF16))
    s_qt = [E(nc.sbuf_tensor(f"b_qt{i}", [128, 8 * 128], BF16)) for i in range(2)]
    s_sc = E(nc.sbuf_tensor("b_sc", [128, 8192], F32))
    s_pb = [E(nc.sbuf_tensor(f"b_pb{i}", [128, 8192], BF16)) for i in range(2)]
    s_mx = E(nc.sbuf_tensor("b_mx", [128, 16], F32))
    s_nb = E(nc.sbuf_tensor("b_nb", [128, 16], F32))
    s_l = E(nc.sbuf_tensor("b_l", [128, 16], F32))
    ps_s = [E(nc.psum_tensor(f"b_pss{i}", [128, 2048], F32)) for i in range(2)]

    b_lk = ctx.sem("b_lk")
    b_lq = [ctx.sem("b_lq0"), ctx.sem("b_lq1")]
    b_mm = ctx.sem("b_mm")
    b_ev = ctx.sem("b_ev")
    b_mx = ctx.sem("b_mx")
    b_ex = ctx.sem("b_ex")
    b_po = [ctx.sem("b_po0"), ctx.sem("b_po1")]
    b_rl = ctx.sem("b_rl")

    def body_sync(sync):
        b_lk.inc(sync.dma_start(out=s_kT[:, :],
                                in_=io["sc_kT"].rearrange("a p k -> p a k")), 16)
        for qt in range(16):
            if qt >= 2:
                sync.wait_ge(b_mm.h, 4 * qt - 4)
            b_lq[qt % 2].inc(sync.dma_start(
                out=s_qt[qt % 2][:, :],
                in_=io["sc_qT"][:, :, qt * 128:(qt + 1) * 128].rearrange("a p t -> p a t")), 16)

    def body_tensor(t):
        t.wait_ge(b_lk.h, 16)
        t.wait_ge(ctx.id_sem.h, 1)
        for qt in range(16):
            t.wait_ge(b_lq[qt % 2].h, (qt // 2 + 1) * 16)
            for su in range(4):
                gi = 4 * qt + su
                if gi >= 2:
                    t.wait_ge(b_ev.h, gi - 1)
                for dt in range(8):
                    for kc in range(4):
                        _i = t.matmul(ps_s[gi % 2][:, kc * 512:(kc + 1) * 512],
                                 s_qt[qt % 2][:, dt * 128:(dt + 1) * 128],
                                 s_kT[:, dt * 8192 + su * 2048 + kc * 512:
                                      dt * 8192 + su * 2048 + (kc + 1) * 512],
                                 start=(dt == 0), stop=(dt == 7))
                b_mm.inc(_i, 1)

    def body_scalar(sc):
        for qt in range(16):
            sc.wait_ge(b_mx.h, qt + 1)
            if qt >= 2:
                sc.wait_ge(b_po[qt % 2].h, (qt // 2) * 16)
            b_ex.inc(sc.activation(out=s_pb[qt % 2][:, :], in_=s_sc[:, :], func=AF.Exp,
                                   bias=s_nb[:, qt:qt + 1], scale=SOFT_SCALE,
                                   accum_out=s_l[:, qt:qt + 1]), 1)

    def body_vector(v):
        for qt in range(16):
            if qt >= 1:
                v.wait_ge(b_ex.h, qt)
            for su in range(4):
                gi = 4 * qt + su
                v.wait_ge(b_mm.h, gi + 1)
                _v = v.reduce_max(s_mx[:, 4 * (qt % 2) + su:4 * (qt % 2) + su + 1],
                                  ps_s[gi % 2][:, :], axis=AX.X)
                if su == 3:
                    ctx.chain(v, _v)
                b_ev.inc(v.tensor_copy(out=s_sc[:, su * 2048:(su + 1) * 2048],
                                       in_=ps_s[gi % 2][:, :]), 1)
            _v = v.reduce_max(s_nb[:, qt:qt + 1],
                              s_mx[:, 4 * (qt % 2):4 * (qt % 2) + 4], axis=AX.X)
            ctx.chain(v, _v)
            b_mx.inc(v.tensor_scalar(out=s_nb[:, qt:qt + 1], in0=s_nb[:, qt:qt + 1],
                                     scalar1=-SOFT_SCALE, scalar2=None, op0=ALU.mult), 1)
            v.wait_ge(b_ex.h, qt + 1)
            b_rl.inc(v.reciprocal(io["rl"][:, qt:qt + 1], s_l[:, qt:qt + 1]), 1)

    def body_gpsimd(g):
        for qt in range(16):
            g.wait_ge(b_ex.h, qt + 1)
            b_po[qt % 2].inc(g.dma_start(
                out=io["sc_p"][qt * 128:(qt + 1) * 128, :], in_=s_pb[qt % 2][:, :]), 16)
        g.wait_ge(b_po[0].h, b_po[0].tot)
        g.wait_ge(b_po[1].h, b_po[1].tot)

    return {"sync": body_sync, "tensor": body_tensor, "scalar": body_scalar,
            "vector": body_vector, "gpsimd": body_gpsimd}




def emit_B2(ctx, io, bufs_es, s_ot):
    nc = ctx.nc
    E = bufs_es.enter_context
    s_v = E(nc.sbuf_tensor("c_v", [128, 64 * 1024], BF16))
    s_pl = [E(nc.sbuf_tensor(f"c_pl{i}", [128, 4096], BF16)) for i in range(2)]
    s_ptl = E(nc.sbuf_tensor("c_ptl", [128, 64 * 128], BF16))
    s_o = E(nc.sbuf_tensor("c_o", [128, 1024], BF16))
    ps_o = [E(nc.psum_tensor(f"c_pso{i}", [128, 1024], F32)) for i in range(2)]
    ps_t = [E(nc.psum_tensor(f"c_pst{i}", [128, 1024], BF16)) for i in range(2)]
    ps_ot = E(nc.psum_tensor("c_psot", [128, 1024], BF16))

    c_lv = ctx.sem("c_lv")
    c_lp = [ctx.sem("c_lp0"), ctx.sem("c_lp1")]
    c_tp = ctx.sem("c_tp")
    c_pe = ctx.sem("c_pe")
    c_mm = ctx.sem("c_mm")
    c_ev = ctx.sem("c_ev")
    c_tpo = ctx.sem("c_tpo")
    c_oe = ctx.sem("c_oe")

    def body_sync(sync):
        c_lv.inc(sync.dma_start(out=s_v[:, :],
                                in_=io["sc_v"].rearrange("(a p) d -> p a d", p=128)), 16)
        for hh in range(32):
            if hh >= 2:
                sync.wait_ge(c_tp.h, 4 * hh - 4)
            c_lp[hh % 2].inc(sync.dma_start(
                out=s_pl[hh % 2][:, :],
                in_=io["sc_p"][(hh // 2) * 128:(hh // 2 + 1) * 128,
                               (hh % 2) * 4096:(hh % 2 + 1) * 4096]), 16)

    def body_tensor(t):
        t.wait_ge(c_lv.h, 16)
        for qt in range(16):
            if qt >= 1:
                t.wait_ge(c_mm.h, qt)
            for grp in range(8):
                gg = 8 * qt + grp
                hh = 2 * qt + grp // 4
                t.wait_ge(c_lp[hh % 2].h, (hh // 2 + 1) * 16)
                if gg >= 2:
                    t.wait_ge(c_pe.h, gg - 1)
                for j in range(8):
                    _i = t.transpose(ps_t[gg % 2][:, j * 128:(j + 1) * 128],
                                s_pl[hh % 2][:, ((grp % 4) * 8 + j) * 128:
                                             ((grp % 4) * 8 + j + 1) * 128],
                                io["ident"][:])
                c_tp.inc(_i, 1)
            t.wait_ge(c_pe.h, 8 * qt + 8)
            if qt >= 2:
                t.wait_ge(c_ev.h, qt - 1)
            for kt in range(64):
                for dc in range(2):
                    _i = t.matmul(ps_o[qt % 2][:, dc * 512:(dc + 1) * 512],
                             s_ptl[:, kt * 128:(kt + 1) * 128],
                             s_v[:, kt * 1024 + dc * 512: kt * 1024 + (dc + 1) * 512],
                             start=(kt == 0), stop=(kt == 63))
            c_mm.inc(_i, 1)
            t.wait_ge(c_ev.h, qt + 1)
            t.wait_ge(c_oe.h, qt)
            for j in range(8):
                _i = t.transpose(ps_ot[:, j * 128:(j + 1) * 128],
                            s_o[:, j * 128:(j + 1) * 128], io["ident"][:])
            c_tpo.inc(_i, 1)

    def body_scalar(sc):
        for qt in range(16):
            for grp in range(8):
                gg = 8 * qt + grp
                sc.wait_ge(c_tp.h, gg + 1)
                c_pe.inc(sc.activation(out=s_ptl[:, grp * 1024:(grp + 1) * 1024],
                                       in_=ps_t[gg % 2][:, :], func=AF.Copy), 1)
            sc.wait_ge(c_mm.h, qt + 1)
            if qt >= 1:
                sc.wait_ge(c_tpo.h, qt)
            c_ev.inc(sc.activation(out=s_o[:, :], in_=ps_o[qt % 2][:, :],
                                   func=AF.Identity, bias=0.0,
                                   scale=io["rl"][:, qt:qt + 1]), 1)
            sc.wait_ge(c_tpo.h, qt + 1)
            c_oe.inc(sc.activation(
                out=s_ot[:, :].rearrange("p (a q) -> p a q", q=2048)[:, :, qt * 128:(qt + 1) * 128],
                in_=ps_ot[:, :], func=AF.Copy), 1)

    return {"sync": body_sync, "tensor": body_tensor, "scalar": body_scalar}




def emit_B3(ctx, io, bufs_es, s_ot):
    nc = ctx.nc
    E = bufs_es.enter_context
    s_wo = E(nc.sbuf_tensor("d_wo", [128, 8 * 4096], BF16))
    s_wsl = [E(nc.sbuf_tensor(f"d_wsl{i}", [128, 1024], F32)) for i in range(2)]
    s_bo128 = E(nc.sbuf_tensor("d_bo128", [128, 4096], F32))
    s_bvT = E(nc.sbuf_tensor("d_bvT", [128, 8], BF16))
    s_ones = E(nc.sbuf_tensor("d_ones", [1, 128], BF16))
    s_xo = [E(nc.sbuf_tensor(f"d_xo{i}", [128, 4096], F32)) for i in range(2)]
    s_pr = E(nc.sbuf_tensor("d_pr", [128, 4096], F32))
    s_xe = E(nc.sbuf_tensor("d_xe", [128, 2048], BF16))
    s_xoo = E(nc.sbuf_tensor("d_xoo", [128, 2048], BF16))
    s_xms = E(nc.sbuf_tensor("d_xms", [128, 32 * 128], BF16))
    s_bor = s_pr[0:1, 0:4096]
    s_bob = s_xms[0:1, 0:4096]
    s_bx = s_xo
    s_xm = s_pr
    s_bns = E(nc.sbuf_tensor("d_bns", [128, 48], F32))
    s_mv = E(nc.sbuf_tensor("d_mv", [128, 2], F32))
    s_sd = E(nc.sbuf_tensor("d_sd", [128, 1], F32))
    s_rs = E(nc.sbuf_tensor("d_rs", [128, 1], F32))
    s_nmr = E(nc.sbuf_tensor("d_nmr", [128, 1], F32))
    ps_p = E(nc.psum_tensor("d_psp", [128, 2048], F32))
    ps_xt = [E(nc.psum_tensor(f"d_psxt{i}", [128, 512], BF16)) for i in range(2)]

    d_lb = ctx.sem("d_lb")
    d_lv = ctx.sem("d_lv")
    d_lw = [ctx.sem("d_lw0"), ctx.sem("d_lw1")]
    d_cw = ctx.sem("d_cw")
    d_on = ctx.sem("d_on")
    d_bm = ctx.sem("d_bm")
    d_bv = ctx.sem("d_bv")
    d_bc = ctx.sem("d_bc")
    d_bb = ctx.sem("d_bb")
    d_be = ctx.sem("d_be")
    d_lx = [ctx.sem("d_lx0"), ctx.sem("d_lx1")]
    d_bx = ctx.sem("d_bx")
    d_mm = ctx.sem("d_mm")
    d_pa = ctx.sem("d_pa")
    d_st = ctx.sem("d_st")
    d_sq = ctx.sem("d_sq")
    d_ln = ctx.sem("d_ln")
    d_xw = ctx.sem("d_xw")
    d_sp = ctx.sem("d_sp")
    d_tp = ctx.sem("d_tp")
    d_se = ctx.sem("d_se")
    d_so = ctx.sem("d_so")

    def body_sync(sync):
        d_lb.inc(sync.dma_start(out=s_bor[:, :], in_=io["i_bo"][None, :]), 16)
        for i in range(32):
            if i >= 2:
                sync.wait_ge(d_cw.h, i - 1)
            d_lw[i % 2].inc(sync.dma_start(
                out=s_wsl[i % 2][:, :],
                in_=io["i_wo"].rearrange("(a p) f -> a p f", p=128)[
                    i // 4, :, (i % 4) * 1024:(i % 4 + 1) * 1024]), 16)
        for tt in range(16):
            if tt >= 2:
                sync.wait_ge(d_bx.h, tt - 1)
            d_lx[tt % 2].inc(sync.dma_start(
                out=s_xo[tt % 2][:, :],
                in_=io["i_xo"][tt * 128:(tt + 1) * 128, :]), 16)
            sync.wait_ge(d_ln.h, tt + 1)
            d_xw.inc(sync.dma_start(
                out=io["sc_xmid"][tt * 128:(tt + 1) * 128, :], in_=s_pr[:, :]), 16)
        sync.wait_ge(d_xw.h, d_xw.tot)

    def body_scalar(sc):
        for i in range(32):
            sc.wait_ge(d_lw[i % 2].h, (i // 2 + 1) * 16)
            d_cw.inc(sc.activation(out=s_wo[:, i * 1024:(i + 1) * 1024],
                                   in_=s_wsl[i % 2][:, :], func=AF.Copy), 1)
        sc.wait_ge(d_bv.h, 2)
        d_bc.inc(sc.activation(out=s_bob[:, :], in_=s_bo128[0:1, :], func=AF.Copy), 1)
        for r in range(2):
            sc.wait_ge(d_bb.h, r + 1)
            d_be.inc(sc.activation(out=s_bo128[:, r * 2048:(r + 1) * 2048],
                                   in_=ps_p[:, :], func=AF.Copy), 1)
        for tt in range(16):
            sc.wait_ge(d_st.h, tt + 1)
            d_sq.inc(sc.activation(out=s_sd[:, :], in_=s_mv[:, 1:2],
                                   func=AF.Sqrt), 1)
            for grp in range(8):
                gg = 8 * tt + grp
                sc.wait_ge(d_tp.h, gg + 1)
                if grp == 0:
                    sc.wait_ge(d_so.h, 16 * tt)
                d_se.inc(sc.activation(out=s_xms[:, grp * 512:(grp + 1) * 512],
                                       in_=ps_xt[grp % 2][:, :], func=AF.Copy), 1)

    def body_tensor(t):
        t.wait_ge(d_cw.h, 32)
        t.wait_ge(d_lb.h, 16)
        t.wait_ge(d_lv.h, 16)
        for half in range(2):
            if half == 1:
                t.wait_ge(d_bv.h, 1)
            for cc in range(4):
                for dt in range(8):
                    _i = t.matmul(ps_p[0:1, cc * 512:(cc + 1) * 512],
                             s_bvT[:, dt:dt + 1],
                             s_wo[:, dt * 4096 + half * 2048 + cc * 512:
                                  dt * 4096 + half * 2048 + (cc + 1) * 512],
                             start=(dt == 0), stop=(dt == 7))
            d_bm.inc(_i, 1)
        t.wait_ge(d_bc.h, 1)
        t.wait_ge(d_on.h, 1)
        t.wait_ge(d_bv.h, 2)
        for r in range(2):
            if r == 1:
                t.wait_ge(d_be.h, 1)
            for cc in range(4):
                _i = t.matmul(ps_p[:, cc * 512:(cc + 1) * 512], s_ones[0:1, :],
                         s_bob[0:1, r * 2048 + cc * 512: r * 2048 + (cc + 1) * 512],
                         start=True, stop=True)
            d_bb.inc(_i, 1)
        for tt in range(16):
            for half in range(2):
                gi = 2 * tt + half
                t.wait_ge(d_be.h, 2) if gi == 0 else None
                if gi >= 1:
                    t.wait_ge(d_pa.h, gi)
                for dt in range(8):
                    for cc in range(4):
                        _i = t.matmul(ps_p[:, cc * 512:(cc + 1) * 512],
                                 s_ot[:, dt * 2048 + tt * 128: dt * 2048 + (tt + 1) * 128],
                                 s_wo[:, dt * 4096 + half * 2048 + cc * 512:
                                      dt * 4096 + half * 2048 + (cc + 1) * 512],
                                 start=(dt == 0), stop=(dt == 7))
                d_mm.inc(_i, 1)
            t.wait_ge(d_sp.h, tt + 1)
            for half in range(2):
                src = s_xe if half == 0 else s_xoo
                for grp in range(4):
                    gg = 8 * tt + half * 4 + grp
                    if gg >= 2:
                        t.wait_ge(d_se.h, gg - 1)
                    for j in range(4):
                        _i = t.transpose(ps_xt[gg % 2][:, j * 128:(j + 1) * 128],
                                    src[:, (grp * 4 + j) * 128:(grp * 4 + j + 1) * 128],
                                    io["ident"][:])
                    d_tp.inc(_i, 1)

    def body_vector(v):
        for half in range(2):
            v.wait_ge(d_bm.h, half + 1)
            v.wait_ge(d_lb.h, 16)
            d_bv.inc(v.tensor_tensor(out=s_bo128[0:1, half * 2048:(half + 1) * 2048],
                                     in0=ps_p[0:1, 0:2048],
                                     in1=s_bor[0:1, half * 2048:(half + 1) * 2048],
                                     op=ALU.add), 1)
        for tt in range(16):
            v.wait_ge(d_lx[tt % 2].h, (tt // 2 + 1) * 16)
            v.wait_ge(d_be.h, 2)
            if tt >= 1:
                v.wait_ge(d_pa.h, 2 * tt)
            d_bx.inc(v.tensor_tensor(out=s_xo[tt % 2][:, :], in0=s_bo128[:, :],
                                     in1=s_xo[tt % 2][:, :], op=ALU.add), 1)
            v.wait_ge(d_bx.h, d_bx.tot)
            for half in range(2):
                gi = 2 * tt + half
                v.wait_ge(d_mm.h, gi + 1)
                if half == 0 and tt >= 1:
                    v.wait_ge(d_xw.h, 16 * tt)
                d_pa.inc(v.tensor_tensor(out=s_pr[:, half * 2048:(half + 1) * 2048],
                                         in0=ps_p[:, :],
                                         in1=s_xo[tt % 2][:, half * 2048:(half + 1) * 2048],
                                         op=ALU.add), 1)
            v.wait_ge(d_pa.h, d_pa.tot)
            for ch in range(8):
                _v = v.bn_stats(s_bns[:, ch * 6:(ch + 1) * 6], s_pr[:, ch * 512:(ch + 1) * 512])
            ctx.chain(v, _v)
            _v = v.bn_aggr(s_mv[:, :], s_bns[:, :])
            ctx.chain(v, _v)
            d_st.inc(v.tensor_scalar(out=s_mv[:, 1:2], in0=s_mv[:, 1:2], scalar1=EPS,
                                     scalar2=None, op0=ALU.add), 1)
            v.wait_ge(d_sq.h, tt + 1)
            _v = v.reciprocal(s_rs[:, :], s_sd[:, :])
            ctx.chain(v, _v)
            _v = v.tensor_scalar(out=s_nmr[:, :], in0=s_mv[:, 0:1], scalar1=s_rs[:, 0:1],
                            scalar2=-1.0, op0=ALU.mult, op1=ALU.mult)
            ctx.chain(v, _v)
            d_ln.inc(v.tensor_scalar(out=s_pr[:, :], in0=s_pr[:, :], scalar1=s_rs[:, 0:1],
                                     scalar2=s_nmr[:, 0:1], op0=ALU.mult, op1=ALU.add), 1)
            v.wait_ge(d_ln.h, d_ln.tot)
            if tt >= 1:
                v.wait_ge(d_tp.h, 8 * tt)
            v.tensor_copy(out=s_xe[:, :], in_=s_pr[:, 0:4096:2])
            d_sp.inc(v.tensor_copy(out=s_xoo[:, :], in_=s_pr[:, 1:4096:2]), 1)

    def body_gpsimd(g):
        d_lv.inc(g.dma_start(out=s_bvT[:, :], in_=io["i_bvT"]), 16)
        d_on.inc(g.memset(s_ones[:, :], 1.0), 1)
        for tt in range(16):
            g.wait_ge(d_se.h, 8 * tt + 8)
            d_so.inc(g.dma_start(
                out=io["sc_xmT"][:, :, tt * 128:(tt + 1) * 128].rearrange("a p t -> p a t"),
                in_=s_xms[:, :].rearrange("p (a t) -> p a t", t=128)), 16)
        g.wait_ge(d_so.h, d_so.tot)

    return {"sync": body_sync, "scalar": body_scalar, "tensor": body_tensor,
            "vector": body_vector, "gpsimd": body_gpsimd}




def emit_C(ctx, io, bufs_es):
    nc = ctx.nc
    E = bufs_es.enter_context
    s_xmT = E(nc.sbuf_tensor("e_xmT", [128, 32 * 2048], BF16))
    s_wt = [[E(nc.sbuf_tensor(f"e_w{w}{i}", [128, 32 * 128], BF16)) for i in range(2)]
            for w in range(2)]
    s_sg = E(nc.sbuf_tensor("e_sg", [128, 1024], F32))
    s_ht = [E(nc.sbuf_tensor(f"e_ht{i}", [128, 2048], BF16)) for i in range(2)]
    ps_g = [E(nc.psum_tensor(f"e_psg{i}", [128, 1024], F32)) for i in range(2)]
    ps_u = [E(nc.psum_tensor(f"e_psu{i}", [128, 1024], F32)) for i in range(2)]

    e_lx = ctx.sem("e_lx")
    e_lw = [ctx.sem("e_lw0"), ctx.sem("e_lw1")]
    e_mm = ctx.sem("e_mm")
    e_sl = ctx.sem("e_sl")
    e_hv = ctx.sem("e_hv")
    e_ho = [ctx.sem("e_ho0"), ctx.sem("e_ho1")]

    def body_sync(sync):
        e_lx.inc(sync.dma_start(out=s_xmT[:, :],
                                in_=io["sc_xmT"].rearrange("a p t -> p a t")), 16)
        for it in range(86):
            if it >= 2:
                sync.wait_ge(e_mm.h, 2 * it - 2)
            e_lw[it % 2].inc(sync.dma_start(
                out=s_wt[0][it % 2][:, :],
                in_=io["sc_wg"][it].rearrange("a p c -> p a c")), 16)
            e_lw[it % 2].inc(sync.dma_start(
                out=s_wt[1][it % 2][:, :],
                in_=io["sc_wu"][it].rearrange("a p c -> p a c")), 16)

    def body_tensor(t):
        t.wait_ge(e_lx.h, 16)
        for it in range(86):
            t.wait_ge(e_lw[it % 2].h, (it // 2 + 1) * 32)
            for th in range(2):
                unit = 2 * it + th
                if unit >= 2:
                    t.wait_ge(e_sl.h, unit - 1)
                    t.wait_ge(e_hv.h, unit - 1)
                for w in range(2):
                    ps = (ps_g if w == 0 else ps_u)[unit % 2]
                    for rt in range(32):
                        for tc in range(2):
                            _i = t.matmul(ps[:, tc * 512:(tc + 1) * 512],
                                     s_wt[w][it % 2][:, rt * 128:(rt + 1) * 128],
                                     s_xmT[:, rt * 2048 + th * 1024 + tc * 512:
                                           rt * 2048 + th * 1024 + (tc + 1) * 512],
                                     start=(rt == 0), stop=(rt == 31))
                e_mm.inc(_i, 1)

    def body_scalar(sc):
        for unit in range(172):
            sc.wait_ge(e_mm.h, unit + 1)
            if unit >= 1:
                sc.wait_ge(e_hv.h, unit)
            e_sl.inc(sc.activation(out=s_sg[:, :], in_=ps_g[unit % 2][:, :],
                                   func=AF.Silu), 1)

    def body_vector(v):
        for unit in range(172):
            it, th = unit // 2, unit % 2
            v.wait_ge(e_sl.h, unit + 1)
            if th == 0 and it >= 2:
                v.wait_ge(e_ho[it % 2].h, (it // 2) * 16)
            e_hv.inc(v.tensor_tensor(out=s_ht[it % 2][:, th * 1024:(th + 1) * 1024],
                                     in0=s_sg[:, :], in1=ps_u[unit % 2][:, :],
                                     op=ALU.mult), 1)

    def body_gpsimd(g):
        for it in range(86):
            g.wait_ge(e_hv.h, 2 * it + 2)
            e_ho[it % 2].inc(g.dma_start(out=io["sc_hT"][it], in_=s_ht[it % 2][:, :]), 16)
        g.wait_ge(e_ho[0].h, e_ho[0].tot)
        g.wait_ge(e_ho[1].h, e_ho[1].tot)

    return {"sync": body_sync, "tensor": body_tensor, "scalar": body_scalar,
            "vector": body_vector, "gpsimd": body_gpsimd}




def emit_D(ctx, io, bufs_es):
    nc = ctx.nc
    E = bufs_es.enter_context
    s_hq = E(nc.sbuf_tensor("f_hq", [128, 86 * 512], BF16))
    s_wd2 = [E(nc.sbuf_tensor(f"f_wd{i}", [128, 86 * 128], BF16)) for i in range(2)]
    s_tmp = [E(nc.sbuf_tensor(f"f_tmp{i}", [128, 512], BF16)) for i in range(2)]
    s_ff = E(nc.sbuf_tensor("f_ff", [128, 4 * 4096], BF16))
    s_xo2 = [E(nc.sbuf_tensor("f_xo0", [128, 4096], F32))] * 2
    s_row = E(nc.sbuf_tensor("f_row", [128, 4096], F32))
    s_bns = E(nc.sbuf_tensor("f_bns", [128, 48], F32))
    s_mv = E(nc.sbuf_tensor("f_mv", [128, 2], F32))
    s_sd = E(nc.sbuf_tensor("f_sd", [128, 1], F32))
    s_rs = E(nc.sbuf_tensor("f_rs", [128, 1], F32))
    s_nmr = E(nc.sbuf_tensor("f_nmr", [128, 1], F32))
    ps_d = [E(nc.psum_tensor(f"f_psd{i}", [128, 512], F32)) for i in range(2)]
    ps_t2 = [E(nc.psum_tensor(f"f_pst{i}", [128, 512], BF16)) for i in range(2)]

    f_lh = [ctx.sem("f_lh0"), ctx.sem("f_lh1")]
    f_lw = [ctx.sem("f_lw0"), ctx.sem("f_lw1")]
    f_mm = ctx.sem("f_mm")
    f_ev = ctx.sem("f_ev")
    f_tp = ctx.sem("f_tp")
    f_fe = ctx.sem("f_fe")
    f_lx = [ctx.sem("f_lx0"), ctx.sem("f_lx1")]
    f_ad = ctx.sem("f_ad")
    f_st = ctx.sem("f_st")
    f_sq = ctx.sem("f_sq")
    f_ln = ctx.sem("f_ln")
    f_oo = ctx.sem("f_oo")

    def body_sync(sync):
        for tb in range(4):
            sync.wait_ge(f_mm.h, 32 * tb)
            for hf in range(2):
                f_lh[hf].inc(sync.dma_start(
                    out=s_hq[:, hf * 43 * 512:(hf + 1) * 43 * 512]
                        .rearrange("p (a t) -> p a t", t=512),
                    in_=io["sc_hT"][hf * 43:(hf + 1) * 43, :, tb * 512:(tb + 1) * 512]
                        .rearrange("a p t -> p a t")), 16)
            for ht in range(32):
                ghi = 32 * tb + ht
                if ghi >= 2:
                    sync.wait_ge(f_mm.h, ghi - 1)
                f_lw[ghi % 2].inc(sync.dma_start(
                    out=s_wd2[ghi % 2][:, :],
                    in_=io["sc_wd"][ht].rearrange("a p c -> p a c")), 16)

    def body_gpsimd(g):
        for tb in range(4):
            for tt in range(4):
                gti = 4 * tb + tt
                if gti >= 1:
                    g.wait_ge(f_ad.h, gti)
                f_lx[gti % 2].inc(g.dma_start(
                    out=s_xo2[gti % 2][:, :],
                    in_=io["sc_xmid"][gti * 128:(gti + 1) * 128, :]), 16)
                g.wait_ge(f_ln.h, gti + 1)
                f_oo.inc(g.dma_start(
                    out=io["out"][gti * 128:(gti + 1) * 128, :], in_=s_row[:, :]), 16)
        g.wait_ge(f_oo.h, f_oo.tot)

    def body_tensor(t):
        for tb in range(4):
            t.wait_ge(f_lh[0].h, 16 * (tb + 1))
            t.wait_ge(f_lh[1].h, 16 * (tb + 1))
            for ht in range(32):
                ghi = 32 * tb + ht
                t.wait_ge(f_lw[ghi % 2].h, (ghi // 2 + 1) * 16)
                if ghi >= 2:
                    t.wait_ge(f_ev.h, ghi - 1)
                for rt in range(86):
                    _i = t.matmul(ps_d[ghi % 2][:, :],
                             s_wd2[ghi % 2][:, rt * 128:(rt + 1) * 128],
                             s_hq[:, rt * 512:(rt + 1) * 512],
                             start=(rt == 0), stop=(rt == 85))
                f_mm.inc(_i, 1)
                t.wait_ge(f_ev.h, ghi + 1)
                if ghi >= 2:
                    t.wait_ge(f_fe.h, ghi - 1)
                for j in range(4):
                    _i = t.transpose(ps_t2[ghi % 2][:, j * 128:(j + 1) * 128],
                                s_tmp[ghi % 2][:, j * 128:(j + 1) * 128], io["ident"][:])
                f_tp.inc(_i, 1)

    def body_scalar(sc):
        for tb in range(4):
            for ht in range(32):
                ghi = 32 * tb + ht
                sc.wait_ge(f_mm.h, ghi + 1)
                if ghi >= 2:
                    sc.wait_ge(f_tp.h, ghi - 1)
                f_ev.inc(sc.activation(out=s_tmp[ghi % 2][:, :], in_=ps_d[ghi % 2][:, :],
                                       func=AF.Copy), 1)
                sc.wait_ge(f_tp.h, ghi + 1)
                if ht == 0:
                    sc.wait_ge(f_ad.h, 4 * tb)
                f_fe.inc(sc.activation(
                    out=s_ff[:, :].rearrange("p (a c) -> p a c", c=4096)[:, :, ht * 128:(ht + 1) * 128],
                    in_=ps_t2[ghi % 2][:, :], func=AF.Copy), 1)
            for tt in range(4):
                gti = 4 * tb + tt
                sc.wait_ge(f_st.h, gti + 1)
                f_sq.inc(sc.activation(out=s_sd[:, :], in_=s_mv[:, 1:2],
                                       func=AF.Sqrt), 1)

    def body_vector(v):
        for tb in range(4):
            for tt in range(4):
                gti = 4 * tb + tt
                v.wait_ge(f_fe.h, 32 * (tb + 1))
                v.wait_ge(f_lx[gti % 2].h, (gti // 2 + 1) * 16)
                if gti >= 1:
                    v.wait_ge(f_oo.h, 16 * gti)
                f_ad.inc(v.tensor_tensor(out=s_row[:, :],
                                         in0=s_ff[:, tt * 4096:(tt + 1) * 4096],
                                         in1=s_xo2[gti % 2][:, :], op=ALU.add), 1)
                v.wait_ge(f_ad.h, f_ad.tot)
                for ch in range(8):
                    _v = v.bn_stats(s_bns[:, ch * 6:(ch + 1) * 6],
                               s_row[:, ch * 512:(ch + 1) * 512])
                ctx.chain(v, _v)
                _v = v.bn_aggr(s_mv[:, :], s_bns[:, :])
                ctx.chain(v, _v)
                f_st.inc(v.tensor_scalar(out=s_mv[:, 1:2], in0=s_mv[:, 1:2], scalar1=EPS,
                                         scalar2=None, op0=ALU.add), 1)
                v.wait_ge(f_sq.h, gti + 1)
                _v = v.reciprocal(s_rs[:, :], s_sd[:, :])
                ctx.chain(v, _v)
                _v = v.tensor_scalar(out=s_nmr[:, :], in0=s_mv[:, 0:1], scalar1=s_rs[:, 0:1],
                                scalar2=-1.0, op0=ALU.mult, op1=ALU.mult)
                ctx.chain(v, _v)
                f_ln.inc(v.tensor_scalar(out=s_row[:, :], in0=s_row[:, :],
                                         scalar1=s_rs[:, 0:1], scalar2=s_nmr[:, 0:1],
                                         op0=ALU.mult, op1=ALU.add), 1)

    return {"sync": body_sync, "tensor": body_tensor, "scalar": body_scalar,
            "vector": body_vector, "gpsimd": body_gpsimd}



PHASES_ORDER = ["A", "B1", "B2", "B3", "C", "D"]


def build_nc(upto="D", dev_outputs=()):
    nc = bass.Bass("TRN2", target_bir_lowering=False)

    def scratch(name, shape, dtype):
        kind = "ExternalOutput" if name in dev_outputs else "Internal"
        return nc.dram_tensor(name, shape, dtype, kind=kind).ap()

    io = {}
    io["i_xT"] = nc.dram_tensor("i_xT", [H, S], F32, kind="ExternalInput").ap()
    io["i_xqT"] = nc.dram_tensor("i_xqT", [H, TOWN], F32, kind="ExternalInput").ap()
    io["i_xo"] = nc.dram_tensor("i_xo", [TOWN, H], F32, kind="ExternalInput").ap()
    io["i_wqkv"] = nc.dram_tensor("i_wqkv", [H, 3 * D], F32, kind="ExternalInput").ap()
    io["i_bqT"] = nc.dram_tensor("i_bqT", [128, 16], F32, kind="ExternalInput").ap()
    io["i_bo"] = nc.dram_tensor("i_bo", [H], F32, kind="ExternalInput").ap()
    io["i_bvT"] = nc.dram_tensor("i_bvT", [128, 8], F32, kind="ExternalInput").ap()
    io["i_wo"] = nc.dram_tensor("i_wo", [D, H], F32, kind="ExternalInput").ap()
    io["i_pg"] = nc.dram_tensor("i_pg", [2048, NI], U8, kind="ExternalInput").ap()
    io["i_pu"] = nc.dram_tensor("i_pu", [2048, NI], U8, kind="ExternalInput").ap()
    io["i_pd"] = nc.dram_tensor("i_pd", [NI // 2, H], U8, kind="ExternalInput").ap()
    io["i_sbg"] = nc.dram_tensor("i_sbg", [64, NI], F32, kind="ExternalInput").ap()
    io["i_zbg"] = nc.dram_tensor("i_zbg", [64, NI], F32, kind="ExternalInput").ap()
    io["i_sbu"] = nc.dram_tensor("i_sbu", [64, NI], F32, kind="ExternalInput").ap()
    io["i_zbu"] = nc.dram_tensor("i_zbu", [64, NI], F32, kind="ExternalInput").ap()
    io["i_sbd"] = nc.dram_tensor("i_sbd", [172, H], F32, kind="ExternalInput").ap()
    io["i_zbd"] = nc.dram_tensor("i_zbd", [172, H], F32, kind="ExternalInput").ap()

    io["sc_qT"] = scratch("sc_qT", [8, 128, TOWN], BF16)
    io["sc_kT"] = scratch("sc_kT", [8, 128, S], BF16)
    io["sc_v"] = scratch("sc_v", [S, D], BF16)
    io["sc_p"] = scratch("sc_p", [TOWN, S], BF16)
    io["sc_xmid"] = scratch("sc_xmid", [TOWN, H], F32)
    io["sc_xmT"] = scratch("sc_xmT", [32, 128, TOWN], BF16)
    io["sc_wg"] = scratch("sc_wg", [86, 32, 128, 128], BF16)
    io["sc_wu"] = scratch("sc_wu", [86, 32, 128, 128], BF16)
    io["sc_wd"] = scratch("sc_wd", [32, 86, 128, 128], BF16)
    io["sc_hT"] = scratch("sc_hT", [86, 128, TOWN], BF16)
    io["out"] = nc.dram_tensor("out", [TOWN, H], F32, kind="ExternalOutput").ap()

    with contextlib.ExitStack() as ges:
        ctx = Ctx(nc, ges)
        io["ident"] = ges.enter_context(nc.sbuf_tensor("g_ident", [128, 128], BF16))
        io["rl"] = ges.enter_context(nc.sbuf_tensor("g_rl", [128, 16], F32))
        ctx.id_sem = ctx.sem("g_id")

        with contextlib.ExitStack() as ph:
            em_a = emit_A(ctx, io, ph)
            em_p = emit_P0(ctx, io, ph)

            def gp(g):
                _g = g.memset(io["ident"][:], 0.0)
                ctx.chain(g, _g)
                _g = g.affine_select(out=io["ident"][:], in_=io["ident"][:],
                                compare_op=ALU.not_equal, fill=1.0, base=0,
                                pattern=[[-1, 128]], channel_multiplier=1)
                ctx.id_sem.inc(_g, 1)
                em_p["gpsimd"](g)

            run_phase(ctx, "ph_A", {
                "sync": em_a["sync"], "scalar": em_a["scalar"], "tensor": em_a["tensor"],
                "gpsimd": gp, "vector": em_p["vector"]})

        if upto == "A":
            return nc, io

        with contextlib.ExitStack() as ph:
            em = emit_B1(ctx, io, ph)
            run_phase(ctx, "ph_B1", em)
        if upto == "B1":
            return nc, io
        with contextlib.ExitStack() as oter:
            s_ot = oter.enter_context(nc.sbuf_tensor("g_ot", [128, 8 * 2048], BF16))
            with contextlib.ExitStack() as ph:
                em = emit_B2(ctx, io, ph, s_ot)
                run_phase(ctx, "ph_B2", em)
            if upto != "B2":
                with contextlib.ExitStack() as ph:
                    em = emit_B3(ctx, io, ph, s_ot)
                    run_phase(ctx, "ph_B3", em)
        if upto in ("B2", "B3"):
            return nc, io
        with contextlib.ExitStack() as ph:
            em = emit_C(ctx, io, ph)
            run_phase(ctx, "ph_C", em)
        if upto == "C":
            return nc, io
        with contextlib.ExitStack() as ph:
            em = emit_D(ctx, io, ph)
            run_phase(ctx, "ph_D", em)
    return nc, io


def _host_inputs(inputs):
    x = np.ascontiguousarray(np.asarray(inputs["L0_attn_x"], dtype=np.float32))
    wqkv = np.ascontiguousarray(np.asarray(inputs["L0_attn_Wqkv"], dtype=np.float32))
    bqkv = np.ascontiguousarray(np.asarray(inputs["L0_attn_bqkv"], dtype=np.float32))
    wo = np.ascontiguousarray(np.asarray(inputs["L0_attn_Wo"], dtype=np.float32))
    bo = np.ascontiguousarray(np.asarray(inputs["L0_attn_bo"], dtype=np.float32))

    perm = np.concatenate([np.arange(0, NI, 2), np.arange(1, NI, 2)])

    def pack_bytes(wq_i32):
        return np.asarray(wq_i32, dtype=np.int32).view(np.uint8)[:, 0::4]

    pg = np.ascontiguousarray(pack_bytes(inputs["L0_ffn_gate_Wq"])[perm].T)
    pu = np.ascontiguousarray(pack_bytes(inputs["L0_ffn_up_Wq"])[perm].T)
    pd = np.ascontiguousarray(pack_bytes(inputs["L0_ffn_down_Wq"]).T)

    def tp(a):
        return np.ascontiguousarray(np.asarray(a, dtype=np.float32).T)

    sbg = tp(np.asarray(inputs["L0_ffn_gate_scale"])[perm])
    zbg = tp(np.asarray(inputs["L0_ffn_gate_zero"])[perm])
    sbu = tp(np.asarray(inputs["L0_ffn_up_scale"])[perm])
    zbu = tp(np.asarray(inputs["L0_ffn_up_zero"])[perm])
    sbd = tp(np.asarray(inputs["L0_ffn_down_scale"]))
    zbd = tp(np.asarray(inputs["L0_ffn_down_zero"]))

    bqT = np.ascontiguousarray(bqkv[0:2048].reshape(16, 128).T)
    bvT = np.ascontiguousarray(bqkv[2048:3072].reshape(8, 128).T)

    xT = [np.ascontiguousarray(x[b].T) for b in range(B)]

    in_maps = []
    for c in range(NCORE):
        b, r = c // 4, c % 4
        in_maps.append(dict(
            i_xT=xT[b],
            i_xqT=np.ascontiguousarray(xT[b][:, r * TOWN:(r + 1) * TOWN]),
            i_xo=np.ascontiguousarray(x[b, r * TOWN:(r + 1) * TOWN]),
            i_wqkv=wqkv, i_bqT=bqT, i_bo=bo, i_bvT=bvT, i_wo=wo,
            i_pg=pg, i_pu=pu, i_pd=pd,
            i_sbg=sbg, i_zbg=zbg, i_sbu=sbu, i_zbu=zbu, i_sbd=sbd, i_zbd=zbd,
        ))
    return in_maps


def kernel(**inputs):
    in_maps = _host_inputs(inputs)
    nc, io = build_nc()
    res = run_bass_kernel_spmd(nc, in_maps, core_ids=list(range(NCORE)))
    out = np.empty((B, S, H), dtype=np.float32)
    for c in range(NCORE):
        b, r = c // 4, c % 4
        out[b, r * TOWN:(r + 1) * TOWN] = res.results[c]["out"]
    return out


# version 08
# speedup vs baseline: 1.2899x; 1.2899x over previous
import contextlib
import numpy as np
import concourse.bass as bass
import concourse.mybir as mybir
from concourse.bass_utils import run_bass_kernel_spmd

F32 = mybir.dt.float32
BF16 = mybir.dt.bfloat16
U8 = mybir.dt.uint8
I32 = mybir.dt.int32
AF = mybir.ActivationFunctionType
ALU = mybir.AluOpType
AX = mybir.AxisListType

B, S, H, D, NI = 2, 8192, 4096, 1024, 11008
TOWN = 2048
NCORE = 8
SOFT_SCALE = 0.03125
EPS = 1e-5
DQ_SLABS_D = [(0, 2048), (2048, 2048)]


class SemT:
    def __init__(self, nc, es, name):
        self.h = es.enter_context(nc.semaphore(name))
        self.tot = 0

    def inc(self, instr, n=16):
        instr.then_inc(self.h, n)
        self.tot += n
        return instr


class Ctx:
    def __init__(self, nc, es):
        self.nc = nc
        self.es = es
        self.bar = SemT(nc, es, "bar")
        self.sems = {}
        self.chain_sems = {}

    def sem(self, name):
        s = SemT(self.nc, self.es, name)
        self.sems[name] = s
        return s

    def chain(self, eng, ins):
        key = ins.ins.engine if hasattr(ins, "ins") else None
        name = f"ch_{key}"
        if name not in self.chain_sems:
            self.chain_sems[name] = SemT(self.nc, self.es, name)
        s = self.chain_sems[name]
        s.inc(ins, 1)
        eng.wait_ge(s.h, s.tot)


ALL_ENGINES = ("tensor", "vector", "scalar", "gpsimd", "sync")


def run_phase(ctx, name, emitters):
    nc = ctx.nc
    base = ctx.bar.tot
    ctx.bar.tot += len(ALL_ENGINES)
    tgt = ctx.bar.tot

    with nc.Block(name) as blk:
        for ename in ALL_ENGINES:
            fn = emitters.get(ename)

            def body(eng, fn=fn):
                if fn is not None:
                    fn(eng)
                eng.drain().then_inc(ctx.bar.h, 1)
                eng.wait_ge(ctx.bar.h, tgt)
            getattr(blk, ename)(body)




def emit_P0(ctx, io, bufs_es, hooks=None):
    nc = ctx.nc
    E = bufs_es.enter_context
    W = 2048
    s_pi = [E(nc.sbuf_tensor(f"p_pi{i}", [128, W], I32)) for i in range(2)]
    s_sb = [E(nc.sbuf_tensor(f"p_sb{i}", [128, W], BF16)) for i in range(2)]
    s_zb = [E(nc.sbuf_tensor(f"p_zb{i}", [128, W], BF16)) for i in range(2)]
    s_ti = E(nc.sbuf_tensor("p_ti", [128, W // 2], I32))
    s_wh = [E(nc.sbuf_tensor(f"p_wh{i}", [128, W], BF16)) for i in range(2)]
    s_wl = [E(nc.sbuf_tensor(f"p_wl{i}", [128, W], BF16)) for i in range(2)]

    ld = [ctx.sem("p_ld0"), ctx.sem("p_ld1")]
    dq = ctx.sem("p_dq")
    wo = [ctx.sem("p_wo0"), ctx.sem("p_wo1")]

    jobs = [
        (io["i_pd"], io["i_sbd"], io["i_zbd"], io["sc_wd"], 43, 43, DQ_SLABS_D, True),
    ]
    slabs = []
    for (pk, sbv, zbv, outw, njt, lofs, slablist, is_down) in jobs:
        for jt in range(njt):
            for (c0, w) in slablist:
                slabs.append((pk, sbv, zbv, outw, jt, lofs, c0, w, is_down))

    def emit_wout(g, si):
        (pk, sbv, zbv, outw, jt, lofs, c0, w, is_down) = slabs[si]
        nt, t0 = w // 128, c0 // 128
        g.wait_ge(dq.h, si + 1)
        for src, rt in ((s_wh[si % 2], jt), (s_wl[si % 2], lofs + jt)):
            dst = outw[t0:t0 + nt, :, rt * 128:(rt + 1) * 128].rearrange("a p c -> p a c")
            wo[si % 2].inc(g.dma_start(out=dst,
                               in_=src[:, 0:w].rearrange("p (a c) -> p a c", c=128)), 16)

    def body_gpsimd(g):
        for si, (pk, sbv, zbv, outw, jt, lofs, c0, w, is_down) in enumerate(slabs):
            if hooks and si in hooks:
                hooks[si](g)
            if si >= 2:
                g.wait_ge(dq.h, si - 1)
            r0 = jt * 128
            g0 = jt * 4
            ld[si % 2].inc(g.dma_start(out=s_pi[si % 2][:, 0:w], in_=pk[r0:r0 + 128, c0:c0 + w]), 16)
            ld[si % 2].inc(g.dma_start(
                out=s_sb[si % 2][:, 0:w],
                in_=sbv[g0:g0 + 4, c0:c0 + w].unsqueeze(1).broadcast_to([4, 32, w])), 16)
            ld[si % 2].inc(g.dma_start(
                out=s_zb[si % 2][:, 0:w],
                in_=zbv[g0:g0 + 4, c0:c0 + w].unsqueeze(1).broadcast_to([4, 32, w])), 16)
            if si >= 1:
                emit_wout(g, si - 1)
        emit_wout(g, len(slabs) - 1)
        g.wait_ge(wo[0].h, wo[0].tot)
        g.wait_ge(wo[1].h, wo[1].tot)

    def body_vector(v):
        for si, (pk, sbv, zbv, outw, jt, lofs, c0, w, is_down) in enumerate(slabs):
            v.wait_ge(ld[si % 2].h, (si // 2 + 1) * 48)
            if si >= 2:
                v.wait_ge(wo[si % 2].h, (si // 2) * 32)
            b = si % 2
            hw = w // 2
            for piece in range(2):
                p0, p1 = piece * hw, piece * hw + hw
                _v = v.tensor_scalar(out=s_ti[:, 0:hw], in0=s_pi[b][:, p0:p1], scalar1=4,
                                scalar2=None, op0=ALU.logical_shift_right)
                ctx.chain(v, _v)
                _v = v.tensor_tensor(out=s_wh[b][:, p0:p1], in0=s_ti[:, 0:hw],
                                in1=s_zb[b][:, p0:p1], op=ALU.subtract)
                ctx.chain(v, _v)
                _v = v.tensor_tensor(out=s_wh[b][:, p0:p1], in0=s_wh[b][:, p0:p1],
                                in1=s_sb[b][:, p0:p1], op=ALU.mult)
                _v = v.tensor_scalar(out=s_ti[:, 0:hw], in0=s_pi[b][:, p0:p1], scalar1=15,
                                scalar2=None, op0=ALU.bitwise_and)
                ctx.chain(v, _v)
                _v = v.tensor_tensor(out=s_wl[b][:, p0:p1], in0=s_ti[:, 0:hw],
                                in1=s_zb[b][:, p0:p1], op=ALU.subtract)
                ctx.chain(v, _v)
                ins = v.tensor_tensor(out=s_wl[b][:, p0:p1], in0=s_wl[b][:, p0:p1],
                                      in1=s_sb[b][:, p0:p1], op=ALU.mult)
                if piece == 1:
                    dq.inc(ins, 1)
                else:
                    ctx.chain(v, ins)

    return {"gpsimd": body_gpsimd, "vector": body_vector}




def emit_A(ctx, io, bufs_es):
    nc = ctx.nc
    E = bufs_es.enter_context
    s_w = E(nc.sbuf_tensor("a_w", [128, 32 * 1024], BF16))
    s_wsl = [E(nc.sbuf_tensor(f"a_wsl{i}", [128, 2048], F32)) for i in range(2)]
    s_xf = E(nc.sbuf_tensor("a_xf", [128, 32 * 256], F32))
    s_xb = [E(nc.sbuf_tensor(f"a_xb{i}", [128, 32 * 256], BF16)) for i in range(2)]
    s_st = [E(nc.sbuf_tensor(f"a_st{i}", [128, 2048], BF16)) for i in range(2)]
    s_bqT = E(nc.sbuf_tensor("a_bqT", [128, 16], F32))
    ps_mm = [E(nc.psum_tensor(f"a_ps{i}", [128, 2048], F32)) for i in range(2)]

    ld_b = ctx.sem("a_ldb")
    ld_w = [ctx.sem("a_ldw0"), ctx.sem("a_ldw1")]
    ld_x = ctx.sem("a_ldx")
    cv_w = ctx.sem("a_cvw")
    cv_x = ctx.sem("a_cvx")
    mm_d = ctx.sem("a_mmd")
    ev_d = ctx.sem("a_evd")
    st_o = [ctx.sem("a_sto0"), ctx.sem("a_sto1")]

    passes = [
        ("k", 1024, 8, io["i_xqT"], io["sc_kTo"], 8),
        ("v", 2048, 8, io["i_xqT"], None, None),
        ("q", 0, 8, io["i_xqT"], io["sc_qT"], 0),
    ]
    nsl_cum = [0, 8, 16, 24]
    ctx.a_sto = st_o

    def body_sync(sync):
        ld_b.inc(sync.dma_start(out=s_bqT[:, :], in_=io["i_bqT"]), 16)
        for pi, (pname, wofs, nsl, xsrc, outT, bcol) in enumerate(passes):
            for i in range(16):
                gsi = pi * 16 + i
                if gsi >= 2:
                    sync.wait_ge(cv_w.h, gsi - 1)
                ld_w[gsi % 2].inc(sync.dma_start(
                    out=s_wsl[gsi % 2][:, :].rearrange("p (a f) -> p a f", a=2),
                    in_=io["i_wqkv"].rearrange("(a p) f -> p a f", p=128)[
                        :, 2 * i:2 * i + 2, wofs:wofs + 1024]), 16)
            for i in range(nsl):
                gxi = nsl_cum[pi] + i
                if gxi >= 1:
                    sync.wait_ge(cv_x.h, gxi)
                c0 = i * 256
                ld_x.inc(sync.dma_start(
                    out=s_xf[:, :],
                    in_=xsrc[:, c0:c0 + 256].rearrange("(a p) t -> p a t", p=128)), 16)

    def body_scalar(sc):
        for pi, (pname, wofs, nsl, xsrc, outT, bcol) in enumerate(passes):
            if pi >= 1:
                sc.wait_ge(mm_d.h, nsl_cum[pi])
            for i in range(16):
                gsi = pi * 16 + i
                sc.wait_ge(ld_w[gsi % 2].h, (gsi // 2 + 1) * 16)
                cv_w.inc(sc.activation(out=s_w[:, i * 2048:(i + 1) * 2048],
                              in_=s_wsl[gsi % 2][:, :], func=AF.Copy), 1)
            def emit_evac(i):
                gxi = nsl_cum[pi] + i
                sc.wait_ge(mm_d.h, gxi + 1)
                if gxi >= 2:
                    sc.wait_ge(st_o[gxi % 2].h, (gxi // 2) * 16)
                if pname == "v":
                    ins = sc.activation(out=s_st[i % 2][:, :], in_=ps_mm[i % 2][:, :],
                                           func=AF.Copy)
                    ev_d.inc(ins, 1)
                else:
                    for dt in range(8):
                        ins = sc.activation(out=s_st[i % 2][:, dt * 256:(dt + 1) * 256],
                                      in_=ps_mm[i % 2][:, dt * 256:(dt + 1) * 256],
                                      func=AF.Identity,
                                      bias=s_bqT[:, bcol + dt: bcol + dt + 1])
                    ev_d.inc(ins, 1)
                sc.wait_ge(ev_d.h, ev_d.tot)
                c0 = i * 256
                if pname == "v":
                    st_o[gxi % 2].inc(sc.dma_start(
                        out=io["sc_vo"][c0:c0 + 256, :].rearrange("(a p) d -> p a d", p=128),
                        in_=s_st[i % 2][:, :].rearrange("p (a d) -> p a d", a=2)), 16)
                else:
                    st_o[gxi % 2].inc(sc.dma_start(
                        out=outT.rearrange("(a p) t -> p a t", p=128)[:, :, c0:c0 + 256],
                        in_=s_st[i % 2][:, :].rearrange("p (a t) -> p a t", a=8)), 16)

            for i in range(nsl):
                gxi = nsl_cum[pi] + i
                sc.wait_ge(ld_x.h, (gxi + 1) * 16)
                if gxi >= 2:
                    sc.wait_ge(mm_d.h, gxi - 1)
                cv_x.inc(sc.activation(out=s_xb[i % 2][:, :], in_=s_xf[:, :], func=AF.Copy), 1)
                if i >= 1:
                    emit_evac(i - 1)
            emit_evac(nsl - 1)
        sc.wait_ge(st_o[0].h, st_o[0].tot)
        sc.wait_ge(st_o[1].h, st_o[1].tot)

    def body_tensor(t):
        t.wait_ge(ld_b.h, 16)
        for pi, (pname, wofs, nsl, xsrc, outT, bcol) in enumerate(passes):
            for i in range(nsl):
                gxi = nsl_cum[pi] + i
                t.wait_ge(cv_w.h, (pi + 1) * 16)
                t.wait_ge(cv_x.h, gxi + 1)
                if gxi >= 2:
                    t.wait_ge(ev_d.h, gxi - 1)
                ps = ps_mm[i % 2]
                xb = s_xb[i % 2]
                if pname == "v":
                    for tsub in range(2):
                        for a in range(32):
                            for dc in range(2):
                                o0 = tsub * 1024 + dc * 512
                                _i = t.matmul(ps[:, o0:o0 + 512],
                                         xb[:, a * 256 + tsub * 128: a * 256 + tsub * 128 + 128],
                                         s_w[:, a * 1024 + dc * 512: a * 1024 + (dc + 1) * 512],
                                         start=(a == 0), stop=(a == 31))
                else:
                    for dt in range(8):
                        for a in range(32):
                            _i = t.matmul(ps[:, dt * 256:(dt + 1) * 256],
                                     s_w[:, a * 1024 + dt * 128: a * 1024 + dt * 128 + 128],
                                     xb[:, a * 256:(a + 1) * 256],
                                     start=(a == 0), stop=(a == 31))
                mm_d.inc(_i, 1)

    return {"sync": body_sync, "scalar": body_scalar, "tensor": body_tensor}





def emit_B1(ctx, io, bufs_es):
    nc = ctx.nc
    E = bufs_es.enter_context
    s_kT = E(nc.sbuf_tensor("b_kT", [128, 8 * 8192], BF16))
    s_qt = [E(nc.sbuf_tensor(f"b_qt{i}", [128, 8 * 128], BF16)) for i in range(2)]
    s_sc = E(nc.sbuf_tensor("b_sc", [128, 8192], F32))
    s_pb = [E(nc.sbuf_tensor(f"b_pb{i}", [128, 8192], BF16)) for i in range(2)]
    s_mx = E(nc.sbuf_tensor("b_mx", [128, 16], F32))
    s_nb = E(nc.sbuf_tensor("b_nb", [128, 16], F32))
    s_l = E(nc.sbuf_tensor("b_l", [128, 16], F32))
    ps_s = [E(nc.psum_tensor(f"b_pss{i}", [128, 2048], F32)) for i in range(2)]

    b_lk = ctx.sem("b_lk")
    b_lq = [ctx.sem("b_lq0"), ctx.sem("b_lq1")]
    b_mm = ctx.sem("b_mm")
    b_ev = ctx.sem("b_ev")
    b_mx = ctx.sem("b_mx")
    b_ex = ctx.sem("b_ex")
    b_po = [ctx.sem("b_po0"), ctx.sem("b_po1")]
    b_rl = ctx.sem("b_rl")

    def body_sync(sync):
        sync.wait_ge(ctx.cs_k.h, 1)
        for r in range(4):
            b_lk.inc(sync.dma_start(
                out=s_kT[:, :].rearrange("p (a t) -> p a t", t=8192)[:, :, r * 2048:(r + 1) * 2048],
                in_=io["sc_kag"].rearrange("(r a p) t -> r p a t", r=4, p=128)[r]), 16)
        for qt in range(16):
            if qt >= 2:
                sync.wait_ge(b_mm.h, 4 * qt - 4)
            b_lq[qt % 2].inc(sync.dma_start(
                out=s_qt[qt % 2][:, :],
                in_=io["sc_qT"].rearrange("(a p) t -> p a t", p=128)[:, :, qt * 128:(qt + 1) * 128]), 16)

    def body_tensor(t):
        t.wait_ge(b_lk.h, 64)
        t.wait_ge(ctx.id_sem.h, 1)
        for qt in range(16):
            t.wait_ge(b_lq[qt % 2].h, (qt // 2 + 1) * 16)
            for su in range(4):
                gi = 4 * qt + su
                if gi >= 2:
                    t.wait_ge(b_ev.h, gi - 1)
                for dt in range(8):
                    for kc in range(4):
                        _i = t.matmul(ps_s[gi % 2][:, kc * 512:(kc + 1) * 512],
                                 s_qt[qt % 2][:, dt * 128:(dt + 1) * 128],
                                 s_kT[:, dt * 8192 + su * 2048 + kc * 512:
                                      dt * 8192 + su * 2048 + (kc + 1) * 512],
                                 start=(dt == 0), stop=(dt == 7))
                b_mm.inc(_i, 1)

    def body_scalar(sc):
        for qt in range(16):
            sc.wait_ge(b_mx.h, qt + 1)
            if qt >= 2:
                sc.wait_ge(b_po[qt % 2].h, (qt // 2) * 16)
            b_ex.inc(sc.activation(out=s_pb[qt % 2][:, :], in_=s_sc[:, :], func=AF.Exp,
                                   bias=s_nb[:, qt:qt + 1], scale=SOFT_SCALE,
                                   accum_out=s_l[:, qt:qt + 1]), 1)

    def body_vector(v):
        for qt in range(16):
            if qt >= 1:
                v.wait_ge(b_ex.h, qt)
            for su in range(4):
                gi = 4 * qt + su
                v.wait_ge(b_mm.h, gi + 1)
                _v = v.reduce_max(s_mx[:, 4 * (qt % 2) + su:4 * (qt % 2) + su + 1],
                                  ps_s[gi % 2][:, :], axis=AX.X)
                if su == 3:
                    ctx.chain(v, _v)
                b_ev.inc(v.tensor_copy(out=s_sc[:, su * 2048:(su + 1) * 2048],
                                       in_=ps_s[gi % 2][:, :]), 1)
            _v = v.reduce_max(s_nb[:, qt:qt + 1],
                              s_mx[:, 4 * (qt % 2):4 * (qt % 2) + 4], axis=AX.X)
            ctx.chain(v, _v)
            b_mx.inc(v.tensor_scalar(out=s_nb[:, qt:qt + 1], in0=s_nb[:, qt:qt + 1],
                                     scalar1=-SOFT_SCALE, scalar2=None, op0=ALU.mult), 1)
            v.wait_ge(b_ex.h, qt + 1)
            b_rl.inc(v.reciprocal(io["rl"][:, qt:qt + 1], s_l[:, qt:qt + 1]), 1)

    def body_gpsimd(g):
        for qt in range(16):
            g.wait_ge(b_ex.h, qt + 1)
            b_po[qt % 2].inc(g.dma_start(
                out=io["sc_p"][qt * 128:(qt + 1) * 128, :], in_=s_pb[qt % 2][:, :]), 16)
        g.wait_ge(b_po[0].h, b_po[0].tot)
        g.wait_ge(b_po[1].h, b_po[1].tot)

    return {"sync": body_sync, "tensor": body_tensor, "scalar": body_scalar,
            "vector": body_vector, "gpsimd": body_gpsimd}




def emit_B2(ctx, io, bufs_es, s_ot):
    nc = ctx.nc
    E = bufs_es.enter_context
    s_v = E(nc.sbuf_tensor("c_v", [128, 64 * 1024], BF16))
    s_pl = [E(nc.sbuf_tensor(f"c_pl{i}", [128, 4096], BF16)) for i in range(2)]
    s_ptl = E(nc.sbuf_tensor("c_ptl", [128, 64 * 128], BF16))
    s_o = E(nc.sbuf_tensor("c_o", [128, 1024], BF16))
    ps_o = [E(nc.psum_tensor(f"c_pso{i}", [128, 1024], F32)) for i in range(2)]
    ps_t = [E(nc.psum_tensor(f"c_pst{i}", [128, 1024], BF16)) for i in range(2)]
    ps_ot = E(nc.psum_tensor("c_psot", [128, 1024], BF16))

    c_lv = ctx.sem("c_lv")
    c_lp = [ctx.sem("c_lp0"), ctx.sem("c_lp1")]
    c_tp = ctx.sem("c_tp")
    c_pe = ctx.sem("c_pe")
    c_mm = ctx.sem("c_mm")
    c_ev = ctx.sem("c_ev")
    c_tpo = ctx.sem("c_tpo")
    c_oe = ctx.sem("c_oe")

    def body_sync(sync):
        sync.wait_ge(ctx.cs_v.h, 1)
        c_lv.inc(sync.dma_start(out=s_v[:, :],
                                in_=io["sc_vag"].rearrange("(a p) d -> p a d", p=128)), 16)
        for hh in range(32):
            if hh >= 2:
                sync.wait_ge(c_tp.h, 4 * hh - 4)
            c_lp[hh % 2].inc(sync.dma_start(
                out=s_pl[hh % 2][:, :],
                in_=io["sc_p"][(hh // 2) * 128:(hh // 2 + 1) * 128,
                               (hh % 2) * 4096:(hh % 2 + 1) * 4096]), 16)

    def body_tensor(t):
        t.wait_ge(c_lv.h, 16)
        for qt in range(16):
            if qt >= 1:
                t.wait_ge(c_mm.h, qt)
            for grp in range(8):
                gg = 8 * qt + grp
                hh = 2 * qt + grp // 4
                t.wait_ge(c_lp[hh % 2].h, (hh // 2 + 1) * 16)
                if gg >= 2:
                    t.wait_ge(c_pe.h, gg - 1)
                for j in range(8):
                    _i = t.transpose(ps_t[gg % 2][:, j * 128:(j + 1) * 128],
                                s_pl[hh % 2][:, ((grp % 4) * 8 + j) * 128:
                                             ((grp % 4) * 8 + j + 1) * 128],
                                io["ident"][:])
                c_tp.inc(_i, 1)
            t.wait_ge(c_pe.h, 8 * qt + 8)
            if qt >= 2:
                t.wait_ge(c_ev.h, qt - 1)
            for kt in range(64):
                for dc in range(2):
                    _i = t.matmul(ps_o[qt % 2][:, dc * 512:(dc + 1) * 512],
                             s_ptl[:, kt * 128:(kt + 1) * 128],
                             s_v[:, kt * 1024 + dc * 512: kt * 1024 + (dc + 1) * 512],
                             start=(kt == 0), stop=(kt == 63))
            c_mm.inc(_i, 1)
            t.wait_ge(c_ev.h, qt + 1)
            t.wait_ge(c_oe.h, qt)
            for j in range(8):
                _i = t.transpose(ps_ot[:, j * 128:(j + 1) * 128],
                            s_o[:, j * 128:(j + 1) * 128], io["ident"][:])
            c_tpo.inc(_i, 1)

    def body_scalar(sc):
        for qt in range(16):
            for grp in range(8):
                gg = 8 * qt + grp
                sc.wait_ge(c_tp.h, gg + 1)
                c_pe.inc(sc.activation(out=s_ptl[:, grp * 1024:(grp + 1) * 1024],
                                       in_=ps_t[gg % 2][:, :], func=AF.Copy), 1)
            sc.wait_ge(c_mm.h, qt + 1)
            if qt >= 1:
                sc.wait_ge(c_tpo.h, qt)
            c_ev.inc(sc.activation(out=s_o[:, :], in_=ps_o[qt % 2][:, :],
                                   func=AF.Identity, bias=0.0,
                                   scale=io["rl"][:, qt:qt + 1]), 1)
            sc.wait_ge(c_tpo.h, qt + 1)
            c_oe.inc(sc.activation(
                out=s_ot[:, :].rearrange("p (a q) -> p a q", q=2048)[:, :, qt * 128:(qt + 1) * 128],
                in_=ps_ot[:, :], func=AF.Copy), 1)

    return {"sync": body_sync, "tensor": body_tensor, "scalar": body_scalar}




def emit_B3(ctx, io, bufs_es, s_ot):
    nc = ctx.nc
    E = bufs_es.enter_context
    s_wo = E(nc.sbuf_tensor("d_wo", [128, 8 * 4096], BF16))
    s_wsl = [E(nc.sbuf_tensor(f"d_wsl{i}", [128, 1024], F32)) for i in range(2)]
    s_bo128 = E(nc.sbuf_tensor("d_bo128", [128, 4096], F32))
    s_bvT = E(nc.sbuf_tensor("d_bvT", [128, 8], BF16))
    s_ones = E(nc.sbuf_tensor("d_ones", [1, 128], BF16))
    s_xo = [E(nc.sbuf_tensor(f"d_xo{i}", [128, 4096], F32)) for i in range(2)]
    s_pr = E(nc.sbuf_tensor("d_pr", [128, 4096], F32))
    s_xe = E(nc.sbuf_tensor("d_xe", [128, 2048], BF16))
    s_xoo = E(nc.sbuf_tensor("d_xoo", [128, 2048], BF16))
    s_xms = E(nc.sbuf_tensor("d_xms", [128, 32 * 128], BF16))
    s_bor = s_pr[0:1, 0:4096]
    s_bob = s_xms[0:1, 0:4096]
    s_bx = s_xo
    s_xm = s_pr
    s_bns = E(nc.sbuf_tensor("d_bns", [128, 48], F32))
    s_mv = E(nc.sbuf_tensor("d_mv", [128, 2], F32))
    s_sd = E(nc.sbuf_tensor("d_sd", [128, 1], F32))
    s_rs = E(nc.sbuf_tensor("d_rs", [128, 1], F32))
    s_nmr = E(nc.sbuf_tensor("d_nmr", [128, 1], F32))
    ps_p = E(nc.psum_tensor("d_psp", [128, 2048], F32))
    ps_xt = [E(nc.psum_tensor(f"d_psxt{i}", [128, 512], BF16)) for i in range(2)]

    d_lb = ctx.sem("d_lb")
    d_lv = ctx.sem("d_lv")
    d_lw = [ctx.sem("d_lw0"), ctx.sem("d_lw1")]
    d_cw = ctx.sem("d_cw")
    d_on = ctx.sem("d_on")
    d_bm = ctx.sem("d_bm")
    d_bv = ctx.sem("d_bv")
    d_bc = ctx.sem("d_bc")
    d_bb = ctx.sem("d_bb")
    d_be = ctx.sem("d_be")
    d_lx = [ctx.sem("d_lx0"), ctx.sem("d_lx1")]
    d_bx = ctx.sem("d_bx")
    d_mm = ctx.sem("d_mm")
    d_pa = ctx.sem("d_pa")
    d_st = ctx.sem("d_st")
    d_sq = ctx.sem("d_sq")
    d_ln = ctx.sem("d_ln")
    d_xw = ctx.sem("d_xw")
    d_sp = ctx.sem("d_sp")
    d_tp = ctx.sem("d_tp")
    d_se = ctx.sem("d_se")
    d_so = ctx.sem("d_so")

    def body_sync(sync):
        d_lb.inc(sync.dma_start(out=s_bor[:, :], in_=io["i_bo"][None, :]), 16)
        for i in range(32):
            if i >= 2:
                sync.wait_ge(d_cw.h, i - 1)
            d_lw[i % 2].inc(sync.dma_start(
                out=s_wsl[i % 2][:, :],
                in_=io["i_wo"].rearrange("(a p) f -> a p f", p=128)[
                    i // 4, :, (i % 4) * 1024:(i % 4 + 1) * 1024]), 16)
        for tt in range(16):
            if tt >= 2:
                sync.wait_ge(d_bx.h, tt - 1)
            d_lx[tt % 2].inc(sync.dma_start(
                out=s_xo[tt % 2][:, :],
                in_=io["i_xo"][tt * 128:(tt + 1) * 128, :]), 16)
            sync.wait_ge(d_ln.h, tt + 1)
            d_xw.inc(sync.dma_start(
                out=io["sc_xmid"][tt * 128:(tt + 1) * 128, :], in_=s_pr[:, :]), 16)
        sync.wait_ge(d_xw.h, d_xw.tot)

    def body_scalar(sc):
        for i in range(32):
            sc.wait_ge(d_lw[i % 2].h, (i // 2 + 1) * 16)
            d_cw.inc(sc.activation(out=s_wo[:, i * 1024:(i + 1) * 1024],
                                   in_=s_wsl[i % 2][:, :], func=AF.Copy), 1)
        sc.wait_ge(d_bv.h, 2)
        d_bc.inc(sc.activation(out=s_bob[:, :], in_=s_bo128[0:1, :], func=AF.Copy), 1)
        for r in range(2):
            sc.wait_ge(d_bb.h, r + 1)
            d_be.inc(sc.activation(out=s_bo128[:, r * 2048:(r + 1) * 2048],
                                   in_=ps_p[:, :], func=AF.Copy), 1)
        for tt in range(16):
            sc.wait_ge(d_st.h, tt + 1)
            d_sq.inc(sc.activation(out=s_sd[:, :], in_=s_mv[:, 1:2],
                                   func=AF.Sqrt), 1)
            for grp in range(8):
                gg = 8 * tt + grp
                sc.wait_ge(d_tp.h, gg + 1)
                if grp == 0:
                    sc.wait_ge(d_so.h, 16 * tt)
                d_se.inc(sc.activation(out=s_xms[:, grp * 512:(grp + 1) * 512],
                                       in_=ps_xt[grp % 2][:, :], func=AF.Copy), 1)

    def body_tensor(t):
        t.wait_ge(d_cw.h, 32)
        t.wait_ge(d_lb.h, 16)
        t.wait_ge(d_lv.h, 16)
        for half in range(2):
            if half == 1:
                t.wait_ge(d_bv.h, 1)
            for cc in range(4):
                for dt in range(8):
                    _i = t.matmul(ps_p[0:1, cc * 512:(cc + 1) * 512],
                             s_bvT[:, dt:dt + 1],
                             s_wo[:, dt * 4096 + half * 2048 + cc * 512:
                                  dt * 4096 + half * 2048 + (cc + 1) * 512],
                             start=(dt == 0), stop=(dt == 7))
            d_bm.inc(_i, 1)
        t.wait_ge(d_bc.h, 1)
        t.wait_ge(d_on.h, 1)
        t.wait_ge(d_bv.h, 2)
        for r in range(2):
            if r == 1:
                t.wait_ge(d_be.h, 1)
            for cc in range(4):
                _i = t.matmul(ps_p[:, cc * 512:(cc + 1) * 512], s_ones[0:1, :],
                         s_bob[0:1, r * 2048 + cc * 512: r * 2048 + (cc + 1) * 512],
                         start=True, stop=True)
            d_bb.inc(_i, 1)
        for tt in range(16):
            for half in range(2):
                gi = 2 * tt + half
                t.wait_ge(d_be.h, 2) if gi == 0 else None
                if gi >= 1:
                    t.wait_ge(d_pa.h, gi)
                for dt in range(8):
                    for cc in range(4):
                        _i = t.matmul(ps_p[:, cc * 512:(cc + 1) * 512],
                                 s_ot[:, dt * 2048 + tt * 128: dt * 2048 + (tt + 1) * 128],
                                 s_wo[:, dt * 4096 + half * 2048 + cc * 512:
                                      dt * 4096 + half * 2048 + (cc + 1) * 512],
                                 start=(dt == 0), stop=(dt == 7))
                d_mm.inc(_i, 1)
            t.wait_ge(d_sp.h, tt + 1)
            for half in range(2):
                src = s_xe if half == 0 else s_xoo
                for grp in range(4):
                    gg = 8 * tt + half * 4 + grp
                    if gg >= 2:
                        t.wait_ge(d_se.h, gg - 1)
                    for j in range(4):
                        _i = t.transpose(ps_xt[gg % 2][:, j * 128:(j + 1) * 128],
                                    src[:, (grp * 4 + j) * 128:(grp * 4 + j + 1) * 128],
                                    io["ident"][:])
                    d_tp.inc(_i, 1)

    def body_vector(v):
        for half in range(2):
            v.wait_ge(d_bm.h, half + 1)
            v.wait_ge(d_lb.h, 16)
            d_bv.inc(v.tensor_tensor(out=s_bo128[0:1, half * 2048:(half + 1) * 2048],
                                     in0=ps_p[0:1, 0:2048],
                                     in1=s_bor[0:1, half * 2048:(half + 1) * 2048],
                                     op=ALU.add), 1)
        for tt in range(16):
            v.wait_ge(d_lx[tt % 2].h, (tt // 2 + 1) * 16)
            v.wait_ge(d_be.h, 2)
            if tt >= 1:
                v.wait_ge(d_pa.h, 2 * tt)
            d_bx.inc(v.tensor_tensor(out=s_xo[tt % 2][:, :], in0=s_bo128[:, :],
                                     in1=s_xo[tt % 2][:, :], op=ALU.add), 1)
            v.wait_ge(d_bx.h, d_bx.tot)
            for half in range(2):
                gi = 2 * tt + half
                v.wait_ge(d_mm.h, gi + 1)
                if half == 0 and tt >= 1:
                    v.wait_ge(d_xw.h, 16 * tt)
                d_pa.inc(v.tensor_tensor(out=s_pr[:, half * 2048:(half + 1) * 2048],
                                         in0=ps_p[:, :],
                                         in1=s_xo[tt % 2][:, half * 2048:(half + 1) * 2048],
                                         op=ALU.add), 1)
            v.wait_ge(d_pa.h, d_pa.tot)
            for ch in range(8):
                _v = v.bn_stats(s_bns[:, ch * 6:(ch + 1) * 6], s_pr[:, ch * 512:(ch + 1) * 512])
            ctx.chain(v, _v)
            _v = v.bn_aggr(s_mv[:, :], s_bns[:, :])
            ctx.chain(v, _v)
            d_st.inc(v.tensor_scalar(out=s_mv[:, 1:2], in0=s_mv[:, 1:2], scalar1=EPS,
                                     scalar2=None, op0=ALU.add), 1)
            v.wait_ge(d_sq.h, tt + 1)
            _v = v.reciprocal(s_rs[:, :], s_sd[:, :])
            ctx.chain(v, _v)
            _v = v.tensor_scalar(out=s_nmr[:, :], in0=s_mv[:, 0:1], scalar1=s_rs[:, 0:1],
                            scalar2=-1.0, op0=ALU.mult, op1=ALU.mult)
            ctx.chain(v, _v)
            d_ln.inc(v.tensor_scalar(out=s_pr[:, :], in0=s_pr[:, :], scalar1=s_rs[:, 0:1],
                                     scalar2=s_nmr[:, 0:1], op0=ALU.mult, op1=ALU.add), 1)
            v.wait_ge(d_ln.h, d_ln.tot)
            if tt >= 1:
                v.wait_ge(d_tp.h, 8 * tt)
            v.tensor_copy(out=s_xe[:, :], in_=s_pr[:, 0:4096:2])
            d_sp.inc(v.tensor_copy(out=s_xoo[:, :], in_=s_pr[:, 1:4096:2]), 1)

    def body_gpsimd(g):
        d_lv.inc(g.dma_start(out=s_bvT[:, :], in_=io["i_bvT"]), 16)
        d_on.inc(g.memset(s_ones[:, :], 1.0), 1)
        for tt in range(16):
            g.wait_ge(d_se.h, 8 * tt + 8)
            d_so.inc(g.dma_start(
                out=io["sc_xmT"][:, :, tt * 128:(tt + 1) * 128].rearrange("a p t -> p a t"),
                in_=s_xms[:, :].rearrange("p (a t) -> p a t", t=128)), 16)
        g.wait_ge(d_so.h, d_so.tot)

    return {"sync": body_sync, "scalar": body_scalar, "tensor": body_tensor,
            "vector": body_vector, "gpsimd": body_gpsimd}




def emit_C(ctx, io, bufs_es):
    nc = ctx.nc
    E = bufs_es.enter_context
    s_xm = E(nc.sbuf_tensor("e_xm", [128, 32 * 1024], BF16))
    s_wt = [[E(nc.sbuf_tensor(f"e_w{w}{i}", [128, 32 * 128], BF16)) for i in range(2)]
            for w in range(2)]
    s_pi = [E(nc.sbuf_tensor(f"e_pi{i}", [128, 2048], I32)) for i in range(2)]
    s_ti = E(nc.sbuf_tensor("e_ti", [128, 2048], I32))
    s_sb = [E(nc.sbuf_tensor(f"e_sb{i}", [128, 2048], BF16)) for i in range(2)]
    s_zb = [E(nc.sbuf_tensor(f"e_zb{i}", [128, 2048], BF16)) for i in range(2)]
    s_sg = E(nc.sbuf_tensor("e_sg", [128, 1024], F32))
    s_ht = [E(nc.sbuf_tensor(f"e_ht{i}", [128, 1024], BF16)) for i in range(2)]
    ps_g = [E(nc.psum_tensor(f"e_psg{i}", [128, 1024], F32)) for i in range(2)]
    ps_u = [E(nc.psum_tensor(f"e_psu{i}", [128, 1024], F32)) for i in range(2)]

    e_lx = ctx.sem("e_lx")
    e_ld = [ctx.sem("e_ld0"), ctx.sem("e_ld1")]
    e_dq = ctx.sem("e_dq")
    e_mm = ctx.sem("e_mm")
    e_sl = ctx.sem("e_sl")
    e_hv = ctx.sem("e_hv")
    e_ho = ctx.sem("e_ho")

    MATIN = [(io["i_pg"], io["i_sbg"], io["i_zbg"]),
             (io["i_pu"], io["i_sbu"], io["i_zbu"])]

    def body_gpsimd2(g):
        for dqg in range(344):
            it, mat = (dqg % 172) // 2, dqg % 2
            b = dqg % 2
            if dqg >= 2:
                g.wait_ge(e_dq.h, dqg - 1)
            pk, sbv, zbv = MATIN[mat]
            e_ld[b].inc(g.dma_start(
                out=s_pi[b][:, :],
                in_=pk.rearrange("(a p) n -> p a n", p=128)[:, :, it * 128:(it + 1) * 128]), 16)
            for src, dst in ((sbv, s_sb[b]), (zbv, s_zb[b])):
                for g0 in range(4):
                    e_ld[b].inc(g.dma_start(
                        out=dst[g0 * 32:(g0 + 1) * 32, :],
                        in_=src.rearrange("(j g) n -> g j n", g=4)[g0, :, it * 128:(it + 1) * 128]
                            .unsqueeze(0).broadcast_to([32, 16, 128])), 16)

    def body_sync(sync):
        for pss in range(2):
            if pss == 1:
                sync.wait_ge(e_mm.h, 172)
            e_lx.inc(sync.dma_start(
                out=s_xm[:, :],
                in_=io["sc_xmT"][:, :, pss * 1024:(pss + 1) * 1024]
                    .rearrange("a p t -> p a t")), 16)
            for it in range(86):
                git = pss * 86 + it
                sync.wait_ge(e_hv.h, git + 1)
                e_ho.inc(sync.dma_start(
                    out=io["sc_hT"][it][:, pss * 1024:(pss + 1) * 1024],
                    in_=s_ht[git % 2][:, :]), 16)
        sync.wait_ge(e_ho.h, e_ho.tot)

    def body_vector(v):
        for pss in range(2):
            for it in range(86):
                git = pss * 86 + it
                for mat in range(2):
                    dqg = 2 * git + mat
                    b = dqg % 2
                    wb = it % 2
                    v.wait_ge(e_ld[b].h, (dqg // 2 + 1) * 144)
                    if dqg >= 4:
                        v.wait_ge(e_mm.h, dqg - 3)
                    hi = s_wt[mat][wb][:, 0:2048]
                    lo = s_wt[mat][wb][:, 2048:4096]
                    _v = v.tensor_scalar(out=s_ti[:, :], in0=s_pi[b][:, :], scalar1=4,
                                         scalar2=None, op0=ALU.logical_shift_right)
                    ctx.chain(v, _v)
                    _v = v.tensor_tensor(out=hi, in0=s_ti[:, :], in1=s_zb[b][:, :],
                                         op=ALU.subtract)
                    ctx.chain(v, _v)
                    v.tensor_tensor(out=hi, in0=hi, in1=s_sb[b][:, :], op=ALU.mult)
                    _v = v.tensor_scalar(out=s_ti[:, :], in0=s_pi[b][:, :], scalar1=15,
                                         scalar2=None, op0=ALU.bitwise_and)
                    ctx.chain(v, _v)
                    _v = v.tensor_tensor(out=lo, in0=s_ti[:, :], in1=s_zb[b][:, :],
                                         op=ALU.subtract)
                    ctx.chain(v, _v)
                    e_dq.inc(v.tensor_tensor(out=lo, in0=lo, in1=s_sb[b][:, :],
                                             op=ALU.mult), 1)
                if git >= 1:
                    emit_hmult(v, git - 1)
            if pss == 1:
                emit_hmult(v, 171)

    def emit_hmult(v, git):
        v.wait_ge(e_mm.h, 2 * git + 2)
        v.wait_ge(e_sl.h, git + 1)
        if git >= 2:
            v.wait_ge(e_ho.h, 16 * (git - 1))
        e_hv.inc(v.tensor_tensor(out=s_ht[git % 2][:, :], in0=s_sg[:, :],
                                 in1=ps_u[git % 2][:, :], op=ALU.mult), 1)

    def body_tensor(t):
        for pss in range(2):
            t.wait_ge(e_lx.h, 16 * (pss + 1))
            for it in range(86):
                git = pss * 86 + it
                wb = it % 2
                for mat in range(2):
                    dqg = 2 * git + mat
                    t.wait_ge(e_dq.h, dqg + 1)
                    if git >= 2:
                        t.wait_ge((e_sl if mat == 0 else e_hv).h, git - 1)
                    ps = (ps_g if mat == 0 else ps_u)[wb]
                    for rt in range(32):
                        for tc in range(2):
                            _i = t.matmul(ps[:, tc * 512:(tc + 1) * 512],
                                     s_wt[mat][wb][:, rt * 128:(rt + 1) * 128],
                                     s_xm[:, rt * 1024 + tc * 512:
                                          rt * 1024 + (tc + 1) * 512],
                                     start=(rt == 0), stop=(rt == 31))
                    e_mm.inc(_i, 1)

    def body_scalar(sc):
        for git in range(172):
            sc.wait_ge(e_mm.h, 2 * git + 1)
            if git >= 1:
                sc.wait_ge(e_hv.h, git)
            e_sl.inc(sc.activation(out=s_sg[:, :], in_=ps_g[git % 2][:, :],
                                   func=AF.Silu), 1)

    return {"sync": body_sync, "tensor": body_tensor, "scalar": body_scalar,
            "vector": body_vector, "gpsimd": body_gpsimd2}




def emit_D(ctx, io, bufs_es):
    nc = ctx.nc
    E = bufs_es.enter_context
    s_hq = E(nc.sbuf_tensor("f_hq", [128, 86 * 512], BF16))
    s_wd2 = [E(nc.sbuf_tensor(f"f_wd{i}", [128, 86 * 128], BF16)) for i in range(2)]
    s_tmp = [E(nc.sbuf_tensor(f"f_tmp{i}", [128, 512], BF16)) for i in range(2)]
    s_ff = E(nc.sbuf_tensor("f_ff", [128, 4 * 4096], BF16))
    s_xo2 = [E(nc.sbuf_tensor("f_xo0", [128, 4096], F32))] * 2
    s_row = E(nc.sbuf_tensor("f_row", [128, 4096], F32))
    s_bns = E(nc.sbuf_tensor("f_bns", [128, 48], F32))
    s_mv = E(nc.sbuf_tensor("f_mv", [128, 2], F32))
    s_sd = E(nc.sbuf_tensor("f_sd", [128, 1], F32))
    s_rs = E(nc.sbuf_tensor("f_rs", [128, 1], F32))
    s_nmr = E(nc.sbuf_tensor("f_nmr", [128, 1], F32))
    ps_d = [E(nc.psum_tensor(f"f_psd{i}", [128, 512], F32)) for i in range(2)]
    ps_t2 = [E(nc.psum_tensor(f"f_pst{i}", [128, 512], BF16)) for i in range(2)]

    f_lh = [ctx.sem("f_lh0"), ctx.sem("f_lh1")]
    f_lw = [ctx.sem("f_lw0"), ctx.sem("f_lw1")]
    f_mm = ctx.sem("f_mm")
    f_ev = ctx.sem("f_ev")
    f_tp = ctx.sem("f_tp")
    f_fe = ctx.sem("f_fe")
    f_lx = [ctx.sem("f_lx0"), ctx.sem("f_lx1")]
    f_ad = ctx.sem("f_ad")
    f_st = ctx.sem("f_st")
    f_sq = ctx.sem("f_sq")
    f_ln = ctx.sem("f_ln")
    f_oo = ctx.sem("f_oo")

    def body_sync(sync):
        for tb in range(4):
            for ht in range(32):
                ghi = 32 * tb + ht
                if ghi >= 2:
                    sync.wait_ge(f_mm.h, ghi - 1)
                f_lw[ghi % 2].inc(sync.dma_start(
                    out=s_wd2[ghi % 2][:, :],
                    in_=io["sc_wd"][ht]), 16)

    def body_gpsimd(g):
        for tb in range(4):
            for tt in range(4):
                gti = 4 * tb + tt
                if gti >= 1:
                    g.wait_ge(f_ad.h, gti)
                f_lx[gti % 2].inc(g.dma_start(
                    out=s_xo2[gti % 2][:, :],
                    in_=io["sc_xmid"][gti * 128:(gti + 1) * 128, :]), 16)
                g.wait_ge(f_ln.h, gti + 1)
                f_oo.inc(g.dma_start(
                    out=io["out"][gti * 128:(gti + 1) * 128, :], in_=s_row[:, :]), 16)
        g.wait_ge(f_oo.h, f_oo.tot)

    def body_tensor(t):
        for tb in range(4):
            t.wait_ge(f_lh[0].h, 16 * (tb + 1))
            t.wait_ge(f_lh[1].h, 16 * (tb + 1))
            for ht in range(32):
                ghi = 32 * tb + ht
                t.wait_ge(f_lw[ghi % 2].h, (ghi // 2 + 1) * 16)
                if ghi >= 2:
                    t.wait_ge(f_ev.h, ghi - 1)
                for rt in range(86):
                    _i = t.matmul(ps_d[ghi % 2][:, :],
                             s_wd2[ghi % 2][:, rt * 128:(rt + 1) * 128],
                             s_hq[:, rt * 512:(rt + 1) * 512],
                             start=(rt == 0), stop=(rt == 85))
                f_mm.inc(_i, 1)
                t.wait_ge(f_ev.h, ghi + 1)
                if ghi >= 2:
                    t.wait_ge(f_fe.h, ghi - 1)
                for j in range(4):
                    _i = t.transpose(ps_t2[ghi % 2][:, j * 128:(j + 1) * 128],
                                s_tmp[ghi % 2][:, j * 128:(j + 1) * 128], io["ident"][:])
                f_tp.inc(_i, 1)

    def body_scalar(sc):
        for tb in range(4):
            if tb >= 1:
                sc.wait_ge(f_mm.h, 32 * tb)
            for hf in range(2):
                f_lh[hf].inc(sc.dma_start(
                    out=s_hq[:, hf * 43 * 512:(hf + 1) * 43 * 512]
                        .rearrange("p (a t) -> p a t", t=512),
                    in_=io["sc_hT"][hf * 43:(hf + 1) * 43, :, tb * 512:(tb + 1) * 512]
                        .rearrange("a p t -> p a t")), 16)
            for ht in range(32):
                ghi = 32 * tb + ht
                sc.wait_ge(f_mm.h, ghi + 1)
                if ghi >= 2:
                    sc.wait_ge(f_tp.h, ghi - 1)
                f_ev.inc(sc.activation(out=s_tmp[ghi % 2][:, :], in_=ps_d[ghi % 2][:, :],
                                       func=AF.Copy), 1)
                sc.wait_ge(f_tp.h, ghi + 1)
                if ht == 0:
                    sc.wait_ge(f_ad.h, 4 * tb)
                f_fe.inc(sc.activation(
                    out=s_ff[:, :].rearrange("p (a c) -> p a c", c=4096)[:, :, ht * 128:(ht + 1) * 128],
                    in_=ps_t2[ghi % 2][:, :], func=AF.Copy), 1)
            for tt in range(4):
                gti = 4 * tb + tt
                sc.wait_ge(f_st.h, gti + 1)
                f_sq.inc(sc.activation(out=s_sd[:, :], in_=s_mv[:, 1:2],
                                       func=AF.Sqrt), 1)

    def body_vector(v):
        for tb in range(4):
            for tt in range(4):
                gti = 4 * tb + tt
                v.wait_ge(f_fe.h, 32 * (tb + 1))
                v.wait_ge(f_lx[gti % 2].h, (gti // 2 + 1) * 16)
                if gti >= 1:
                    v.wait_ge(f_oo.h, 16 * gti)
                f_ad.inc(v.tensor_tensor(out=s_row[:, :],
                                         in0=s_ff[:, tt * 4096:(tt + 1) * 4096],
                                         in1=s_xo2[gti % 2][:, :], op=ALU.add), 1)
                v.wait_ge(f_ad.h, f_ad.tot)
                for ch in range(8):
                    _v = v.bn_stats(s_bns[:, ch * 6:(ch + 1) * 6],
                               s_row[:, ch * 512:(ch + 1) * 512])
                ctx.chain(v, _v)
                _v = v.bn_aggr(s_mv[:, :], s_bns[:, :])
                ctx.chain(v, _v)
                f_st.inc(v.tensor_scalar(out=s_mv[:, 1:2], in0=s_mv[:, 1:2], scalar1=EPS,
                                         scalar2=None, op0=ALU.add), 1)
                v.wait_ge(f_sq.h, gti + 1)
                _v = v.reciprocal(s_rs[:, :], s_sd[:, :])
                ctx.chain(v, _v)
                _v = v.tensor_scalar(out=s_nmr[:, :], in0=s_mv[:, 0:1], scalar1=s_rs[:, 0:1],
                                scalar2=-1.0, op0=ALU.mult, op1=ALU.mult)
                ctx.chain(v, _v)
                f_ln.inc(v.tensor_scalar(out=s_row[:, :], in0=s_row[:, :],
                                         scalar1=s_rs[:, 0:1], scalar2=s_nmr[:, 0:1],
                                         op0=ALU.mult, op1=ALU.add), 1)

    return {"sync": body_sync, "tensor": body_tensor, "scalar": body_scalar,
            "vector": body_vector, "gpsimd": body_gpsimd}



PHASES_ORDER = ["A", "B1", "B2", "B3", "C", "D"]


def build_nc(upto="D", dev_outputs=()):
    nc = bass.Bass("TRN2", target_bir_lowering=False)

    def scratch(name, shape, dtype):
        kind = "ExternalOutput" if name in dev_outputs else "Internal"
        return nc.dram_tensor(name, shape, dtype, kind=kind).ap()

    io = {}
    io["i_xqT"] = nc.dram_tensor("i_xqT", [H, TOWN], F32, kind="ExternalInput").ap()
    io["i_xo"] = nc.dram_tensor("i_xo", [TOWN, H], F32, kind="ExternalInput").ap()
    io["i_wqkv"] = nc.dram_tensor("i_wqkv", [H, 3 * D], F32, kind="ExternalInput").ap()
    io["i_bqT"] = nc.dram_tensor("i_bqT", [128, 16], F32, kind="ExternalInput").ap()
    io["i_bo"] = nc.dram_tensor("i_bo", [H], F32, kind="ExternalInput").ap()
    io["i_bvT"] = nc.dram_tensor("i_bvT", [128, 8], F32, kind="ExternalInput").ap()
    io["i_wo"] = nc.dram_tensor("i_wo", [D, H], F32, kind="ExternalInput").ap()
    io["i_pg"] = nc.dram_tensor("i_pg", [2048, NI], U8, kind="ExternalInput").ap()
    io["i_pu"] = nc.dram_tensor("i_pu", [2048, NI], U8, kind="ExternalInput").ap()
    io["i_pd"] = nc.dram_tensor("i_pd", [NI // 2, H], U8, kind="ExternalInput").ap()
    io["i_sbg"] = nc.dram_tensor("i_sbg", [64, NI], F32, kind="ExternalInput").ap()
    io["i_zbg"] = nc.dram_tensor("i_zbg", [64, NI], F32, kind="ExternalInput").ap()
    io["i_sbu"] = nc.dram_tensor("i_sbu", [64, NI], F32, kind="ExternalInput").ap()
    io["i_zbu"] = nc.dram_tensor("i_zbu", [64, NI], F32, kind="ExternalInput").ap()
    io["i_sbd"] = nc.dram_tensor("i_sbd", [172, H], F32, kind="ExternalInput").ap()
    io["i_zbd"] = nc.dram_tensor("i_zbd", [172, H], F32, kind="ExternalInput").ap()

    io["sc_qT"] = scratch("sc_qT", [D, TOWN], BF16)
    io["sc_kTo"] = scratch("sc_kTo", [D, TOWN], BF16)
    io["sc_kag"] = scratch("sc_kag", [4 * D, TOWN], BF16)
    io["sc_vo"] = scratch("sc_vo", [TOWN, D], BF16)
    io["sc_vag"] = scratch("sc_vag", [S, D], BF16)
    io["sc_p"] = scratch("sc_p", [TOWN, S], BF16)
    io["sc_xmid"] = scratch("sc_xmid", [TOWN, H], F32)
    io["sc_xmT"] = scratch("sc_xmT", [32, 128, TOWN], BF16)
    io["sc_wd"] = scratch("sc_wd", [32, 128, 86 * 128], BF16)
    io["sc_hT"] = scratch("sc_hT", [86, 128, TOWN], BF16)
    io["out"] = nc.dram_tensor("out", [TOWN, H], F32, kind="ExternalOutput").ap()

    RG = [[0, 1, 2, 3], [4, 5, 6, 7]]
    with contextlib.ExitStack() as ges:
        ctx = Ctx(nc, ges)
        io["ident"] = ges.enter_context(nc.sbuf_tensor("g_ident", [128, 128], BF16))
        io["rl"] = ges.enter_context(nc.sbuf_tensor("g_rl", [128, 16], F32))
        ctx.id_sem = ctx.sem("g_id")
        ctx.cs_k = ctx.sem("cs_k")
        ctx.cs_v = ctx.sem("cs_v")

        def mk_ag(src_key, dst_key, sto_need, cs):
            def hook(g):
                g.wait_ge(ctx.a_sto[0].h, sto_need)
                g.wait_ge(ctx.a_sto[1].h, sto_need)
                cs.inc(g.collective_compute(
                    "AllGather", ALU.bypass, replica_groups=RG,
                    ins=[io[src_key].opt()], outs=[io[dst_key].opt()]), 1)
            return hook

        with contextlib.ExitStack() as ph:
            em_a = emit_A(ctx, io, ph)
            hooks = {50: mk_ag("sc_kTo", "sc_kag", 64, ctx.cs_k),
                     72: mk_ag("sc_vo", "sc_vag", 128, ctx.cs_v)}
            em_p = emit_P0(ctx, io, ph, hooks)

            def gp(g):
                _g = g.memset(io["ident"][:], 0.0)
                ctx.chain(g, _g)
                _g = g.affine_select(out=io["ident"][:], in_=io["ident"][:],
                                compare_op=ALU.not_equal, fill=1.0, base=0,
                                pattern=[[-1, 128]], channel_multiplier=1)
                ctx.id_sem.inc(_g, 1)
                em_p["gpsimd"](g)

            run_phase(ctx, "ph_A", {
                "sync": em_a["sync"], "scalar": em_a["scalar"], "tensor": em_a["tensor"],
                "gpsimd": gp, "vector": em_p["vector"]})

        if upto == "A":
            return nc, io

        with contextlib.ExitStack() as ph:
            em = emit_B1(ctx, io, ph)
            run_phase(ctx, "ph_B1", em)
        if upto == "B1":
            return nc, io
        with contextlib.ExitStack() as oter:
            s_ot = oter.enter_context(nc.sbuf_tensor("g_ot", [128, 8 * 2048], BF16))
            with contextlib.ExitStack() as ph:
                em = emit_B2(ctx, io, ph, s_ot)
                run_phase(ctx, "ph_B2", em)
            if upto != "B2":
                with contextlib.ExitStack() as ph:
                    em = emit_B3(ctx, io, ph, s_ot)
                    run_phase(ctx, "ph_B3", em)
        if upto in ("B2", "B3"):
            return nc, io
        with contextlib.ExitStack() as ph:
            em = emit_C(ctx, io, ph)
            run_phase(ctx, "ph_C", em)
        if upto == "C":
            return nc, io
        with contextlib.ExitStack() as ph:
            em = emit_D(ctx, io, ph)
            run_phase(ctx, "ph_D", em)
    return nc, io


def _host_inputs(inputs):
    x = np.ascontiguousarray(np.asarray(inputs["L0_attn_x"], dtype=np.float32))
    wqkv = np.ascontiguousarray(np.asarray(inputs["L0_attn_Wqkv"], dtype=np.float32))
    bqkv = np.ascontiguousarray(np.asarray(inputs["L0_attn_bqkv"], dtype=np.float32))
    wo = np.ascontiguousarray(np.asarray(inputs["L0_attn_Wo"], dtype=np.float32))
    bo = np.ascontiguousarray(np.asarray(inputs["L0_attn_bo"], dtype=np.float32))

    perm = np.concatenate([np.arange(0, NI, 2), np.arange(1, NI, 2)])

    def pack_bytes(wq_i32):
        return np.asarray(wq_i32, dtype=np.int32).view(np.uint8)[:, 0::4]

    pg = np.ascontiguousarray(pack_bytes(inputs["L0_ffn_gate_Wq"])[perm].T)
    pu = np.ascontiguousarray(pack_bytes(inputs["L0_ffn_up_Wq"])[perm].T)
    pd = np.ascontiguousarray(pack_bytes(inputs["L0_ffn_down_Wq"]).T)

    def tp(a):
        return np.ascontiguousarray(np.asarray(a, dtype=np.float32).T)

    sbg = tp(np.asarray(inputs["L0_ffn_gate_scale"])[perm])
    zbg = tp(np.asarray(inputs["L0_ffn_gate_zero"])[perm])
    sbu = tp(np.asarray(inputs["L0_ffn_up_scale"])[perm])
    zbu = tp(np.asarray(inputs["L0_ffn_up_zero"])[perm])
    sbd = tp(np.asarray(inputs["L0_ffn_down_scale"]))
    zbd = tp(np.asarray(inputs["L0_ffn_down_zero"]))

    bqT = np.ascontiguousarray(bqkv[0:2048].reshape(16, 128).T)
    bvT = np.ascontiguousarray(bqkv[2048:3072].reshape(8, 128).T)

    in_maps = []
    for c in range(NCORE):
        b, r = c // 4, c % 4
        in_maps.append(dict(
            i_xqT=np.ascontiguousarray(x[b, r * TOWN:(r + 1) * TOWN].T),
            i_xo=np.ascontiguousarray(x[b, r * TOWN:(r + 1) * TOWN]),
            i_wqkv=wqkv, i_bqT=bqT, i_bo=bo, i_bvT=bvT, i_wo=wo,
            i_pg=pg, i_pu=pu, i_pd=pd,
            i_sbg=sbg, i_zbg=zbg, i_sbu=sbu, i_zbu=zbu, i_sbd=sbd, i_zbd=zbd,
        ))
    return in_maps


def kernel(**inputs):
    in_maps = _host_inputs(inputs)
    nc, io = build_nc()
    res = run_bass_kernel_spmd(nc, in_maps, core_ids=list(range(NCORE)))
    out = np.empty((B, S, H), dtype=np.float32)
    for c in range(NCORE):
        b, r = c // 4, c % 4
        out[b, r * TOWN:(r + 1) * TOWN] = res.results[c]["out"]
    return out


# version 18
# speedup vs baseline: 1.3212x; 1.0242x over previous
import contextlib
import numpy as np
import concourse.bass as bass
import concourse.mybir as mybir
from concourse.bass_utils import run_bass_kernel_spmd

F32 = mybir.dt.float32
BF16 = mybir.dt.bfloat16
U8 = mybir.dt.uint8
I32 = mybir.dt.int32
I16 = mybir.dt.int16
AF = mybir.ActivationFunctionType
ALU = mybir.AluOpType
AX = mybir.AxisListType

B, S, H, D, NI = 2, 8192, 4096, 1024, 11008
TOWN = 2048
NCORE = 8
SOFT_SCALE = 0.03125
EPS = 1e-5
DQ_SLABS_D = [(0, 2048), (2048, 2048)]
RG = [[0, 1, 2, 3], [4, 5, 6, 7]]


class SemT:
    def __init__(self, nc, es, name):
        self.h = es.enter_context(nc.semaphore(name))
        self.tot = 0

    def inc(self, instr, n=16):
        instr.then_inc(self.h, n)
        self.tot += n
        return instr


class Ctx:
    def __init__(self, nc, es):
        self.nc = nc
        self.es = es
        self.bar = SemT(nc, es, "bar")
        self.sems = {}
        self.chain_sems = {}

    def sem(self, name):
        s = SemT(self.nc, self.es, name)
        self.sems[name] = s
        return s

    def chain(self, eng, ins):
        key = ins.ins.engine if hasattr(ins, "ins") else None
        name = f"ch_{key}"
        if name not in self.chain_sems:
            self.chain_sems[name] = SemT(self.nc, self.es, name)
        s = self.chain_sems[name]
        s.inc(ins, 1)
        eng.wait_ge(s.h, s.tot)


ALL_ENGINES = ("tensor", "vector", "scalar", "gpsimd", "sync")


def run_phase(ctx, name, emitters):
    nc = ctx.nc
    base = ctx.bar.tot
    ctx.bar.tot += len(ALL_ENGINES)
    tgt = ctx.bar.tot

    with nc.Block(name) as blk:
        for ename in ALL_ENGINES:
            fn = emitters.get(ename)

            def body(eng, fn=fn):
                if fn is not None:
                    fn(eng)
                eng.drain().then_inc(ctx.bar.h, 1)
                eng.wait_ge(ctx.bar.h, tgt)
            getattr(blk, ename)(body)




def emit_P0(ctx, io, bufs_es, hooks=None):
    nc = ctx.nc
    E = bufs_es.enter_context
    W = 2048
    s_pi = [E(nc.sbuf_tensor(f"p_pi{i}", [128, W], I16)) for i in range(2)]
    s_sb = [E(nc.sbuf_tensor(f"p_sb{i}", [128, W], BF16)) for i in range(2)]
    s_zb = [E(nc.sbuf_tensor(f"p_zb{i}", [128, W], BF16)) for i in range(2)]
    s_ti = [E(nc.sbuf_tensor(f"p_ti{i}", [128, W], I16)) for i in range(2)]
    s_wh = [E(nc.sbuf_tensor(f"p_wh{i}", [128, W], BF16)) for i in range(2)]
    s_wl = [E(nc.sbuf_tensor(f"p_wl{i}", [128, W], BF16)) for i in range(2)]

    ld = [ctx.sem("p_ld0"), ctx.sem("p_ld1")]
    dqs = [ctx.sem("p_dqv"), ctx.sem("p_dqp")]
    wo = [ctx.sem("p_wo0"), ctx.sem("p_wo1")]

    jobs = [
        (io["i_pd"], io["i_sbd"], io["i_zbd"], io["sc_wd"], 43, 43, DQ_SLABS_D, True),
    ]
    slabs = []
    for (pk, sbv, zbv, outw, njt, lofs, slablist, is_down) in jobs:
        for jt in range(njt):
            for (c0, w) in slablist:
                slabs.append((pk, sbv, zbv, outw, jt, lofs, c0, w, is_down))

    def emit_dq(eng, si):
        (pk, sbv, zbv, outw, jt, lofs, c0, w, is_down) = slabs[si]
        b = si % 2
        eng.wait_ge(ld[b].h, (si // 2 + 1) * 48)
        if si >= 2:
            eng.wait_ge(wo[b].h, (si // 2) * 32)
        _v = eng.tensor_scalar(out=s_ti[b][:, 0:w], in0=s_pi[b][:, 0:w], scalar1=4,
                               scalar2=None, op0=ALU.logical_shift_right)
        ctx.chain(eng, _v)
        _v = eng.tensor_tensor(out=s_wh[b][:, 0:w], in0=s_ti[b][:, 0:w],
                               in1=s_zb[b][:, 0:w], op=ALU.subtract)
        ctx.chain(eng, _v)
        eng.tensor_tensor(out=s_wh[b][:, 0:w], in0=s_wh[b][:, 0:w],
                          in1=s_sb[b][:, 0:w], op=ALU.mult)
        _v = eng.tensor_scalar(out=s_ti[b][:, 0:w], in0=s_pi[b][:, 0:w], scalar1=15,
                               scalar2=None, op0=ALU.bitwise_and)
        ctx.chain(eng, _v)
        _v = eng.tensor_tensor(out=s_wl[b][:, 0:w], in0=s_ti[b][:, 0:w],
                               in1=s_zb[b][:, 0:w], op=ALU.subtract)
        ctx.chain(eng, _v)
        dqs[b].inc(eng.tensor_tensor(out=s_wl[b][:, 0:w], in0=s_wl[b][:, 0:w],
                                     in1=s_sb[b][:, 0:w], op=ALU.mult), 1)

    def emit_wout(g, si):
        (pk, sbv, zbv, outw, jt, lofs, c0, w, is_down) = slabs[si]
        nt, t0 = w // 128, c0 // 128
        g.wait_ge(dqs[si % 2].h, si // 2 + 1)
        for src, rt in ((s_wh[si % 2], jt), (s_wl[si % 2], lofs + jt)):
            dst = outw[t0:t0 + nt, :, rt * 128:(rt + 1) * 128].rearrange("a p c -> p a c")
            wo[si % 2].inc(g.dma_start(out=dst,
                               in_=src[:, 0:w].rearrange("p (a c) -> p a c", c=128)), 16)

    def body_gpsimd(g):
        for si, (pk, sbv, zbv, outw, jt, lofs, c0, w, is_down) in enumerate(slabs):
            for fn in (hooks or {}).get(si, []):
                fn(g)
            if si >= 2:
                g.wait_ge(dqs[si % 2].h, si // 2)
            r0 = jt * 128
            g0 = jt * 4
            ld[si % 2].inc(g.dma_start(out=s_pi[si % 2][:, 0:w], in_=pk[r0:r0 + 128, c0:c0 + w]), 16)
            ld[si % 2].inc(g.dma_start(
                out=s_sb[si % 2][:, 0:w],
                in_=sbv[g0:g0 + 4, c0:c0 + w].unsqueeze(1).broadcast_to([4, 32, w])), 16)
            ld[si % 2].inc(g.dma_start(
                out=s_zb[si % 2][:, 0:w],
                in_=zbv[g0:g0 + 4, c0:c0 + w].unsqueeze(1).broadcast_to([4, 32, w])), 16)
            if si >= 1:
                emit_wout(g, si - 1)
        emit_wout(g, len(slabs) - 1)
        g.wait_ge(wo[0].h, wo[0].tot)
        g.wait_ge(wo[1].h, wo[1].tot)

    def body_vector(v):
        for si in range(len(slabs)):
            emit_dq(v, si)

    return {"gpsimd": body_gpsimd, "vector": body_vector}




def emit_A(ctx, io, bufs_es):
    nc = ctx.nc
    E = bufs_es.enter_context
    s_w = E(nc.sbuf_tensor("a_w", [128, 32 * 1024], BF16))
    s_wsl = [E(nc.sbuf_tensor(f"a_wsl{i}", [128, 2048], F32)) for i in range(2)]
    s_xb = [E(nc.sbuf_tensor(f"a_xb{i}", [128, 32 * 256], BF16)) for i in range(2)]
    s_st = [E(nc.sbuf_tensor(f"a_st{i}", [128, 2048], BF16)) for i in range(2)]
    s_bqT = E(nc.sbuf_tensor("a_bqT", [128, 16], F32))
    ps_mm = [E(nc.psum_tensor(f"a_ps{i}", [128, 2048], F32)) for i in range(2)]

    ld_b = ctx.sem("a_ldb")
    ld_w = [ctx.sem("a_ldw0"), ctx.sem("a_ldw1")]
    ld_x = ctx.sem("a_ldx")
    cv_w = ctx.sem("a_cvw")
    mm_d = ctx.sem("a_mmd")
    ev_d = ctx.sem("a_evd")
    st_o = [ctx.sem("a_sto0"), ctx.sem("a_sto1")]

    passes = [
        ("k", 1024, 8, io["i_xqT"], io["sc_kTo"], 8),
        ("v", 2048, 8, io["i_xqT"], None, None),
        ("q", 0, 8, io["i_xqT"], io["sc_qT"], 0),
    ]
    nsl_cum = [0, 8, 16, 24]
    ctx.a_sto = st_o

    def body_sync(sync):
        ld_b.inc(sync.dma_start(out=s_bqT[:, :], in_=io["i_bqT"]), 16)
        for pi, (pname, wofs, nsl, xsrc, outT, bcol) in enumerate(passes):
            for i in range(16):
                gsi = pi * 16 + i
                if gsi >= 2:
                    sync.wait_ge(cv_w.h, gsi - 1)
                ld_w[gsi % 2].inc(sync.dma_start(
                    out=s_wsl[gsi % 2][:, :].rearrange("p (a f) -> p a f", a=2),
                    in_=io["i_wqkv"].rearrange("(a p) f -> p a f", p=128)[
                        :, 2 * i:2 * i + 2, wofs:wofs + 1024]), 16)

    def body_scalar(sc):
        for pi, (pname, wofs, nsl, xsrc, outT, bcol) in enumerate(passes):
            if pi >= 1:
                sc.wait_ge(mm_d.h, nsl_cum[pi])
            for i in range(16):
                gsi = pi * 16 + i
                sc.wait_ge(ld_w[gsi % 2].h, (gsi // 2 + 1) * 16)
                cv_w.inc(sc.activation(out=s_w[:, i * 2048:(i + 1) * 2048],
                              in_=s_wsl[gsi % 2][:, :], func=AF.Copy), 1)
            def emit_evac(i):
                gxi = nsl_cum[pi] + i
                sc.wait_ge(mm_d.h, gxi + 1)
                if gxi >= 2:
                    sc.wait_ge(st_o[gxi % 2].h, (gxi // 2) * 16)
                if pname == "v":
                    ins = sc.activation(out=s_st[i % 2][:, :], in_=ps_mm[i % 2][:, :],
                                           func=AF.Copy)
                    ev_d.inc(ins, 1)
                else:
                    for dt in range(8):
                        ins = sc.activation(out=s_st[i % 2][:, dt * 256:(dt + 1) * 256],
                                      in_=ps_mm[i % 2][:, dt * 256:(dt + 1) * 256],
                                      func=AF.Identity,
                                      bias=s_bqT[:, bcol + dt: bcol + dt + 1])
                    ev_d.inc(ins, 1)
                sc.wait_ge(ev_d.h, ev_d.tot)
                c0 = i * 256
                if pname == "v":
                    st_o[gxi % 2].inc(sc.dma_start(
                        out=io["sc_vo"][c0:c0 + 256, :].rearrange("(a p) d -> p a d", p=128),
                        in_=s_st[i % 2][:, :].rearrange("p (a d) -> p a d", a=2)), 16)
                else:
                    st_o[gxi % 2].inc(sc.dma_start(
                        out=outT.rearrange("(a p) t -> p a t", p=128)[:, :, c0:c0 + 256],
                        in_=s_st[i % 2][:, :].rearrange("p (a t) -> p a t", a=8)), 16)

            for i in range(nsl):
                if i >= 1:
                    emit_evac(i - 1)
            emit_evac(nsl - 1)
        sc.wait_ge(st_o[0].h, st_o[0].tot)
        sc.wait_ge(st_o[1].h, st_o[1].tot)

    def body_tensor(t):
        t.wait_ge(ld_b.h, 16)
        for pi, (pname, wofs, nsl, xsrc, outT, bcol) in enumerate(passes):
            for i in range(nsl):
                gxi = nsl_cum[pi] + i
                t.wait_ge(cv_w.h, (pi + 1) * 16)
                t.wait_ge(ld_x.h, 16 * (gxi + 1))
                if gxi >= 2:
                    t.wait_ge(ev_d.h, gxi - 1)
                ps = ps_mm[i % 2]
                xb = s_xb[i % 2]
                if pname == "v":
                    for tsub in range(2):
                        for a in range(32):
                            for dc in range(2):
                                o0 = tsub * 1024 + dc * 512
                                _i = t.matmul(ps[:, o0:o0 + 512],
                                         xb[:, a * 256 + tsub * 128: a * 256 + tsub * 128 + 128],
                                         s_w[:, a * 1024 + dc * 512: a * 1024 + (dc + 1) * 512],
                                         start=(a == 0), stop=(a == 31))
                else:
                    for dt in range(8):
                        for a in range(32):
                            _i = t.matmul(ps[:, dt * 256:(dt + 1) * 256],
                                     s_w[:, a * 1024 + dt * 128: a * 1024 + dt * 128 + 128],
                                     xb[:, a * 256:(a + 1) * 256],
                                     start=(a == 0), stop=(a == 31))
                mm_d.inc(_i, 1)

    def mk_xload(gxi):
        pi, i = gxi // 8, gxi % 8
        xsrc = passes[pi][3]
        c0 = i * 256

        def f(g):
            if gxi >= 2:
                g.wait_ge(mm_d.h, gxi - 1)
            ld_x.inc(g.dma_start(
                out=s_xb[gxi % 2][:, :].rearrange("p (a t) -> p a t", t=256),
                in_=xsrc[:, c0:c0 + 256].rearrange("(a p) t -> p a t", p=128)), 16)
        return f

    xhooks = {}
    for gxi in range(24):
        xhooks.setdefault(min(2 * gxi, 84), []).append(mk_xload(gxi))

    return {"sync": body_sync, "scalar": body_scalar, "tensor": body_tensor,
            "xhooks": xhooks}





def emit_B1(ctx, io, bufs_es):
    nc = ctx.nc
    E = bufs_es.enter_context
    s_kT = E(nc.sbuf_tensor("b_kT", [128, 8 * 8192], BF16))
    s_qt = [E(nc.sbuf_tensor(f"b_qt{i}", [128, 8 * 128], BF16)) for i in range(2)]
    s_sc = E(nc.sbuf_tensor("b_sc", [128, 8192], F32))
    s_pb = [E(nc.sbuf_tensor(f"b_pb{i}", [128, 8192], BF16)) for i in range(2)]
    s_mx = E(nc.sbuf_tensor("b_mx", [128, 16], F32))
    s_nb = E(nc.sbuf_tensor("b_nb", [128, 16], F32))
    s_l = E(nc.sbuf_tensor("b_l", [128, 16], F32))
    ps_s = [E(nc.psum_tensor(f"b_pss{i}", [128, 2048], F32)) for i in range(2)]

    b_lk = ctx.sem("b_lk")
    b_lq = [ctx.sem("b_lq0"), ctx.sem("b_lq1")]
    b_mm = ctx.sem("b_mm")
    b_ev = ctx.sem("b_ev")
    b_mx = ctx.sem("b_mx")
    b_ex = ctx.sem("b_ex")
    b_po = [ctx.sem("b_po0"), ctx.sem("b_po1")]
    b_rl = ctx.sem("b_rl")

    def body_sync(sync):
        sync.wait_ge(ctx.cs_k.h, 1)
        for r in range(4):
            b_lk.inc(sync.dma_start(
                out=s_kT[:, :].rearrange("p (a t) -> p a t", t=8192)[:, :, r * 2048:(r + 1) * 2048],
                in_=io["sc_kag"].rearrange("(r a p) t -> r p a t", r=4, p=128)[r]), 16)
        for qt in range(16):
            if qt >= 2:
                sync.wait_ge(b_mm.h, 4 * qt - 4)
            b_lq[qt % 2].inc(sync.dma_start(
                out=s_qt[qt % 2][:, :],
                in_=io["sc_qT"].rearrange("(a p) t -> p a t", p=128)[:, :, qt * 128:(qt + 1) * 128]), 16)

    def body_tensor(t):
        t.wait_ge(b_lk.h, 64)
        t.wait_ge(ctx.id_sem.h, 1)
        for qt in range(16):
            t.wait_ge(b_lq[qt % 2].h, (qt // 2 + 1) * 16)
            for su in range(4):
                gi = 4 * qt + su
                if gi >= 2:
                    t.wait_ge(b_ev.h, gi - 1)
                for dt in range(8):
                    for kc in range(4):
                        _i = t.matmul(ps_s[gi % 2][:, kc * 512:(kc + 1) * 512],
                                 s_qt[qt % 2][:, dt * 128:(dt + 1) * 128],
                                 s_kT[:, dt * 8192 + su * 2048 + kc * 512:
                                      dt * 8192 + su * 2048 + (kc + 1) * 512],
                                 start=(dt == 0), stop=(dt == 7))
                b_mm.inc(_i, 1)

    def body_scalar(sc):
        for qt in range(16):
            sc.wait_ge(b_mx.h, qt + 1)
            if qt >= 2:
                sc.wait_ge(b_po[qt % 2].h, (qt // 2) * 16)
            b_ex.inc(sc.activation(out=s_pb[qt % 2][:, :], in_=s_sc[:, :], func=AF.Exp,
                                   bias=s_nb[:, qt:qt + 1], scale=SOFT_SCALE,
                                   accum_out=s_l[:, qt:qt + 1]), 1)

    def body_vector(v):
        for qt in range(16):
            if qt >= 1:
                v.wait_ge(b_ex.h, qt)
            for su in range(4):
                gi = 4 * qt + su
                v.wait_ge(b_mm.h, gi + 1)
                _v = v.reduce_max(s_mx[:, 4 * (qt % 2) + su:4 * (qt % 2) + su + 1],
                                  ps_s[gi % 2][:, :], axis=AX.X)
                if su == 3:
                    ctx.chain(v, _v)
                b_ev.inc(v.tensor_copy(out=s_sc[:, su * 2048:(su + 1) * 2048],
                                       in_=ps_s[gi % 2][:, :]), 1)
            _v = v.reduce_max(s_nb[:, qt:qt + 1],
                              s_mx[:, 4 * (qt % 2):4 * (qt % 2) + 4], axis=AX.X)
            ctx.chain(v, _v)
            b_mx.inc(v.tensor_scalar(out=s_nb[:, qt:qt + 1], in0=s_nb[:, qt:qt + 1],
                                     scalar1=-SOFT_SCALE, scalar2=None, op0=ALU.mult), 1)
            v.wait_ge(b_ex.h, qt + 1)
            b_rl.inc(v.reciprocal(io["rl"][:, qt:qt + 1], s_l[:, qt:qt + 1]), 1)

    def body_gpsimd(g):
        for qt in range(16):
            g.wait_ge(b_ex.h, qt + 1)
            b_po[qt % 2].inc(g.dma_start(
                out=io["sc_p"][qt * 128:(qt + 1) * 128, :], in_=s_pb[qt % 2][:, :]), 16)
        g.wait_ge(b_po[0].h, b_po[0].tot)
        g.wait_ge(b_po[1].h, b_po[1].tot)

    return {"sync": body_sync, "tensor": body_tensor, "scalar": body_scalar,
            "vector": body_vector, "gpsimd": body_gpsimd}




def emit_B2(ctx, io, bufs_es, s_ot):
    nc = ctx.nc
    E = bufs_es.enter_context
    s_v = E(nc.sbuf_tensor("c_v", [128, 64 * 1024], BF16))
    s_pl = [E(nc.sbuf_tensor(f"c_pl{i}", [128, 4096], BF16)) for i in range(2)]
    s_ptl = E(nc.sbuf_tensor("c_ptl", [128, 64 * 128], BF16))
    s_o = E(nc.sbuf_tensor("c_o", [128, 1024], BF16))
    ps_o = [E(nc.psum_tensor(f"c_pso{i}", [128, 1024], F32)) for i in range(2)]
    ps_t = [E(nc.psum_tensor(f"c_pst{i}", [128, 1024], BF16)) for i in range(2)]
    ps_ot = E(nc.psum_tensor("c_psot", [128, 1024], BF16))

    c_lv = ctx.sem("c_lv")
    c_lp = [ctx.sem("c_lp0"), ctx.sem("c_lp1")]
    c_tp = ctx.sem("c_tp")
    c_pe = ctx.sem("c_pe")
    c_mm = ctx.sem("c_mm")
    c_ev = ctx.sem("c_ev")
    c_tpo = ctx.sem("c_tpo")
    c_oe = ctx.sem("c_oe")

    def body_sync(sync):
        sync.wait_ge(ctx.cs_v.h, 1)
        c_lv.inc(sync.dma_start(out=s_v[:, :],
                                in_=io["sc_vag"].rearrange("(a p) d -> p a d", p=128)), 16)
        for hh in range(32):
            if hh >= 2:
                sync.wait_ge(c_tp.h, 4 * hh - 4)
            c_lp[hh % 2].inc(sync.dma_start(
                out=s_pl[hh % 2][:, :],
                in_=io["sc_p"][(hh // 2) * 128:(hh // 2 + 1) * 128,
                               (hh % 2) * 4096:(hh % 2 + 1) * 4096]), 16)

    def body_tensor(t):
        t.wait_ge(c_lv.h, 16)
        for qt in range(16):
            if qt >= 1:
                t.wait_ge(c_mm.h, qt)
            for grp in range(8):
                gg = 8 * qt + grp
                hh = 2 * qt + grp // 4
                t.wait_ge(c_lp[hh % 2].h, (hh // 2 + 1) * 16)
                if gg >= 2:
                    t.wait_ge(c_pe.h, gg - 1)
                for j in range(8):
                    _i = t.transpose(ps_t[gg % 2][:, j * 128:(j + 1) * 128],
                                s_pl[hh % 2][:, ((grp % 4) * 8 + j) * 128:
                                             ((grp % 4) * 8 + j + 1) * 128],
                                io["ident"][:])
                c_tp.inc(_i, 1)
            t.wait_ge(c_pe.h, 8 * qt + 8)
            if qt >= 2:
                t.wait_ge(c_ev.h, qt - 1)
            for kt in range(64):
                for dc in range(2):
                    _i = t.matmul(ps_o[qt % 2][:, dc * 512:(dc + 1) * 512],
                             s_ptl[:, kt * 128:(kt + 1) * 128],
                             s_v[:, kt * 1024 + dc * 512: kt * 1024 + (dc + 1) * 512],
                             start=(kt == 0), stop=(kt == 63))
            c_mm.inc(_i, 1)
            t.wait_ge(c_ev.h, qt + 1)
            t.wait_ge(c_oe.h, qt)
            for j in range(8):
                _i = t.transpose(ps_ot[:, j * 128:(j + 1) * 128],
                            s_o[:, j * 128:(j + 1) * 128], io["ident"][:])
            c_tpo.inc(_i, 1)

    def body_scalar(sc):
        for qt in range(16):
            for grp in range(8):
                gg = 8 * qt + grp
                sc.wait_ge(c_tp.h, gg + 1)
                c_pe.inc(sc.activation(out=s_ptl[:, grp * 1024:(grp + 1) * 1024],
                                       in_=ps_t[gg % 2][:, :], func=AF.Copy), 1)
            sc.wait_ge(c_mm.h, qt + 1)
            if qt >= 1:
                sc.wait_ge(c_tpo.h, qt)
            c_ev.inc(sc.activation(out=s_o[:, :], in_=ps_o[qt % 2][:, :],
                                   func=AF.Identity, bias=0.0,
                                   scale=io["rl"][:, qt:qt + 1]), 1)
            sc.wait_ge(c_tpo.h, qt + 1)
            c_oe.inc(sc.activation(
                out=s_ot[:, :].rearrange("p (a q) -> p a q", q=2048)[:, :, qt * 128:(qt + 1) * 128],
                in_=ps_ot[:, :], func=AF.Copy), 1)

    return {"sync": body_sync, "tensor": body_tensor, "scalar": body_scalar}




def emit_B3(ctx, io, bufs_es, s_ot):
    nc = ctx.nc
    E = bufs_es.enter_context
    s_wo = E(nc.sbuf_tensor("d_wo", [128, 8 * 4096], BF16))
    s_wsl = [E(nc.sbuf_tensor(f"d_wsl{i}", [128, 1024], F32)) for i in range(2)]
    s_bo128 = E(nc.sbuf_tensor("d_bo128", [128, 4096], F32))
    s_bvT = E(nc.sbuf_tensor("d_bvT", [128, 8], BF16))
    s_ones = E(nc.sbuf_tensor("d_ones", [1, 128], BF16))
    s_xo = [E(nc.sbuf_tensor(f"d_xo{i}", [128, 4096], F32)) for i in range(2)]
    s_pr = E(nc.sbuf_tensor("d_pr", [128, 4096], F32))
    s_xe = E(nc.sbuf_tensor("d_xe", [128, 2048], BF16))
    s_xoo = E(nc.sbuf_tensor("d_xoo", [128, 2048], BF16))
    s_xms = E(nc.sbuf_tensor("d_xms", [128, 32 * 128], BF16))
    s_bor = s_pr[0:1, 0:4096]
    s_bob = s_xms[0:1, 0:4096]
    s_bx = s_xo
    s_xm = s_pr
    s_bns = E(nc.sbuf_tensor("d_bns", [128, 48], F32))
    s_mv = E(nc.sbuf_tensor("d_mv", [128, 2], F32))
    s_sd = E(nc.sbuf_tensor("d_sd", [128, 1], F32))
    s_rs = E(nc.sbuf_tensor("d_rs", [128, 1], F32))
    s_nmr = E(nc.sbuf_tensor("d_nmr", [128, 1], F32))
    ps_p = E(nc.psum_tensor("d_psp", [128, 2048], F32))
    ps_xt = [E(nc.psum_tensor(f"d_psxt{i}", [128, 512], BF16)) for i in range(2)]

    d_lb = ctx.sem("d_lb")
    d_lv = ctx.sem("d_lv")
    d_lw = [ctx.sem("d_lw0"), ctx.sem("d_lw1")]
    d_cw = ctx.sem("d_cw")
    d_on = ctx.sem("d_on")
    d_bm = ctx.sem("d_bm")
    d_bv = ctx.sem("d_bv")
    d_bc = ctx.sem("d_bc")
    d_bb = ctx.sem("d_bb")
    d_be = ctx.sem("d_be")
    d_lx = [ctx.sem("d_lx0"), ctx.sem("d_lx1")]
    d_bx = ctx.sem("d_bx")
    d_mm = ctx.sem("d_mm")
    d_pa = ctx.sem("d_pa")
    d_st = ctx.sem("d_st")
    d_sq = ctx.sem("d_sq")
    d_ln = ctx.sem("d_ln")
    d_xw = ctx.sem("d_xw")
    d_sp = ctx.sem("d_sp")
    d_tp = ctx.sem("d_tp")
    d_se = ctx.sem("d_se")
    d_so = ctx.sem("d_so")

    def body_sync(sync):
        d_lb.inc(sync.dma_start(out=s_bor[:, :], in_=io["i_bo"][None, :]), 16)
        for i in range(32):
            if i >= 2:
                sync.wait_ge(d_cw.h, i - 1)
            d_lw[i % 2].inc(sync.dma_start(
                out=s_wsl[i % 2][:, :],
                in_=io["i_wo"].rearrange("(a p) f -> a p f", p=128)[
                    i // 4, :, (i % 4) * 1024:(i % 4 + 1) * 1024]), 16)
        for tt in range(16):
            if tt >= 2:
                sync.wait_ge(d_bx.h, tt - 1)
            d_lx[tt % 2].inc(sync.dma_start(
                out=s_xo[tt % 2][:, :],
                in_=io["i_xo"][tt * 128:(tt + 1) * 128, :]), 16)
            sync.wait_ge(d_ln.h, tt + 1)
            d_xw.inc(sync.dma_start(
                out=io["sc_xmid"][tt * 128:(tt + 1) * 128, :], in_=s_pr[:, :]), 16)
        sync.wait_ge(d_xw.h, d_xw.tot)

    def body_scalar(sc):
        for i in range(32):
            sc.wait_ge(d_lw[i % 2].h, (i // 2 + 1) * 16)
            d_cw.inc(sc.activation(out=s_wo[:, i * 1024:(i + 1) * 1024],
                                   in_=s_wsl[i % 2][:, :], func=AF.Copy), 1)
        sc.wait_ge(d_bv.h, 2)
        d_bc.inc(sc.activation(out=s_bob[:, :], in_=s_bo128[0:1, :], func=AF.Copy), 1)
        for r in range(2):
            sc.wait_ge(d_bb.h, r + 1)
            d_be.inc(sc.activation(out=s_bo128[:, r * 2048:(r + 1) * 2048],
                                   in_=ps_p[:, :], func=AF.Copy), 1)
        for tt in range(16):
            sc.wait_ge(d_st.h, tt + 1)
            d_sq.inc(sc.activation(out=s_sd[:, :], in_=s_mv[:, 1:2],
                                   func=AF.Sqrt), 1)
            for grp in range(8):
                gg = 8 * tt + grp
                sc.wait_ge(d_tp.h, gg + 1)
                if grp == 0:
                    sc.wait_ge(d_so.h, 16 * tt)
                d_se.inc(sc.activation(out=s_xms[:, grp * 512:(grp + 1) * 512],
                                       in_=ps_xt[grp % 2][:, :], func=AF.Copy), 1)

    def body_tensor(t):
        t.wait_ge(d_cw.h, 32)
        t.wait_ge(d_lb.h, 16)
        t.wait_ge(d_lv.h, 16)
        for half in range(2):
            if half == 1:
                t.wait_ge(d_bv.h, 1)
            for cc in range(4):
                for dt in range(8):
                    _i = t.matmul(ps_p[0:1, cc * 512:(cc + 1) * 512],
                             s_bvT[:, dt:dt + 1],
                             s_wo[:, dt * 4096 + half * 2048 + cc * 512:
                                  dt * 4096 + half * 2048 + (cc + 1) * 512],
                             start=(dt == 0), stop=(dt == 7))
            d_bm.inc(_i, 1)
        t.wait_ge(d_bc.h, 1)
        t.wait_ge(d_on.h, 1)
        t.wait_ge(d_bv.h, 2)
        for r in range(2):
            if r == 1:
                t.wait_ge(d_be.h, 1)
            for cc in range(4):
                _i = t.matmul(ps_p[:, cc * 512:(cc + 1) * 512], s_ones[0:1, :],
                         s_bob[0:1, r * 2048 + cc * 512: r * 2048 + (cc + 1) * 512],
                         start=True, stop=True)
            d_bb.inc(_i, 1)
        for tt in range(16):
            for half in range(2):
                gi = 2 * tt + half
                t.wait_ge(d_be.h, 2) if gi == 0 else None
                if gi >= 1:
                    t.wait_ge(d_pa.h, gi)
                for dt in range(8):
                    for cc in range(4):
                        _i = t.matmul(ps_p[:, cc * 512:(cc + 1) * 512],
                                 s_ot[:, dt * 2048 + tt * 128: dt * 2048 + (tt + 1) * 128],
                                 s_wo[:, dt * 4096 + half * 2048 + cc * 512:
                                      dt * 4096 + half * 2048 + (cc + 1) * 512],
                                 start=(dt == 0), stop=(dt == 7))
                d_mm.inc(_i, 1)
            t.wait_ge(d_sp.h, tt + 1)
            for half in range(2):
                src = s_xe if half == 0 else s_xoo
                for grp in range(4):
                    gg = 8 * tt + half * 4 + grp
                    if gg >= 2:
                        t.wait_ge(d_se.h, gg - 1)
                    for j in range(4):
                        _i = t.transpose(ps_xt[gg % 2][:, j * 128:(j + 1) * 128],
                                    src[:, (grp * 4 + j) * 128:(grp * 4 + j + 1) * 128],
                                    io["ident"][:])
                    d_tp.inc(_i, 1)

    def body_vector(v):
        for half in range(2):
            v.wait_ge(d_bm.h, half + 1)
            v.wait_ge(d_lb.h, 16)
            d_bv.inc(v.tensor_tensor(out=s_bo128[0:1, half * 2048:(half + 1) * 2048],
                                     in0=ps_p[0:1, 0:2048],
                                     in1=s_bor[0:1, half * 2048:(half + 1) * 2048],
                                     op=ALU.add), 1)
        for tt in range(16):
            v.wait_ge(d_lx[tt % 2].h, (tt // 2 + 1) * 16)
            v.wait_ge(d_be.h, 2)
            if tt >= 1:
                v.wait_ge(d_pa.h, 2 * tt)
            d_bx.inc(v.tensor_tensor(out=s_xo[tt % 2][:, :], in0=s_bo128[:, :],
                                     in1=s_xo[tt % 2][:, :], op=ALU.add), 1)
            v.wait_ge(d_bx.h, d_bx.tot)
            for half in range(2):
                gi = 2 * tt + half
                v.wait_ge(d_mm.h, gi + 1)
                if half == 0 and tt >= 1:
                    v.wait_ge(d_xw.h, 16 * tt)
                d_pa.inc(v.tensor_tensor(out=s_pr[:, half * 2048:(half + 1) * 2048],
                                         in0=ps_p[:, :],
                                         in1=s_xo[tt % 2][:, half * 2048:(half + 1) * 2048],
                                         op=ALU.add), 1)
            v.wait_ge(d_pa.h, d_pa.tot)
            for ch in range(8):
                _v = v.bn_stats(s_bns[:, ch * 6:(ch + 1) * 6], s_pr[:, ch * 512:(ch + 1) * 512])
            ctx.chain(v, _v)
            _v = v.bn_aggr(s_mv[:, :], s_bns[:, :])
            ctx.chain(v, _v)
            d_st.inc(v.tensor_scalar(out=s_mv[:, 1:2], in0=s_mv[:, 1:2], scalar1=EPS,
                                     scalar2=None, op0=ALU.add), 1)
            v.wait_ge(d_sq.h, tt + 1)
            _v = v.reciprocal(s_rs[:, :], s_sd[:, :])
            ctx.chain(v, _v)
            _v = v.tensor_scalar(out=s_nmr[:, :], in0=s_mv[:, 0:1], scalar1=s_rs[:, 0:1],
                            scalar2=-1.0, op0=ALU.mult, op1=ALU.mult)
            ctx.chain(v, _v)
            d_ln.inc(v.tensor_scalar(out=s_pr[:, :], in0=s_pr[:, :], scalar1=s_rs[:, 0:1],
                                     scalar2=s_nmr[:, 0:1], op0=ALU.mult, op1=ALU.add), 1)
            v.wait_ge(d_ln.h, d_ln.tot)
            if tt >= 1:
                v.wait_ge(d_tp.h, 8 * tt)
            v.tensor_copy(out=s_xe[:, :], in_=s_pr[:, 0:4096:2])
            d_sp.inc(v.tensor_copy(out=s_xoo[:, :], in_=s_pr[:, 1:4096:2]), 1)

    def body_gpsimd(g):
        d_lv.inc(g.dma_start(out=s_bvT[:, :], in_=io["i_bvT"]), 16)
        d_on.inc(g.memset(s_ones[:, :], 1.0), 1)
        for tt in range(16):
            g.wait_ge(d_se.h, 8 * tt + 8)
            d_so.inc(g.dma_start(
                out=io["sc_xmT"][:, :, tt * 128:(tt + 1) * 128].rearrange("a p t -> p a t"),
                in_=s_xms[:, :].rearrange("p (a t) -> p a t", t=128)), 16)
        g.wait_ge(d_so.h, d_so.tot)

    return {"sync": body_sync, "scalar": body_scalar, "tensor": body_tensor,
            "vector": body_vector, "gpsimd": body_gpsimd}




def emit_C(ctx, io, bufs_es):
    nc = ctx.nc
    E = bufs_es.enter_context
    NB = 4
    s_xm = E(nc.sbuf_tensor("e_xm", [128, 32 * 1024], BF16))
    s_wt = [[E(nc.sbuf_tensor(f"e_w{w}{i}", [128, 32 * 128], BF16)) for i in range(2)]
            for w in range(2)]
    s_pi = [E(nc.sbuf_tensor(f"e_pi{i}", [128, 2048], I16)) for i in range(NB)]
    s_th = E(nc.sbuf_tensor("e_th", [128, 2048], I16))
    s_tl = E(nc.sbuf_tensor("e_tl", [128, 2048], I16))
    s_sb = [E(nc.sbuf_tensor(f"e_sb{i}", [128, 2048], BF16)) for i in range(NB)]
    s_zb = [E(nc.sbuf_tensor(f"e_zb{i}", [128, 2048], BF16)) for i in range(NB)]
    s_sg = E(nc.sbuf_tensor("e_sg", [128, 1024], F32))
    s_ht = [E(nc.sbuf_tensor(f"e_ht{i}", [128, 1024], BF16)) for i in range(2)]
    ps_g = [E(nc.psum_tensor(f"e_psg{i}", [128, 1024], F32)) for i in range(2)]
    ps_u = [E(nc.psum_tensor(f"e_psu{i}", [128, 1024], F32)) for i in range(2)]

    e_lx = ctx.sem("e_lx")
    e_ld = [ctx.sem(f"e_ld{i}") for i in range(NB)]
    e_dq = ctx.sem("e_dq")
    e_mm = ctx.sem("e_mm")
    e_sl = ctx.sem("e_sl")
    e_hv = ctx.sem("e_hv")
    e_ho = ctx.sem("e_ho")

    MATIN = [(io["i_pg"], io["i_sbg"], io["i_zbg"]),
             (io["i_pu"], io["i_sbu"], io["i_zbu"])]

    def body_gpsimd2(g):
        for dqg in range(344):
            it, mat = (dqg % 172) // 2, dqg % 2
            b = dqg % NB
            if dqg >= NB:
                g.wait_ge(e_dq.h, dqg - (NB - 1))
            pk, sbv, zbv = MATIN[mat]
            e_ld[b].inc(g.dma_start(
                out=s_pi[b][:, :],
                in_=pk.rearrange("(a p) n -> p a n", p=128)[:, :, it * 128:(it + 1) * 128]), 16)
            for src, dst in ((sbv, s_sb[b]), (zbv, s_zb[b])):
                for g0 in range(4):
                    e_ld[b].inc(g.dma_start(
                        out=dst[g0 * 32:(g0 + 1) * 32, :],
                        in_=src.rearrange("(j g) n -> g j n", g=4)[g0, :, it * 128:(it + 1) * 128]
                            .unsqueeze(0).broadcast_to([32, 16, 128])), 16)

    def body_sync(sync):
        for pss in range(2):
            if pss == 1:
                sync.wait_ge(e_mm.h, 172)
            e_lx.inc(sync.dma_start(
                out=s_xm[:, :],
                in_=io["sc_xmT"][:, :, pss * 1024:(pss + 1) * 1024]
                    .rearrange("a p t -> p a t")), 16)
            for it in range(86):
                git = pss * 86 + it
                sync.wait_ge(e_hv.h, git + 1)
                e_ho.inc(sync.dma_start(
                    out=io["sc_hT"][it][:, pss * 1024:(pss + 1) * 1024],
                    in_=s_ht[git % 2][:, :]), 16)
        sync.wait_ge(e_ho.h, e_ho.tot)

    def body_vector(v):
        for pss in range(2):
            for it in range(86):
                git = pss * 86 + it
                for mat in range(2):
                    dqg = 2 * git + mat
                    b = dqg % NB
                    wb = it % 2
                    v.wait_ge(e_ld[b].h, (dqg // NB + 1) * 144)
                    if dqg >= 4:
                        v.wait_ge(e_mm.h, dqg - 3)
                    hi = s_wt[mat][wb][:, 0:2048]
                    lo = s_wt[mat][wb][:, 2048:4096]
                    v.tensor_scalar(out=s_th[:, :], in0=s_pi[b][:, :], scalar1=4,
                                    scalar2=None, op0=ALU.logical_shift_right)
                    _v = v.tensor_scalar(out=s_tl[:, :], in0=s_pi[b][:, :], scalar1=15,
                                         scalar2=None, op0=ALU.bitwise_and)
                    ctx.chain(v, _v)
                    v.tensor_tensor(out=hi, in0=s_th[:, :], in1=s_zb[b][:, :],
                                    op=ALU.subtract)
                    _v = v.tensor_tensor(out=lo, in0=s_tl[:, :], in1=s_zb[b][:, :],
                                         op=ALU.subtract)
                    ctx.chain(v, _v)
                    v.tensor_tensor(out=hi, in0=hi, in1=s_sb[b][:, :], op=ALU.mult)
                    e_dq.inc(v.tensor_tensor(out=lo, in0=lo, in1=s_sb[b][:, :],
                                             op=ALU.mult), 1)
                if git >= 1:
                    emit_hmult(v, git - 1)
            if pss == 1:
                emit_hmult(v, 171)

    def emit_hmult(v, git):
        v.wait_ge(e_mm.h, 2 * git + 2)
        v.wait_ge(e_sl.h, git + 1)
        if git >= 2:
            v.wait_ge(e_ho.h, 16 * (git - 1))
        e_hv.inc(v.tensor_tensor(out=s_ht[git % 2][:, :], in0=s_sg[:, :],
                                 in1=ps_u[git % 2][:, :], op=ALU.mult), 1)

    def body_tensor(t):
        for pss in range(2):
            t.wait_ge(e_lx.h, 16 * (pss + 1))
            for it in range(86):
                git = pss * 86 + it
                wb = it % 2
                for mat in range(2):
                    dqg = 2 * git + mat
                    t.wait_ge(e_dq.h, dqg + 1)
                    if git >= 2:
                        t.wait_ge((e_sl if mat == 0 else e_hv).h, git - 1)
                    ps = (ps_g if mat == 0 else ps_u)[wb]
                    for rt in range(32):
                        for tc in range(2):
                            _i = t.matmul(ps[:, tc * 512:(tc + 1) * 512],
                                     s_wt[mat][wb][:, rt * 128:(rt + 1) * 128],
                                     s_xm[:, rt * 1024 + tc * 512:
                                          rt * 1024 + (tc + 1) * 512],
                                     start=(rt == 0), stop=(rt == 31))
                    e_mm.inc(_i, 1)

    def body_scalar(sc):
        for git in range(172):
            sc.wait_ge(e_mm.h, 2 * git + 1)
            if git >= 1:
                sc.wait_ge(e_hv.h, git)
            e_sl.inc(sc.activation(out=s_sg[:, :], in_=ps_g[git % 2][:, :],
                                   func=AF.Silu), 1)

    return {"sync": body_sync, "tensor": body_tensor, "scalar": body_scalar,
            "vector": body_vector, "gpsimd": body_gpsimd2}




def emit_D(ctx, io, bufs_es):
    nc = ctx.nc
    E = bufs_es.enter_context
    s_hq = E(nc.sbuf_tensor("f_hq", [128, 86 * 512], BF16))
    s_wdh = [E(nc.sbuf_tensor(f"f_wd{i}", [128, 43 * 128], BF16)) for i in range(4)]
    s_tmp = [E(nc.sbuf_tensor(f"f_tmp{i}", [128, 512], BF16)) for i in range(2)]
    s_ff = E(nc.sbuf_tensor("f_ff", [128, 4 * 4096], BF16))
    s_xo2 = [E(nc.sbuf_tensor("f_xo0", [128, 4096], F32))] * 2
    s_row = E(nc.sbuf_tensor("f_row", [128, 4096], F32))
    s_bns = E(nc.sbuf_tensor("f_bns", [128, 48], F32))
    s_mv = E(nc.sbuf_tensor("f_mv", [128, 2], F32))
    s_sd = E(nc.sbuf_tensor("f_sd", [128, 1], F32))
    s_rs = E(nc.sbuf_tensor("f_rs", [128, 1], F32))
    s_nmr = E(nc.sbuf_tensor("f_nmr", [128, 1], F32))
    ps_d = [E(nc.psum_tensor(f"f_psd{i}", [128, 512], F32)) for i in range(2)]
    ps_t2 = [E(nc.psum_tensor(f"f_pst{i}", [128, 512], BF16)) for i in range(2)]

    f_lh = ctx.sem("f_lh")
    f_lw = [ctx.sem(f"f_lw{i}") for i in range(4)]
    f_mm = ctx.sem("f_mm")
    f_ev = ctx.sem("f_ev")
    f_tp = ctx.sem("f_tp")
    f_fe = ctx.sem("f_fe")
    f_lx = [ctx.sem("f_lx0"), ctx.sem("f_lx1")]
    f_ad = ctx.sem("f_ad")
    f_st = ctx.sem("f_st")
    f_sq = ctx.sem("f_sq")
    f_ln = ctx.sem("f_ln")
    f_oo = ctx.sem("f_oo")

    def emit_wload(sync, hgi):
        ht, half = (hgi % 64) // 2, hgi % 2
        if hgi >= 4:
            sync.wait_ge(f_mm.h, hgi - 3)
        f_lw[hgi % 4].inc(sync.dma_start(
            out=s_wdh[hgi % 4][:, :],
            in_=io["sc_wd"][ht][:, half * 5504:(half + 1) * 5504]), 16)

    def body_sync(sync):
        for tb in range(4):
            if tb == 0:
                for hgi in range(3):
                    emit_wload(sync, hgi)
            if tb >= 1:
                sync.wait_ge(f_mm.h, 64 * tb)
            for hf in range(2):
                for aq in range(2):
                    r0, r1 = (0, 21) if aq == 0 else (21, 43)
                    f_lh.inc(sync.dma_start(
                        out=s_hq[:, (hf * 43 + r0) * 512:(hf * 43 + r1) * 512]
                            .rearrange("p (a t) -> p a t", t=512),
                        in_=io["sc_hT"][hf * 43 + r0:hf * 43 + r1, :, tb * 512:(tb + 1) * 512]
                            .rearrange("a p t -> p a t")), 16)
            for hgi in range(64 * tb + 3, 64 * (tb + 1) + (3 if tb < 3 else 0)):
                emit_wload(sync, hgi)

    def body_gpsimd(g):
        for tb in range(4):
            for tt in range(4):
                gti = 4 * tb + tt
                if gti >= 1:
                    g.wait_ge(f_ad.h, gti)
                f_lx[gti % 2].inc(g.dma_start(
                    out=s_xo2[gti % 2][:, :],
                    in_=io["sc_xmid"][gti * 128:(gti + 1) * 128, :]), 16)
                g.wait_ge(f_ln.h, gti + 1)
                f_oo.inc(g.dma_start(
                    out=io["out"][gti * 128:(gti + 1) * 128, :], in_=s_row[:, :]), 16)
        g.wait_ge(f_oo.h, f_oo.tot)

    def emit_tr(t, ghi):
        t.wait_ge(f_ev.h, ghi + 1)
        if ghi >= 2:
            t.wait_ge(f_fe.h, ghi - 1)
        for j in range(4):
            _i = t.transpose(ps_t2[ghi % 2][:, j * 128:(j + 1) * 128],
                        s_tmp[ghi % 2][:, j * 128:(j + 1) * 128], io["ident"][:])
        f_tp.inc(_i, 1)

    def body_tensor(t):
        for tb in range(4):
            t.wait_ge(f_lh.h, 64 * (tb + 1))
            for ht in range(32):
                ghi = 32 * tb + ht
                for half in range(2):
                    hgi = 2 * ghi + half
                    t.wait_ge(f_lw[hgi % 4].h, (hgi // 4 + 1) * 16)
                    if half == 0 and ghi >= 2:
                        t.wait_ge(f_ev.h, ghi - 1)
                    for rt in range(43):
                        _i = t.matmul(ps_d[ghi % 2][:, :],
                                 s_wdh[hgi % 4][:, rt * 128:(rt + 1) * 128],
                                 s_hq[:, (half * 43 + rt) * 512:(half * 43 + rt + 1) * 512],
                                 start=(half == 0 and rt == 0), stop=(half == 1 and rt == 42))
                    f_mm.inc(_i, 1)
                if ghi >= 1:
                    emit_tr(t, ghi - 1)
        emit_tr(t, 127)

    def body_scalar(sc):
        for tb in range(4):
            for ht in range(32):
                ghi = 32 * tb + ht
                sc.wait_ge(f_mm.h, 2 * ghi + 2)
                if ghi >= 2:
                    sc.wait_ge(f_tp.h, ghi - 1)
                f_ev.inc(sc.activation(out=s_tmp[ghi % 2][:, :], in_=ps_d[ghi % 2][:, :],
                                       func=AF.Copy), 1)
                sc.wait_ge(f_tp.h, ghi + 1)
                if ht == 0:
                    sc.wait_ge(f_ad.h, 4 * tb)
                f_fe.inc(sc.activation(
                    out=s_ff[:, :].rearrange("p (a c) -> p a c", c=4096)[:, :, ht * 128:(ht + 1) * 128],
                    in_=ps_t2[ghi % 2][:, :], func=AF.Copy), 1)
            for tt in range(4):
                gti = 4 * tb + tt
                sc.wait_ge(f_st.h, gti + 1)
                f_sq.inc(sc.activation(out=s_sd[:, :], in_=s_mv[:, 1:2],
                                       func=AF.Sqrt), 1)

    def body_vector(v):
        for tb in range(4):
            for tt in range(4):
                gti = 4 * tb + tt
                v.wait_ge(f_fe.h, 32 * (tb + 1))
                v.wait_ge(f_lx[gti % 2].h, (gti // 2 + 1) * 16)
                if gti >= 1:
                    v.wait_ge(f_oo.h, 16 * gti)
                f_ad.inc(v.tensor_tensor(out=s_row[:, :],
                                         in0=s_ff[:, tt * 4096:(tt + 1) * 4096],
                                         in1=s_xo2[gti % 2][:, :], op=ALU.add), 1)
                v.wait_ge(f_ad.h, f_ad.tot)
                for ch in range(8):
                    _v = v.bn_stats(s_bns[:, ch * 6:(ch + 1) * 6],
                               s_row[:, ch * 512:(ch + 1) * 512])
                ctx.chain(v, _v)
                _v = v.bn_aggr(s_mv[:, :], s_bns[:, :])
                ctx.chain(v, _v)
                f_st.inc(v.tensor_scalar(out=s_mv[:, 1:2], in0=s_mv[:, 1:2], scalar1=EPS,
                                         scalar2=None, op0=ALU.add), 1)
                v.wait_ge(f_sq.h, gti + 1)
                _v = v.reciprocal(s_rs[:, :], s_sd[:, :])
                ctx.chain(v, _v)
                _v = v.tensor_scalar(out=s_nmr[:, :], in0=s_mv[:, 0:1], scalar1=s_rs[:, 0:1],
                                scalar2=-1.0, op0=ALU.mult, op1=ALU.mult)
                ctx.chain(v, _v)
                f_ln.inc(v.tensor_scalar(out=s_row[:, :], in0=s_row[:, :],
                                         scalar1=s_rs[:, 0:1], scalar2=s_nmr[:, 0:1],
                                         op0=ALU.mult, op1=ALU.add), 1)

    return {"sync": body_sync, "tensor": body_tensor, "scalar": body_scalar,
            "vector": body_vector, "gpsimd": body_gpsimd}



PHASES_ORDER = ["A", "B1", "B2", "B3", "C", "D"]


def build_nc(upto="D", dev_outputs=()):
    nc = bass.Bass("TRN2", target_bir_lowering=False)

    def scratch(name, shape, dtype):
        kind = "ExternalOutput" if name in dev_outputs else "Internal"
        return nc.dram_tensor(name, shape, dtype, kind=kind).ap()

    io = {}
    io["i_xqT"] = nc.dram_tensor("i_xqT", [H, TOWN], F32, kind="ExternalInput").ap()
    io["i_xo"] = nc.dram_tensor("i_xo", [TOWN, H], F32, kind="ExternalInput").ap()
    io["i_wqkv"] = nc.dram_tensor("i_wqkv", [H, 3 * D], F32, kind="ExternalInput").ap()
    io["i_bqT"] = nc.dram_tensor("i_bqT", [128, 16], F32, kind="ExternalInput").ap()
    io["i_bo"] = nc.dram_tensor("i_bo", [H], F32, kind="ExternalInput").ap()
    io["i_bvT"] = nc.dram_tensor("i_bvT", [128, 8], F32, kind="ExternalInput").ap()
    io["i_wo"] = nc.dram_tensor("i_wo", [D, H], F32, kind="ExternalInput").ap()
    io["i_pg"] = nc.dram_tensor("i_pg", [2048, NI], U8, kind="ExternalInput").ap()
    io["i_pu"] = nc.dram_tensor("i_pu", [2048, NI], U8, kind="ExternalInput").ap()
    io["i_pd"] = nc.dram_tensor("i_pd", [NI // 2, H], U8, kind="ExternalInput").ap()
    io["i_sbg"] = nc.dram_tensor("i_sbg", [64, NI], F32, kind="ExternalInput").ap()
    io["i_zbg"] = nc.dram_tensor("i_zbg", [64, NI], F32, kind="ExternalInput").ap()
    io["i_sbu"] = nc.dram_tensor("i_sbu", [64, NI], F32, kind="ExternalInput").ap()
    io["i_zbu"] = nc.dram_tensor("i_zbu", [64, NI], F32, kind="ExternalInput").ap()
    io["i_sbd"] = nc.dram_tensor("i_sbd", [172, H], F32, kind="ExternalInput").ap()
    io["i_zbd"] = nc.dram_tensor("i_zbd", [172, H], F32, kind="ExternalInput").ap()

    io["sc_qT"] = scratch("sc_qT", [D, TOWN], BF16)
    io["sc_kTo"] = scratch("sc_kTo", [D, TOWN], BF16)
    io["sc_kag"] = scratch("sc_kag", [4 * D, TOWN], BF16)
    io["sc_vo"] = scratch("sc_vo", [TOWN, D], BF16)
    io["sc_vag"] = scratch("sc_vag", [S, D], BF16)
    io["sc_p"] = scratch("sc_p", [TOWN, S], BF16)
    io["sc_xmid"] = scratch("sc_xmid", [TOWN, H], F32)
    io["sc_xmT"] = scratch("sc_xmT", [32, 128, TOWN], BF16)
    io["sc_wd"] = scratch("sc_wd", [32, 128, 86 * 128], BF16)
    io["sc_hT"] = scratch("sc_hT", [86, 128, TOWN], BF16)
    io["out"] = nc.dram_tensor("out", [TOWN, H], F32, kind="ExternalOutput").ap()

    with contextlib.ExitStack() as ges:
        ctx = Ctx(nc, ges)
        io["ident"] = ges.enter_context(nc.sbuf_tensor("g_ident", [128, 128], BF16))
        io["rl"] = ges.enter_context(nc.sbuf_tensor("g_rl", [128, 16], F32))
        ctx.id_sem = ctx.sem("g_id")
        ctx.cs_k = ctx.sem("cs_k")
        ctx.cs_v = ctx.sem("cs_v")

        def mk_ag(src_key, dst_key, sto_need, cs):
            def hook(g):
                g.wait_ge(ctx.a_sto[0].h, sto_need)
                g.wait_ge(ctx.a_sto[1].h, sto_need)
                cs.inc(g.collective_compute(
                    "AllGather", ALU.bypass, replica_groups=RG,
                    ins=[io[src_key].opt()], outs=[io[dst_key].opt()]), 1)
            return hook

        with contextlib.ExitStack() as ph:
            em_a = emit_A(ctx, io, ph)
            hooks = em_a["xhooks"]
            hooks.setdefault(24, []).append(mk_ag("sc_kTo", "sc_kag", 64, ctx.cs_k))
            hooks.setdefault(44, []).append(mk_ag("sc_vo", "sc_vag", 128, ctx.cs_v))
            em_p = emit_P0(ctx, io, ph, hooks)

            def gp(g):
                _g = g.memset(io["ident"][:], 0.0)
                ctx.chain(g, _g)
                _g = g.affine_select(out=io["ident"][:], in_=io["ident"][:],
                                compare_op=ALU.not_equal, fill=1.0, base=0,
                                pattern=[[-1, 128]], channel_multiplier=1)
                ctx.id_sem.inc(_g, 1)
                em_p["gpsimd"](g)

            run_phase(ctx, "ph_A", {
                "sync": em_a["sync"], "scalar": em_a["scalar"], "tensor": em_a["tensor"],
                "gpsimd": gp, "vector": em_p["vector"]})

        if upto == "A":
            return nc, io

        with contextlib.ExitStack() as ph:
            em = emit_B1(ctx, io, ph)
            run_phase(ctx, "ph_B1", em)
        if upto == "B1":
            return nc, io
        with contextlib.ExitStack() as oter:
            s_ot = oter.enter_context(nc.sbuf_tensor("g_ot", [128, 8 * 2048], BF16))
            with contextlib.ExitStack() as ph:
                em = emit_B2(ctx, io, ph, s_ot)
                run_phase(ctx, "ph_B2", em)
            if upto != "B2":
                with contextlib.ExitStack() as ph:
                    em = emit_B3(ctx, io, ph, s_ot)
                    run_phase(ctx, "ph_B3", em)
        if upto in ("B2", "B3"):
            return nc, io
        with contextlib.ExitStack() as ph:
            em = emit_C(ctx, io, ph)
            run_phase(ctx, "ph_C", em)
        if upto == "C":
            return nc, io
        with contextlib.ExitStack() as ph:
            em = emit_D(ctx, io, ph)
            run_phase(ctx, "ph_D", em)
    return nc, io


def _host_inputs(inputs):
    x = np.ascontiguousarray(np.asarray(inputs["L0_attn_x"], dtype=np.float32))
    wqkv = np.ascontiguousarray(np.asarray(inputs["L0_attn_Wqkv"], dtype=np.float32))
    bqkv = np.ascontiguousarray(np.asarray(inputs["L0_attn_bqkv"], dtype=np.float32))
    wo = np.ascontiguousarray(np.asarray(inputs["L0_attn_Wo"], dtype=np.float32))
    bo = np.ascontiguousarray(np.asarray(inputs["L0_attn_bo"], dtype=np.float32))

    perm = np.concatenate([np.arange(0, NI, 2), np.arange(1, NI, 2)])

    def pack_bytes(wq_i32):
        return np.asarray(wq_i32, dtype=np.int32).view(np.uint8)[:, 0::4]

    pg = np.ascontiguousarray(pack_bytes(inputs["L0_ffn_gate_Wq"])[perm].T)
    pu = np.ascontiguousarray(pack_bytes(inputs["L0_ffn_up_Wq"])[perm].T)
    pd = np.ascontiguousarray(pack_bytes(inputs["L0_ffn_down_Wq"]).T)

    def tp(a):
        return np.ascontiguousarray(np.asarray(a, dtype=np.float32).T)

    sbg = tp(np.asarray(inputs["L0_ffn_gate_scale"])[perm])
    zbg = tp(np.asarray(inputs["L0_ffn_gate_zero"])[perm])
    sbu = tp(np.asarray(inputs["L0_ffn_up_scale"])[perm])
    zbu = tp(np.asarray(inputs["L0_ffn_up_zero"])[perm])
    sbd = tp(np.asarray(inputs["L0_ffn_down_scale"]))
    zbd = tp(np.asarray(inputs["L0_ffn_down_zero"]))

    bqT = np.ascontiguousarray(bqkv[0:2048].reshape(16, 128).T)
    bvT = np.ascontiguousarray(bqkv[2048:3072].reshape(8, 128).T)

    in_maps = []
    for c in range(NCORE):
        b, r = c // 4, c % 4
        in_maps.append(dict(
            i_xqT=np.ascontiguousarray(x[b, r * TOWN:(r + 1) * TOWN].T),
            i_xo=np.ascontiguousarray(x[b, r * TOWN:(r + 1) * TOWN]),
            i_wqkv=wqkv, i_bqT=bqT, i_bo=bo, i_bvT=bvT, i_wo=wo,
            i_pg=pg, i_pu=pu, i_pd=pd,
            i_sbg=sbg, i_zbg=zbg, i_sbu=sbu, i_zbu=zbu, i_sbd=sbd, i_zbd=zbd,
        ))
    return in_maps


def kernel(**inputs):
    in_maps = _host_inputs(inputs)
    nc, io = build_nc()
    res = run_bass_kernel_spmd(nc, in_maps, core_ids=list(range(NCORE)))
    out = np.empty((B, S, H), dtype=np.float32)
    for c in range(NCORE):
        b, r = c // 4, c % 4
        out[b, r * TOWN:(r + 1) * TOWN] = res.results[c]["out"]
    return out


# version 19
# speedup vs baseline: 1.3217x; 1.0004x over previous
import contextlib
import numpy as np
import concourse.bass as bass
import concourse.mybir as mybir
from concourse.bass_utils import run_bass_kernel_spmd

F32 = mybir.dt.float32
BF16 = mybir.dt.bfloat16
U8 = mybir.dt.uint8
I32 = mybir.dt.int32
I16 = mybir.dt.int16
AF = mybir.ActivationFunctionType
ALU = mybir.AluOpType
AX = mybir.AxisListType

B, S, H, D, NI = 2, 8192, 4096, 1024, 11008
TOWN = 2048
NCORE = 8
SOFT_SCALE = 0.03125
EPS = 1e-5
DQ_SLABS_D = [(0, 2048), (2048, 2048)]
RG = [[0, 1, 2, 3], [4, 5, 6, 7]]


class SemT:
    def __init__(self, nc, es, name):
        self.h = es.enter_context(nc.semaphore(name))
        self.tot = 0

    def inc(self, instr, n=16):
        instr.then_inc(self.h, n)
        self.tot += n
        return instr


class Ctx:
    def __init__(self, nc, es):
        self.nc = nc
        self.es = es
        self.bar = SemT(nc, es, "bar")
        self.sems = {}
        self.chain_sems = {}

    def sem(self, name):
        s = SemT(self.nc, self.es, name)
        self.sems[name] = s
        return s

    def chain(self, eng, ins):
        key = ins.ins.engine if hasattr(ins, "ins") else None
        name = f"ch_{key}"
        if name not in self.chain_sems:
            self.chain_sems[name] = SemT(self.nc, self.es, name)
        s = self.chain_sems[name]
        s.inc(ins, 1)
        eng.wait_ge(s.h, s.tot)


ALL_ENGINES = ("tensor", "vector", "scalar", "gpsimd", "sync")


def run_phase(ctx, name, emitters):
    nc = ctx.nc
    base = ctx.bar.tot
    ctx.bar.tot += len(ALL_ENGINES)
    tgt = ctx.bar.tot

    with nc.Block(name) as blk:
        for ename in ALL_ENGINES:
            fn = emitters.get(ename)

            def body(eng, fn=fn):
                if fn is not None:
                    fn(eng)
                eng.drain().then_inc(ctx.bar.h, 1)
                eng.wait_ge(ctx.bar.h, tgt)
            getattr(blk, ename)(body)




def emit_P0(ctx, io, bufs_es, hooks=None):
    nc = ctx.nc
    E = bufs_es.enter_context
    W = 2048
    s_pi = [E(nc.sbuf_tensor(f"p_pi{i}", [128, W], I16)) for i in range(2)]
    s_sb = [E(nc.sbuf_tensor(f"p_sb{i}", [128, W], BF16)) for i in range(2)]
    s_zb = [E(nc.sbuf_tensor(f"p_zb{i}", [128, W], BF16)) for i in range(2)]
    s_ti = [E(nc.sbuf_tensor(f"p_ti{i}", [128, W], I16)) for i in range(2)]
    s_wh = [E(nc.sbuf_tensor(f"p_wh{i}", [128, W], BF16)) for i in range(2)]
    s_wl = [E(nc.sbuf_tensor(f"p_wl{i}", [128, W], BF16)) for i in range(2)]

    ld = [ctx.sem("p_ld0"), ctx.sem("p_ld1")]
    dqs = [ctx.sem("p_dqv"), ctx.sem("p_dqp")]
    wo = [ctx.sem("p_wo0"), ctx.sem("p_wo1")]

    jobs = [
        (io["i_pd"], io["i_sbd"], io["i_zbd"], io["sc_wd"], 43, 43, DQ_SLABS_D, True),
    ]
    slabs = []
    for (pk, sbv, zbv, outw, njt, lofs, slablist, is_down) in jobs:
        for jt in range(njt):
            for (c0, w) in slablist:
                slabs.append((pk, sbv, zbv, outw, jt, lofs, c0, w, is_down))

    def emit_dq(eng, si):
        (pk, sbv, zbv, outw, jt, lofs, c0, w, is_down) = slabs[si]
        b = si % 2
        eng.wait_ge(ld[b].h, (si // 2 + 1) * 48)
        if si >= 2:
            eng.wait_ge(wo[b].h, (si // 2) * 32)
        _v = eng.tensor_scalar(out=s_ti[b][:, 0:w], in0=s_pi[b][:, 0:w], scalar1=4,
                               scalar2=None, op0=ALU.logical_shift_right)
        ctx.chain(eng, _v)
        _v = eng.tensor_tensor(out=s_wh[b][:, 0:w], in0=s_ti[b][:, 0:w],
                               in1=s_zb[b][:, 0:w], op=ALU.subtract)
        ctx.chain(eng, _v)
        eng.tensor_tensor(out=s_wh[b][:, 0:w], in0=s_wh[b][:, 0:w],
                          in1=s_sb[b][:, 0:w], op=ALU.mult)
        _v = eng.tensor_scalar(out=s_ti[b][:, 0:w], in0=s_pi[b][:, 0:w], scalar1=15,
                               scalar2=None, op0=ALU.bitwise_and)
        ctx.chain(eng, _v)
        _v = eng.tensor_tensor(out=s_wl[b][:, 0:w], in0=s_ti[b][:, 0:w],
                               in1=s_zb[b][:, 0:w], op=ALU.subtract)
        ctx.chain(eng, _v)
        dqs[b].inc(eng.tensor_tensor(out=s_wl[b][:, 0:w], in0=s_wl[b][:, 0:w],
                                     in1=s_sb[b][:, 0:w], op=ALU.mult), 1)

    def emit_wout(g, si):
        (pk, sbv, zbv, outw, jt, lofs, c0, w, is_down) = slabs[si]
        nt, t0 = w // 128, c0 // 128
        g.wait_ge(dqs[si % 2].h, si // 2 + 1)
        for src, rt in ((s_wh[si % 2], jt), (s_wl[si % 2], lofs + jt)):
            dst = outw[t0:t0 + nt, :, rt * 128:(rt + 1) * 128].rearrange("a p c -> p a c")
            wo[si % 2].inc(g.dma_start(out=dst,
                               in_=src[:, 0:w].rearrange("p (a c) -> p a c", c=128)), 16)

    def body_gpsimd(g):
        for si, (pk, sbv, zbv, outw, jt, lofs, c0, w, is_down) in enumerate(slabs):
            for fn in (hooks or {}).get(si, []):
                fn(g)
            if si >= 2:
                g.wait_ge(dqs[si % 2].h, si // 2)
            r0 = jt * 128
            g0 = jt * 4
            ld[si % 2].inc(g.dma_start(out=s_pi[si % 2][:, 0:w], in_=pk[r0:r0 + 128, c0:c0 + w]), 16)
            ld[si % 2].inc(g.dma_start(
                out=s_sb[si % 2][:, 0:w],
                in_=sbv[g0:g0 + 4, c0:c0 + w].unsqueeze(1).broadcast_to([4, 32, w])), 16)
            ld[si % 2].inc(g.dma_start(
                out=s_zb[si % 2][:, 0:w],
                in_=zbv[g0:g0 + 4, c0:c0 + w].unsqueeze(1).broadcast_to([4, 32, w])), 16)
            if si >= 1:
                emit_wout(g, si - 1)
        emit_wout(g, len(slabs) - 1)
        g.wait_ge(wo[0].h, wo[0].tot)
        g.wait_ge(wo[1].h, wo[1].tot)

    def body_vector(v):
        for si in range(len(slabs)):
            emit_dq(v, si)

    return {"gpsimd": body_gpsimd, "vector": body_vector}




def emit_A(ctx, io, bufs_es):
    nc = ctx.nc
    E = bufs_es.enter_context
    s_w = E(nc.sbuf_tensor("a_w", [128, 32 * 1024], BF16))
    s_wsl = [E(nc.sbuf_tensor(f"a_wsl{i}", [128, 2048], F32)) for i in range(2)]
    s_xb = [E(nc.sbuf_tensor(f"a_xb{i}", [128, 32 * 256], BF16)) for i in range(2)]
    s_st = [E(nc.sbuf_tensor(f"a_st{i}", [128, 2048], BF16)) for i in range(2)]
    s_bqT = E(nc.sbuf_tensor("a_bqT", [128, 16], F32))
    ps_mm = [E(nc.psum_tensor(f"a_ps{i}", [128, 2048], F32)) for i in range(2)]

    ld_b = ctx.sem("a_ldb")
    ld_w = [ctx.sem("a_ldw0"), ctx.sem("a_ldw1")]
    ld_x = ctx.sem("a_ldx")
    cv_w = ctx.sem("a_cvw")
    mm_d = ctx.sem("a_mmd")
    ev_d = ctx.sem("a_evd")
    st_o = [ctx.sem("a_sto0"), ctx.sem("a_sto1")]

    passes = [
        ("k", 1024, 8, io["i_xqT"], io["sc_kTo"], 8),
        ("v", 2048, 8, io["i_xqT"], None, None),
        ("q", 0, 8, io["i_xqT"], io["sc_qT"], 0),
    ]
    nsl_cum = [0, 8, 16, 24]
    ctx.a_sto = st_o

    def body_sync(sync):
        ld_b.inc(sync.dma_start(out=s_bqT[:, :], in_=io["i_bqT"]), 16)
        for pi, (pname, wofs, nsl, xsrc, outT, bcol) in enumerate(passes):
            for i in range(16):
                gsi = pi * 16 + i
                if gsi >= 2:
                    sync.wait_ge(cv_w.h, gsi - 1)
                ld_w[gsi % 2].inc(sync.dma_start(
                    out=s_wsl[gsi % 2][:, :].rearrange("p (a f) -> p a f", a=2),
                    in_=io["i_wqkv"].rearrange("(a p) f -> p a f", p=128)[
                        :, 2 * i:2 * i + 2, wofs:wofs + 1024]), 16)

    def body_scalar(sc):
        for pi, (pname, wofs, nsl, xsrc, outT, bcol) in enumerate(passes):
            if pi >= 1:
                sc.wait_ge(mm_d.h, nsl_cum[pi])
            for i in range(16):
                gsi = pi * 16 + i
                sc.wait_ge(ld_w[gsi % 2].h, (gsi // 2 + 1) * 16)
                cv_w.inc(sc.activation(out=s_w[:, i * 2048:(i + 1) * 2048],
                              in_=s_wsl[gsi % 2][:, :], func=AF.Copy), 1)
            def emit_evac(i):
                gxi = nsl_cum[pi] + i
                sc.wait_ge(mm_d.h, gxi + 1)
                if gxi >= 2:
                    sc.wait_ge(st_o[gxi % 2].h, (gxi // 2) * 16)
                if pname == "v":
                    ins = sc.activation(out=s_st[i % 2][:, :], in_=ps_mm[i % 2][:, :],
                                           func=AF.Copy)
                    ev_d.inc(ins, 1)
                else:
                    for dt in range(8):
                        ins = sc.activation(out=s_st[i % 2][:, dt * 256:(dt + 1) * 256],
                                      in_=ps_mm[i % 2][:, dt * 256:(dt + 1) * 256],
                                      func=AF.Identity,
                                      bias=s_bqT[:, bcol + dt: bcol + dt + 1])
                    ev_d.inc(ins, 1)
                sc.wait_ge(ev_d.h, ev_d.tot)
                c0 = i * 256
                if pname == "v":
                    st_o[gxi % 2].inc(sc.dma_start(
                        out=io["sc_vo"][c0:c0 + 256, :].rearrange("(a p) d -> p a d", p=128),
                        in_=s_st[i % 2][:, :].rearrange("p (a d) -> p a d", a=2)), 16)
                else:
                    st_o[gxi % 2].inc(sc.dma_start(
                        out=outT.rearrange("(a p) t -> p a t", p=128)[:, :, c0:c0 + 256],
                        in_=s_st[i % 2][:, :].rearrange("p (a t) -> p a t", a=8)), 16)

            for i in range(nsl):
                if i >= 1:
                    emit_evac(i - 1)
            emit_evac(nsl - 1)
        sc.wait_ge(st_o[0].h, st_o[0].tot)
        sc.wait_ge(st_o[1].h, st_o[1].tot)

    def body_tensor(t):
        t.wait_ge(ld_b.h, 16)
        for pi, (pname, wofs, nsl, xsrc, outT, bcol) in enumerate(passes):
            for i in range(nsl):
                gxi = nsl_cum[pi] + i
                t.wait_ge(cv_w.h, (pi + 1) * 16)
                t.wait_ge(ld_x.h, 16 * (gxi + 1))
                if gxi >= 2:
                    t.wait_ge(ev_d.h, gxi - 1)
                ps = ps_mm[i % 2]
                xb = s_xb[i % 2]
                if pname == "v":
                    for tsub in range(2):
                        for a in range(32):
                            for dc in range(2):
                                o0 = tsub * 1024 + dc * 512
                                _i = t.matmul(ps[:, o0:o0 + 512],
                                         xb[:, a * 256 + tsub * 128: a * 256 + tsub * 128 + 128],
                                         s_w[:, a * 1024 + dc * 512: a * 1024 + (dc + 1) * 512],
                                         start=(a == 0), stop=(a == 31))
                else:
                    for dt in range(8):
                        for a in range(32):
                            _i = t.matmul(ps[:, dt * 256:(dt + 1) * 256],
                                     s_w[:, a * 1024 + dt * 128: a * 1024 + dt * 128 + 128],
                                     xb[:, a * 256:(a + 1) * 256],
                                     start=(a == 0), stop=(a == 31))
                mm_d.inc(_i, 1)

    def mk_xload(gxi):
        pi, i = gxi // 8, gxi % 8
        xsrc = passes[pi][3]
        c0 = i * 256

        def f(g):
            if gxi >= 2:
                g.wait_ge(mm_d.h, gxi - 1)
            ld_x.inc(g.dma_start(
                out=s_xb[gxi % 2][:, :].rearrange("p (a t) -> p a t", t=256),
                in_=xsrc[:, c0:c0 + 256].rearrange("(a p) t -> p a t", p=128)), 16)
        return f

    xhooks = {}
    for gxi in range(24):
        xhooks.setdefault(min(2 * gxi, 84), []).append(mk_xload(gxi))

    return {"sync": body_sync, "scalar": body_scalar, "tensor": body_tensor,
            "xhooks": xhooks}





def emit_B1(ctx, io, bufs_es):
    nc = ctx.nc
    E = bufs_es.enter_context
    s_qt = [None, None]
    s_qt[0] = E(nc.sbuf_tensor("b_qt0", [128, 8 * 128], BF16))
    s_kT = E(nc.sbuf_tensor("b_kT", [128, 8 * 8192], BF16))
    s_qt[1] = E(nc.sbuf_tensor("b_qt1", [128, 8 * 128], BF16))
    s_sc = E(nc.sbuf_tensor("b_sc", [128, 8192], F32))
    s_pb = [E(nc.sbuf_tensor(f"b_pb{i}", [128, 8192], BF16)) for i in range(2)]
    s_mx = E(nc.sbuf_tensor("b_mx", [128, 16], F32))
    s_nb = E(nc.sbuf_tensor("b_nb", [128, 16], F32))
    s_l = E(nc.sbuf_tensor("b_l", [128, 16], F32))
    ps_s = [E(nc.psum_tensor(f"b_pss{i}", [128, 2048], F32)) for i in range(2)]

    b_lk = ctx.sem("b_lk")
    b_lq = [ctx.sem("b_lq0"), ctx.sem("b_lq1")]
    b_mm = ctx.sem("b_mm")
    b_ev = ctx.sem("b_ev")
    b_mx = ctx.sem("b_mx")
    b_ex = ctx.sem("b_ex")
    b_po = [ctx.sem("b_po0"), ctx.sem("b_po1")]
    b_rl = ctx.sem("b_rl")

    def body_sync(sync):
        sync.wait_ge(ctx.cs_k.h, 1)
        for r in range(4):
            b_lk.inc(sync.dma_start(
                out=s_kT[:, :].rearrange("p (a t) -> p a t", t=8192)[:, :, r * 2048:(r + 1) * 2048],
                in_=io["sc_kag"].rearrange("(r a p) t -> r p a t", r=4, p=128)[r]), 16)
        for qt in range(16):
            if qt >= 2:
                sync.wait_ge(b_mm.h, 4 * qt - 4)
            b_lq[qt % 2].inc(sync.dma_start(
                out=s_qt[qt % 2][:, :],
                in_=io["sc_qT"].rearrange("(a p) t -> p a t", p=128)[:, :, qt * 128:(qt + 1) * 128]), 16)

    def body_tensor(t):
        t.wait_ge(b_lk.h, 64)
        t.wait_ge(ctx.id_sem.h, 1)
        for qt in range(16):
            t.wait_ge(b_lq[qt % 2].h, (qt // 2 + 1) * 16)
            for su in range(4):
                gi = 4 * qt + su
                if gi >= 2:
                    t.wait_ge(b_ev.h, gi - 1)
                for dt in range(8):
                    for kc in range(4):
                        _i = t.matmul(ps_s[gi % 2][:, kc * 512:(kc + 1) * 512],
                                 s_qt[qt % 2][:, dt * 128:(dt + 1) * 128],
                                 s_kT[:, dt * 8192 + su * 2048 + kc * 512:
                                      dt * 8192 + su * 2048 + (kc + 1) * 512],
                                 start=(dt == 0), stop=(dt == 7))
                b_mm.inc(_i, 1)

    def body_scalar(sc):
        for qt in range(16):
            sc.wait_ge(b_mx.h, qt + 1)
            if qt >= 2:
                sc.wait_ge(b_po[qt % 2].h, (qt // 2) * 16)
            b_ex.inc(sc.activation(out=s_pb[qt % 2][:, :], in_=s_sc[:, :], func=AF.Exp,
                                   bias=s_nb[:, qt:qt + 1], scale=SOFT_SCALE,
                                   accum_out=s_l[:, qt:qt + 1]), 1)

    def body_vector(v):
        for qt in range(16):
            if qt >= 1:
                v.wait_ge(b_ex.h, qt)
            for su in range(4):
                gi = 4 * qt + su
                v.wait_ge(b_mm.h, gi + 1)
                _v = v.reduce_max(s_mx[:, 4 * (qt % 2) + su:4 * (qt % 2) + su + 1],
                                  ps_s[gi % 2][:, :], axis=AX.X)
                if su == 3:
                    ctx.chain(v, _v)
                b_ev.inc(v.tensor_copy(out=s_sc[:, su * 2048:(su + 1) * 2048],
                                       in_=ps_s[gi % 2][:, :]), 1)
            _v = v.reduce_max(s_nb[:, qt:qt + 1],
                              s_mx[:, 4 * (qt % 2):4 * (qt % 2) + 4], axis=AX.X)
            ctx.chain(v, _v)
            b_mx.inc(v.tensor_scalar(out=s_nb[:, qt:qt + 1], in0=s_nb[:, qt:qt + 1],
                                     scalar1=-SOFT_SCALE, scalar2=None, op0=ALU.mult), 1)
            v.wait_ge(b_ex.h, qt + 1)
            b_rl.inc(v.reciprocal(io["rl"][:, qt:qt + 1], s_l[:, qt:qt + 1]), 1)

    def body_gpsimd(g):
        for qt in range(16):
            g.wait_ge(b_ex.h, qt + 1)
            b_po[qt % 2].inc(g.dma_start(
                out=io["sc_p"][qt * 128:(qt + 1) * 128, :], in_=s_pb[qt % 2][:, :]), 16)
        g.wait_ge(b_po[0].h, b_po[0].tot)
        g.wait_ge(b_po[1].h, b_po[1].tot)

    return {"sync": body_sync, "tensor": body_tensor, "scalar": body_scalar,
            "vector": body_vector, "gpsimd": body_gpsimd}




def emit_B2(ctx, io, bufs_es, s_ot):
    nc = ctx.nc
    E = bufs_es.enter_context
    s_pl = [None, None]
    s_pl[0] = E(nc.sbuf_tensor("c_pl0", [128, 4096], BF16))
    s_v = E(nc.sbuf_tensor("c_v", [128, 64 * 1024], BF16))
    s_pl[1] = E(nc.sbuf_tensor("c_pl1", [128, 4096], BF16))
    s_o = E(nc.sbuf_tensor("c_o", [128, 1024], BF16))
    s_ptl = E(nc.sbuf_tensor("c_ptl", [128, 64 * 128], BF16))
    ps_o = [E(nc.psum_tensor(f"c_pso{i}", [128, 1024], F32)) for i in range(2)]
    ps_t = [E(nc.psum_tensor(f"c_pst{i}", [128, 1024], BF16)) for i in range(2)]
    ps_ot = E(nc.psum_tensor("c_psot", [128, 1024], BF16))

    c_lv = ctx.sem("c_lv")
    c_lp = [ctx.sem("c_lp0"), ctx.sem("c_lp1")]
    c_tp = ctx.sem("c_tp")
    c_pe = ctx.sem("c_pe")
    c_mm = ctx.sem("c_mm")
    c_ev = ctx.sem("c_ev")
    c_tpo = ctx.sem("c_tpo")
    c_oe = ctx.sem("c_oe")

    def body_sync(sync):
        sync.wait_ge(ctx.cs_v.h, 1)
        c_lv.inc(sync.dma_start(out=s_v[:, :],
                                in_=io["sc_vag"].rearrange("(a p) d -> p a d", p=128)), 16)
        for hh in range(32):
            if hh >= 2:
                sync.wait_ge(c_tp.h, 4 * hh - 4)
            c_lp[hh % 2].inc(sync.dma_start(
                out=s_pl[hh % 2][:, :],
                in_=io["sc_p"][(hh // 2) * 128:(hh // 2 + 1) * 128,
                               (hh % 2) * 4096:(hh % 2 + 1) * 4096]), 16)

    def body_tensor(t):
        t.wait_ge(c_lv.h, 16)
        for qt in range(16):
            if qt >= 1:
                t.wait_ge(c_mm.h, qt)
            for grp in range(8):
                gg = 8 * qt + grp
                hh = 2 * qt + grp // 4
                t.wait_ge(c_lp[hh % 2].h, (hh // 2 + 1) * 16)
                if gg >= 2:
                    t.wait_ge(c_pe.h, gg - 1)
                for j in range(8):
                    _i = t.transpose(ps_t[gg % 2][:, j * 128:(j + 1) * 128],
                                s_pl[hh % 2][:, ((grp % 4) * 8 + j) * 128:
                                             ((grp % 4) * 8 + j + 1) * 128],
                                io["ident"][:])
                c_tp.inc(_i, 1)
            t.wait_ge(c_pe.h, 8 * qt + 8)
            if qt >= 2:
                t.wait_ge(c_ev.h, qt - 1)
            for kt in range(64):
                for dc in range(2):
                    _i = t.matmul(ps_o[qt % 2][:, dc * 512:(dc + 1) * 512],
                             s_ptl[:, kt * 128:(kt + 1) * 128],
                             s_v[:, kt * 1024 + dc * 512: kt * 1024 + (dc + 1) * 512],
                             start=(kt == 0), stop=(kt == 63))
            c_mm.inc(_i, 1)
            t.wait_ge(c_ev.h, qt + 1)
            t.wait_ge(c_oe.h, qt)
            for j in range(8):
                _i = t.transpose(ps_ot[:, j * 128:(j + 1) * 128],
                            s_o[:, j * 128:(j + 1) * 128], io["ident"][:])
            c_tpo.inc(_i, 1)

    def body_scalar(sc):
        for qt in range(16):
            for grp in range(8):
                gg = 8 * qt + grp
                sc.wait_ge(c_tp.h, gg + 1)
                c_pe.inc(sc.activation(out=s_ptl[:, grp * 1024:(grp + 1) * 1024],
                                       in_=ps_t[gg % 2][:, :], func=AF.Copy), 1)
            sc.wait_ge(c_mm.h, qt + 1)
            if qt >= 1:
                sc.wait_ge(c_tpo.h, qt)
            c_ev.inc(sc.activation(out=s_o[:, :], in_=ps_o[qt % 2][:, :],
                                   func=AF.Identity, bias=0.0,
                                   scale=io["rl"][:, qt:qt + 1]), 1)
            sc.wait_ge(c_tpo.h, qt + 1)
            c_oe.inc(sc.activation(
                out=s_ot[:, :].rearrange("p (a q) -> p a q", q=2048)[:, :, qt * 128:(qt + 1) * 128],
                in_=ps_ot[:, :], func=AF.Copy), 1)

    return {"sync": body_sync, "tensor": body_tensor, "scalar": body_scalar}




def emit_B3(ctx, io, bufs_es, s_ot):
    nc = ctx.nc
    E = bufs_es.enter_context
    s_wo = E(nc.sbuf_tensor("d_wo", [128, 8 * 4096], BF16))
    s_wsl = [E(nc.sbuf_tensor(f"d_wsl{i}", [128, 1024], F32)) for i in range(2)]
    s_bo128 = E(nc.sbuf_tensor("d_bo128", [128, 4096], F32))
    s_bvT = E(nc.sbuf_tensor("d_bvT", [128, 8], BF16))
    s_ones = E(nc.sbuf_tensor("d_ones", [1, 128], BF16))
    s_xo = [E(nc.sbuf_tensor(f"d_xo{i}", [128, 4096], F32)) for i in range(2)]
    s_pr = E(nc.sbuf_tensor("d_pr", [128, 4096], F32))
    s_xe = E(nc.sbuf_tensor("d_xe", [128, 2048], BF16))
    s_xoo = E(nc.sbuf_tensor("d_xoo", [128, 2048], BF16))
    s_xms = E(nc.sbuf_tensor("d_xms", [128, 32 * 128], BF16))
    s_bor = s_pr[0:1, 0:4096]
    s_bob = s_xms[0:1, 0:4096]
    s_bx = s_xo
    s_xm = s_pr
    s_bns = E(nc.sbuf_tensor("d_bns", [128, 48], F32))
    s_mv = E(nc.sbuf_tensor("d_mv", [128, 2], F32))
    s_sd = E(nc.sbuf_tensor("d_sd", [128, 1], F32))
    s_rs = E(nc.sbuf_tensor("d_rs", [128, 1], F32))
    s_nmr = E(nc.sbuf_tensor("d_nmr", [128, 1], F32))
    ps_p = E(nc.psum_tensor("d_psp", [128, 2048], F32))
    ps_xt = [E(nc.psum_tensor(f"d_psxt{i}", [128, 512], BF16)) for i in range(2)]

    d_lb = ctx.sem("d_lb")
    d_lv = ctx.sem("d_lv")
    d_lw = [ctx.sem("d_lw0"), ctx.sem("d_lw1")]
    d_cw = ctx.sem("d_cw")
    d_on = ctx.sem("d_on")
    d_bm = ctx.sem("d_bm")
    d_bv = ctx.sem("d_bv")
    d_bc = ctx.sem("d_bc")
    d_bb = ctx.sem("d_bb")
    d_be = ctx.sem("d_be")
    d_lx = [ctx.sem("d_lx0"), ctx.sem("d_lx1")]
    d_bx = ctx.sem("d_bx")
    d_mm = ctx.sem("d_mm")
    d_pa = ctx.sem("d_pa")
    d_st = ctx.sem("d_st")
    d_sq = ctx.sem("d_sq")
    d_ln = ctx.sem("d_ln")
    d_xw = ctx.sem("d_xw")
    d_sp = ctx.sem("d_sp")
    d_tp = ctx.sem("d_tp")
    d_se = ctx.sem("d_se")
    d_so = ctx.sem("d_so")

    def body_sync(sync):
        d_lb.inc(sync.dma_start(out=s_bor[:, :], in_=io["i_bo"][None, :]), 16)
        for i in range(32):
            if i >= 2:
                sync.wait_ge(d_cw.h, i - 1)
            d_lw[i % 2].inc(sync.dma_start(
                out=s_wsl[i % 2][:, :],
                in_=io["i_wo"].rearrange("(a p) f -> a p f", p=128)[
                    i // 4, :, (i % 4) * 1024:(i % 4 + 1) * 1024]), 16)
        for tt in range(16):
            if tt >= 2:
                sync.wait_ge(d_bx.h, tt - 1)
            d_lx[tt % 2].inc(sync.dma_start(
                out=s_xo[tt % 2][:, :],
                in_=io["i_xo"][tt * 128:(tt + 1) * 128, :]), 16)
            sync.wait_ge(d_ln.h, tt + 1)
            d_xw.inc(sync.dma_start(
                out=io["sc_xmid"][tt * 128:(tt + 1) * 128, :], in_=s_pr[:, :]), 16)
        sync.wait_ge(d_xw.h, d_xw.tot)

    def body_scalar(sc):
        for i in range(32):
            sc.wait_ge(d_lw[i % 2].h, (i // 2 + 1) * 16)
            d_cw.inc(sc.activation(out=s_wo[:, i * 1024:(i + 1) * 1024],
                                   in_=s_wsl[i % 2][:, :], func=AF.Copy), 1)
        sc.wait_ge(d_bv.h, 2)
        d_bc.inc(sc.activation(out=s_bob[:, :], in_=s_bo128[0:1, :], func=AF.Copy), 1)
        for r in range(2):
            sc.wait_ge(d_bb.h, r + 1)
            d_be.inc(sc.activation(out=s_bo128[:, r * 2048:(r + 1) * 2048],
                                   in_=ps_p[:, :], func=AF.Copy), 1)
        for tt in range(16):
            sc.wait_ge(d_st.h, tt + 1)
            d_sq.inc(sc.activation(out=s_sd[:, :], in_=s_mv[:, 1:2],
                                   func=AF.Sqrt), 1)
            for grp in range(8):
                gg = 8 * tt + grp
                sc.wait_ge(d_tp.h, gg + 1)
                if grp == 0:
                    sc.wait_ge(d_so.h, 16 * tt)
                d_se.inc(sc.activation(out=s_xms[:, grp * 512:(grp + 1) * 512],
                                       in_=ps_xt[grp % 2][:, :], func=AF.Copy), 1)

    def body_tensor(t):
        t.wait_ge(d_cw.h, 32)
        t.wait_ge(d_lb.h, 16)
        t.wait_ge(d_lv.h, 16)
        for half in range(2):
            if half == 1:
                t.wait_ge(d_bv.h, 1)
            for cc in range(4):
                for dt in range(8):
                    _i = t.matmul(ps_p[0:1, cc * 512:(cc + 1) * 512],
                             s_bvT[:, dt:dt + 1],
                             s_wo[:, dt * 4096 + half * 2048 + cc * 512:
                                  dt * 4096 + half * 2048 + (cc + 1) * 512],
                             start=(dt == 0), stop=(dt == 7))
            d_bm.inc(_i, 1)
        t.wait_ge(d_bc.h, 1)
        t.wait_ge(d_on.h, 1)
        t.wait_ge(d_bv.h, 2)
        for r in range(2):
            if r == 1:
                t.wait_ge(d_be.h, 1)
            for cc in range(4):
                _i = t.matmul(ps_p[:, cc * 512:(cc + 1) * 512], s_ones[0:1, :],
                         s_bob[0:1, r * 2048 + cc * 512: r * 2048 + (cc + 1) * 512],
                         start=True, stop=True)
            d_bb.inc(_i, 1)
        for tt in range(16):
            for half in range(2):
                gi = 2 * tt + half
                t.wait_ge(d_be.h, 2) if gi == 0 else None
                if gi >= 1:
                    t.wait_ge(d_pa.h, gi)
                for dt in range(8):
                    for cc in range(4):
                        _i = t.matmul(ps_p[:, cc * 512:(cc + 1) * 512],
                                 s_ot[:, dt * 2048 + tt * 128: dt * 2048 + (tt + 1) * 128],
                                 s_wo[:, dt * 4096 + half * 2048 + cc * 512:
                                      dt * 4096 + half * 2048 + (cc + 1) * 512],
                                 start=(dt == 0), stop=(dt == 7))
                d_mm.inc(_i, 1)
            t.wait_ge(d_sp.h, tt + 1)
            for half in range(2):
                src = s_xe if half == 0 else s_xoo
                for grp in range(4):
                    gg = 8 * tt + half * 4 + grp
                    if gg >= 2:
                        t.wait_ge(d_se.h, gg - 1)
                    for j in range(4):
                        _i = t.transpose(ps_xt[gg % 2][:, j * 128:(j + 1) * 128],
                                    src[:, (grp * 4 + j) * 128:(grp * 4 + j + 1) * 128],
                                    io["ident"][:])
                    d_tp.inc(_i, 1)

    def body_vector(v):
        for half in range(2):
            v.wait_ge(d_bm.h, half + 1)
            v.wait_ge(d_lb.h, 16)
            d_bv.inc(v.tensor_tensor(out=s_bo128[0:1, half * 2048:(half + 1) * 2048],
                                     in0=ps_p[0:1, 0:2048],
                                     in1=s_bor[0:1, half * 2048:(half + 1) * 2048],
                                     op=ALU.add), 1)
        for tt in range(16):
            v.wait_ge(d_lx[tt % 2].h, (tt // 2 + 1) * 16)
            v.wait_ge(d_be.h, 2)
            if tt >= 1:
                v.wait_ge(d_pa.h, 2 * tt)
            d_bx.inc(v.tensor_tensor(out=s_xo[tt % 2][:, :], in0=s_bo128[:, :],
                                     in1=s_xo[tt % 2][:, :], op=ALU.add), 1)
            v.wait_ge(d_bx.h, d_bx.tot)
            for half in range(2):
                gi = 2 * tt + half
                v.wait_ge(d_mm.h, gi + 1)
                if half == 0 and tt >= 1:
                    v.wait_ge(d_xw.h, 16 * tt)
                d_pa.inc(v.tensor_tensor(out=s_pr[:, half * 2048:(half + 1) * 2048],
                                         in0=ps_p[:, :],
                                         in1=s_xo[tt % 2][:, half * 2048:(half + 1) * 2048],
                                         op=ALU.add), 1)
            v.wait_ge(d_pa.h, d_pa.tot)
            for ch in range(8):
                _v = v.bn_stats(s_bns[:, ch * 6:(ch + 1) * 6], s_pr[:, ch * 512:(ch + 1) * 512])
            ctx.chain(v, _v)
            _v = v.bn_aggr(s_mv[:, :], s_bns[:, :])
            ctx.chain(v, _v)
            d_st.inc(v.tensor_scalar(out=s_mv[:, 1:2], in0=s_mv[:, 1:2], scalar1=EPS,
                                     scalar2=None, op0=ALU.add), 1)
            v.wait_ge(d_sq.h, tt + 1)
            _v = v.reciprocal(s_rs[:, :], s_sd[:, :])
            ctx.chain(v, _v)
            _v = v.tensor_scalar(out=s_nmr[:, :], in0=s_mv[:, 0:1], scalar1=s_rs[:, 0:1],
                            scalar2=-1.0, op0=ALU.mult, op1=ALU.mult)
            ctx.chain(v, _v)
            d_ln.inc(v.tensor_scalar(out=s_pr[:, :], in0=s_pr[:, :], scalar1=s_rs[:, 0:1],
                                     scalar2=s_nmr[:, 0:1], op0=ALU.mult, op1=ALU.add), 1)
            v.wait_ge(d_ln.h, d_ln.tot)
            if tt >= 1:
                v.wait_ge(d_tp.h, 8 * tt)
            v.tensor_copy(out=s_xe[:, :], in_=s_pr[:, 0:4096:2])
            d_sp.inc(v.tensor_copy(out=s_xoo[:, :], in_=s_pr[:, 1:4096:2]), 1)

    def body_gpsimd(g):
        d_lv.inc(g.dma_start(out=s_bvT[:, :], in_=io["i_bvT"]), 16)
        d_on.inc(g.memset(s_ones[:, :], 1.0), 1)
        for tt in range(16):
            g.wait_ge(d_se.h, 8 * tt + 8)
            d_so.inc(g.dma_start(
                out=io["sc_xmT"][:, :, tt * 128:(tt + 1) * 128].rearrange("a p t -> p a t"),
                in_=s_xms[:, :].rearrange("p (a t) -> p a t", t=128)), 16)
        g.wait_ge(d_so.h, d_so.tot)

    return {"sync": body_sync, "scalar": body_scalar, "tensor": body_tensor,
            "vector": body_vector, "gpsimd": body_gpsimd}




def emit_C(ctx, io, bufs_es):
    nc = ctx.nc
    E = bufs_es.enter_context
    NB = 4
    s_w00 = E(nc.sbuf_tensor("e_w00", [128, 32 * 128], BF16))
    s_w10 = E(nc.sbuf_tensor("e_w10", [128, 32 * 128], BF16))
    s_xm = E(nc.sbuf_tensor("e_xm", [128, 32 * 1024], BF16))
    s_w01 = E(nc.sbuf_tensor("e_w01", [128, 32 * 128], BF16))
    s_w11 = E(nc.sbuf_tensor("e_w11", [128, 32 * 128], BF16))
    s_wt = [[s_w00, s_w01], [s_w10, s_w11]]
    s_pi = [E(nc.sbuf_tensor(f"e_pi{i}", [128, 2048], I16)) for i in range(NB)]
    s_th = E(nc.sbuf_tensor("e_th", [128, 2048], I16))
    s_tl = E(nc.sbuf_tensor("e_tl", [128, 2048], I16))
    s_sb = [E(nc.sbuf_tensor(f"e_sb{i}", [128, 2048], BF16)) for i in range(NB)]
    s_zb = [E(nc.sbuf_tensor(f"e_zb{i}", [128, 2048], BF16)) for i in range(NB)]
    s_sg = E(nc.sbuf_tensor("e_sg", [128, 1024], F32))
    s_ht = [E(nc.sbuf_tensor(f"e_ht{i}", [128, 1024], BF16)) for i in range(2)]
    ps_g = [E(nc.psum_tensor(f"e_psg{i}", [128, 1024], F32)) for i in range(2)]
    ps_u = [E(nc.psum_tensor(f"e_psu{i}", [128, 1024], F32)) for i in range(2)]

    e_lx = ctx.sem("e_lx")
    e_ld = [ctx.sem(f"e_ld{i}") for i in range(NB)]
    e_dq = ctx.sem("e_dq")
    e_mm = ctx.sem("e_mm")
    e_sl = ctx.sem("e_sl")
    e_hv = ctx.sem("e_hv")
    e_ho = ctx.sem("e_ho")

    MATIN = [(io["i_pg"], io["i_sbg"], io["i_zbg"]),
             (io["i_pu"], io["i_sbu"], io["i_zbu"])]

    def body_gpsimd2(g):
        for dqg in range(344):
            it, mat = (dqg % 172) // 2, dqg % 2
            b = dqg % NB
            if dqg >= NB:
                g.wait_ge(e_dq.h, dqg - (NB - 1))
            pk, sbv, zbv = MATIN[mat]
            e_ld[b].inc(g.dma_start(
                out=s_pi[b][:, :],
                in_=pk.rearrange("(a p) n -> p a n", p=128)[:, :, it * 128:(it + 1) * 128]), 16)
            for src, dst in ((sbv, s_sb[b]), (zbv, s_zb[b])):
                for g0 in range(4):
                    e_ld[b].inc(g.dma_start(
                        out=dst[g0 * 32:(g0 + 1) * 32, :],
                        in_=src.rearrange("(j g) n -> g j n", g=4)[g0, :, it * 128:(it + 1) * 128]
                            .unsqueeze(0).broadcast_to([32, 16, 128])), 16)

    def body_sync(sync):
        for pss in range(2):
            if pss == 1:
                sync.wait_ge(e_mm.h, 172)
            e_lx.inc(sync.dma_start(
                out=s_xm[:, :],
                in_=io["sc_xmT"][:, :, pss * 1024:(pss + 1) * 1024]
                    .rearrange("a p t -> p a t")), 16)
            for it in range(86):
                git = pss * 86 + it
                sync.wait_ge(e_hv.h, git + 1)
                e_ho.inc(sync.dma_start(
                    out=io["sc_hT"][it][:, pss * 1024:(pss + 1) * 1024],
                    in_=s_ht[git % 2][:, :]), 16)
        sync.wait_ge(e_ho.h, e_ho.tot)

    def body_vector(v):
        for pss in range(2):
            for it in range(86):
                git = pss * 86 + it
                for mat in range(2):
                    dqg = 2 * git + mat
                    b = dqg % NB
                    wb = git % 2
                    v.wait_ge(e_ld[b].h, (dqg // NB + 1) * 144)
                    if dqg >= 4:
                        v.wait_ge(e_mm.h, dqg - 3)
                    hi = s_wt[mat][wb][:, 0:2048]
                    lo = s_wt[mat][wb][:, 2048:4096]
                    v.tensor_scalar(out=s_th[:, :], in0=s_pi[b][:, :], scalar1=4,
                                    scalar2=None, op0=ALU.logical_shift_right)
                    _v = v.tensor_scalar(out=s_tl[:, :], in0=s_pi[b][:, :], scalar1=15,
                                         scalar2=None, op0=ALU.bitwise_and)
                    ctx.chain(v, _v)
                    v.tensor_tensor(out=hi, in0=s_th[:, :], in1=s_zb[b][:, :],
                                    op=ALU.subtract)
                    _v = v.tensor_tensor(out=lo, in0=s_tl[:, :], in1=s_zb[b][:, :],
                                         op=ALU.subtract)
                    ctx.chain(v, _v)
                    v.tensor_tensor(out=hi, in0=hi, in1=s_sb[b][:, :], op=ALU.mult)
                    e_dq.inc(v.tensor_tensor(out=lo, in0=lo, in1=s_sb[b][:, :],
                                             op=ALU.mult), 1)
                if git >= 1:
                    emit_hmult(v, git - 1)
            if pss == 1:
                emit_hmult(v, 171)

    def emit_hmult(v, git):
        v.wait_ge(e_mm.h, 2 * git + 2)
        v.wait_ge(e_sl.h, git + 1)
        if git >= 2:
            v.wait_ge(e_ho.h, 16 * (git - 1))
        e_hv.inc(v.tensor_tensor(out=s_ht[git % 2][:, :], in0=s_sg[:, :],
                                 in1=ps_u[git % 2][:, :], op=ALU.mult), 1)

    def body_tensor(t):
        for pss in range(2):
            t.wait_ge(e_lx.h, 16 * (pss + 1))
            for it in range(86):
                git = pss * 86 + it
                wb = git % 2
                for mat in range(2):
                    dqg = 2 * git + mat
                    t.wait_ge(e_dq.h, dqg + 1)
                    if git >= 2:
                        t.wait_ge((e_sl if mat == 0 else e_hv).h, git - 1)
                    ps = (ps_g if mat == 0 else ps_u)[git % 2]
                    for rt in range(32):
                        for tc in range(2):
                            _i = t.matmul(ps[:, tc * 512:(tc + 1) * 512],
                                     s_wt[mat][wb][:, rt * 128:(rt + 1) * 128],
                                     s_xm[:, rt * 1024 + tc * 512:
                                          rt * 1024 + (tc + 1) * 512],
                                     start=(rt == 0), stop=(rt == 31))
                    e_mm.inc(_i, 1)

    def body_scalar(sc):
        for git in range(172):
            sc.wait_ge(e_mm.h, 2 * git + 1)
            if git >= 1:
                sc.wait_ge(e_hv.h, git)
            e_sl.inc(sc.activation(out=s_sg[:, :], in_=ps_g[git % 2][:, :],
                                   func=AF.Silu), 1)

    return {"sync": body_sync, "tensor": body_tensor, "scalar": body_scalar,
            "vector": body_vector, "gpsimd": body_gpsimd2}




def emit_D(ctx, io, bufs_es):
    nc = ctx.nc
    E = bufs_es.enter_context
    s_hq = E(nc.sbuf_tensor("f_hq", [128, 86 * 512], BF16))
    s_wdh = [None] * 4
    s_wdh[0] = E(nc.sbuf_tensor("f_wd0", [128, 43 * 128], BF16))
    s_xo2 = [E(nc.sbuf_tensor("f_xo0", [128, 4096], F32))] * 2
    s_wdh[1] = E(nc.sbuf_tensor("f_wd1", [128, 43 * 128], BF16))
    s_row = E(nc.sbuf_tensor("f_row", [128, 4096], F32))
    s_wdh[2] = E(nc.sbuf_tensor("f_wd2", [128, 43 * 128], BF16))
    s_ff = E(nc.sbuf_tensor("f_ff", [128, 4 * 4096], BF16))
    s_wdh[3] = E(nc.sbuf_tensor("f_wd3", [128, 43 * 128], BF16))
    s_bns = E(nc.sbuf_tensor("f_bns", [128, 48], F32))
    s_mv = E(nc.sbuf_tensor("f_mv", [128, 2], F32))
    s_sd = E(nc.sbuf_tensor("f_sd", [128, 1], F32))
    s_rs = E(nc.sbuf_tensor("f_rs", [128, 1], F32))
    s_nmr = E(nc.sbuf_tensor("f_nmr", [128, 1], F32))
    s_tmp = [E(nc.sbuf_tensor(f"f_tmp{i}", [128, 512], BF16)) for i in range(2)]
    ps_d = [E(nc.psum_tensor(f"f_psd{i}", [128, 512], F32)) for i in range(2)]
    ps_t2 = [E(nc.psum_tensor(f"f_pst{i}", [128, 512], BF16)) for i in range(2)]

    f_lh = ctx.sem("f_lh")
    f_lw = [ctx.sem(f"f_lw{i}") for i in range(4)]
    f_mm = ctx.sem("f_mm")
    f_ev = ctx.sem("f_ev")
    f_tp = ctx.sem("f_tp")
    f_fe = ctx.sem("f_fe")
    f_lx = [ctx.sem("f_lx0"), ctx.sem("f_lx1")]
    f_ad = ctx.sem("f_ad")
    f_st = ctx.sem("f_st")
    f_sq = ctx.sem("f_sq")
    f_ln = ctx.sem("f_ln")
    f_oo = ctx.sem("f_oo")

    def emit_wload(sync, hgi):
        ht, half = (hgi % 64) // 2, hgi % 2
        if hgi >= 4:
            sync.wait_ge(f_mm.h, hgi - 3)
        f_lw[hgi % 4].inc(sync.dma_start(
            out=s_wdh[hgi % 4][:, :],
            in_=io["sc_wd"][ht][:, half * 5504:(half + 1) * 5504]), 16)

    def body_sync(sync):
        for tb in range(4):
            if tb == 0:
                for hgi in range(3):
                    emit_wload(sync, hgi)
            if tb >= 1:
                sync.wait_ge(f_mm.h, 64 * tb)
            for hf in range(2):
                for aq in range(2):
                    r0, r1 = (0, 21) if aq == 0 else (21, 43)
                    f_lh.inc(sync.dma_start(
                        out=s_hq[:, (hf * 43 + r0) * 512:(hf * 43 + r1) * 512]
                            .rearrange("p (a t) -> p a t", t=512),
                        in_=io["sc_hT"][hf * 43 + r0:hf * 43 + r1, :, tb * 512:(tb + 1) * 512]
                            .rearrange("a p t -> p a t")), 16)
            for hgi in range(64 * tb + 3, 64 * (tb + 1) + (3 if tb < 3 else 0)):
                emit_wload(sync, hgi)

    def body_gpsimd(g):
        for tb in range(4):
            for tt in range(4):
                gti = 4 * tb + tt
                if gti >= 1:
                    g.wait_ge(f_ad.h, gti)
                f_lx[gti % 2].inc(g.dma_start(
                    out=s_xo2[gti % 2][:, :],
                    in_=io["sc_xmid"][gti * 128:(gti + 1) * 128, :]), 16)
                g.wait_ge(f_ln.h, gti + 1)
                f_oo.inc(g.dma_start(
                    out=io["out"][gti * 128:(gti + 1) * 128, :], in_=s_row[:, :]), 16)
        g.wait_ge(f_oo.h, f_oo.tot)

    def emit_tr(t, ghi):
        t.wait_ge(f_ev.h, ghi + 1)
        if ghi >= 2:
            t.wait_ge(f_fe.h, ghi - 1)
        for j in range(4):
            _i = t.transpose(ps_t2[ghi % 2][:, j * 128:(j + 1) * 128],
                        s_tmp[ghi % 2][:, j * 128:(j + 1) * 128], io["ident"][:])
        f_tp.inc(_i, 1)

    def body_tensor(t):
        for tb in range(4):
            t.wait_ge(f_lh.h, 64 * (tb + 1))
            for ht in range(32):
                ghi = 32 * tb + ht
                for half in range(2):
                    hgi = 2 * ghi + half
                    t.wait_ge(f_lw[hgi % 4].h, (hgi // 4 + 1) * 16)
                    if half == 0 and ghi >= 2:
                        t.wait_ge(f_ev.h, ghi - 1)
                    for rt in range(43):
                        _i = t.matmul(ps_d[ghi % 2][:, :],
                                 s_wdh[hgi % 4][:, rt * 128:(rt + 1) * 128],
                                 s_hq[:, (half * 43 + rt) * 512:(half * 43 + rt + 1) * 512],
                                 start=(half == 0 and rt == 0), stop=(half == 1 and rt == 42))
                    f_mm.inc(_i, 1)
                if ghi >= 1:
                    emit_tr(t, ghi - 1)
        emit_tr(t, 127)

    def body_scalar(sc):
        for tb in range(4):
            for ht in range(32):
                ghi = 32 * tb + ht
                sc.wait_ge(f_mm.h, 2 * ghi + 2)
                if ghi >= 2:
                    sc.wait_ge(f_tp.h, ghi - 1)
                f_ev.inc(sc.activation(out=s_tmp[ghi % 2][:, :], in_=ps_d[ghi % 2][:, :],
                                       func=AF.Copy), 1)
                sc.wait_ge(f_tp.h, ghi + 1)
                if ht == 0:
                    sc.wait_ge(f_ad.h, 4 * tb)
                f_fe.inc(sc.activation(
                    out=s_ff[:, :].rearrange("p (a c) -> p a c", c=4096)[:, :, ht * 128:(ht + 1) * 128],
                    in_=ps_t2[ghi % 2][:, :], func=AF.Copy), 1)
            for tt in range(4):
                gti = 4 * tb + tt
                sc.wait_ge(f_st.h, gti + 1)
                f_sq.inc(sc.activation(out=s_sd[:, :], in_=s_mv[:, 1:2],
                                       func=AF.Sqrt), 1)

    def body_vector(v):
        for tb in range(4):
            for tt in range(4):
                gti = 4 * tb + tt
                v.wait_ge(f_fe.h, 32 * (tb + 1))
                v.wait_ge(f_lx[gti % 2].h, (gti // 2 + 1) * 16)
                if gti >= 1:
                    v.wait_ge(f_oo.h, 16 * gti)
                f_ad.inc(v.tensor_tensor(out=s_row[:, :],
                                         in0=s_ff[:, tt * 4096:(tt + 1) * 4096],
                                         in1=s_xo2[gti % 2][:, :], op=ALU.add), 1)
                v.wait_ge(f_ad.h, f_ad.tot)
                for ch in range(8):
                    _v = v.bn_stats(s_bns[:, ch * 6:(ch + 1) * 6],
                               s_row[:, ch * 512:(ch + 1) * 512])
                ctx.chain(v, _v)
                _v = v.bn_aggr(s_mv[:, :], s_bns[:, :])
                ctx.chain(v, _v)
                f_st.inc(v.tensor_scalar(out=s_mv[:, 1:2], in0=s_mv[:, 1:2], scalar1=EPS,
                                         scalar2=None, op0=ALU.add), 1)
                v.wait_ge(f_sq.h, gti + 1)
                _v = v.reciprocal(s_rs[:, :], s_sd[:, :])
                ctx.chain(v, _v)
                _v = v.tensor_scalar(out=s_nmr[:, :], in0=s_mv[:, 0:1], scalar1=s_rs[:, 0:1],
                                scalar2=-1.0, op0=ALU.mult, op1=ALU.mult)
                ctx.chain(v, _v)
                f_ln.inc(v.tensor_scalar(out=s_row[:, :], in0=s_row[:, :],
                                         scalar1=s_rs[:, 0:1], scalar2=s_nmr[:, 0:1],
                                         op0=ALU.mult, op1=ALU.add), 1)

    return {"sync": body_sync, "tensor": body_tensor, "scalar": body_scalar,
            "vector": body_vector, "gpsimd": body_gpsimd}



PHASES_ORDER = ["A", "B1", "B2", "B3", "C", "D"]


def build_nc(upto="D", dev_outputs=()):
    nc = bass.Bass("TRN2", target_bir_lowering=False)

    def scratch(name, shape, dtype):
        kind = "ExternalOutput" if name in dev_outputs else "Internal"
        return nc.dram_tensor(name, shape, dtype, kind=kind).ap()

    io = {}
    io["i_xqT"] = nc.dram_tensor("i_xqT", [H, TOWN], F32, kind="ExternalInput").ap()
    io["i_xo"] = nc.dram_tensor("i_xo", [TOWN, H], F32, kind="ExternalInput").ap()
    io["i_wqkv"] = nc.dram_tensor("i_wqkv", [H, 3 * D], F32, kind="ExternalInput").ap()
    io["i_bqT"] = nc.dram_tensor("i_bqT", [128, 16], F32, kind="ExternalInput").ap()
    io["i_bo"] = nc.dram_tensor("i_bo", [H], F32, kind="ExternalInput").ap()
    io["i_bvT"] = nc.dram_tensor("i_bvT", [128, 8], F32, kind="ExternalInput").ap()
    io["i_wo"] = nc.dram_tensor("i_wo", [D, H], F32, kind="ExternalInput").ap()
    io["i_pg"] = nc.dram_tensor("i_pg", [2048, NI], U8, kind="ExternalInput").ap()
    io["i_pu"] = nc.dram_tensor("i_pu", [2048, NI], U8, kind="ExternalInput").ap()
    io["i_pd"] = nc.dram_tensor("i_pd", [NI // 2, H], U8, kind="ExternalInput").ap()
    io["i_sbg"] = nc.dram_tensor("i_sbg", [64, NI], F32, kind="ExternalInput").ap()
    io["i_zbg"] = nc.dram_tensor("i_zbg", [64, NI], F32, kind="ExternalInput").ap()
    io["i_sbu"] = nc.dram_tensor("i_sbu", [64, NI], F32, kind="ExternalInput").ap()
    io["i_zbu"] = nc.dram_tensor("i_zbu", [64, NI], F32, kind="ExternalInput").ap()
    io["i_sbd"] = nc.dram_tensor("i_sbd", [172, H], F32, kind="ExternalInput").ap()
    io["i_zbd"] = nc.dram_tensor("i_zbd", [172, H], F32, kind="ExternalInput").ap()

    io["sc_qT"] = scratch("sc_qT", [D, TOWN], BF16)
    io["sc_kTo"] = scratch("sc_kTo", [D, TOWN], BF16)
    io["sc_kag"] = scratch("sc_kag", [4 * D, TOWN], BF16)
    io["sc_vo"] = scratch("sc_vo", [TOWN, D], BF16)
    io["sc_vag"] = scratch("sc_vag", [S, D], BF16)
    io["sc_p"] = scratch("sc_p", [TOWN, S], BF16)
    io["sc_xmid"] = scratch("sc_xmid", [TOWN, H], F32)
    io["sc_xmT"] = scratch("sc_xmT", [32, 128, TOWN], BF16)
    io["sc_wd"] = scratch("sc_wd", [32, 128, 86 * 128], BF16)
    io["sc_hT"] = scratch("sc_hT", [86, 128, TOWN], BF16)
    io["out"] = nc.dram_tensor("out", [TOWN, H], F32, kind="ExternalOutput").ap()

    with contextlib.ExitStack() as ges:
        ctx = Ctx(nc, ges)
        io["ident"] = ges.enter_context(nc.sbuf_tensor("g_ident", [128, 128], BF16))
        io["rl"] = ges.enter_context(nc.sbuf_tensor("g_rl", [128, 16], F32))
        ctx.id_sem = ctx.sem("g_id")
        ctx.cs_k = ctx.sem("cs_k")
        ctx.cs_v = ctx.sem("cs_v")

        def mk_ag(src_key, dst_key, sto_need, cs):
            def hook(g):
                g.wait_ge(ctx.a_sto[0].h, sto_need)
                g.wait_ge(ctx.a_sto[1].h, sto_need)
                cs.inc(g.collective_compute(
                    "AllGather", ALU.bypass, replica_groups=RG,
                    ins=[io[src_key].opt()], outs=[io[dst_key].opt()]), 1)
            return hook

        with contextlib.ExitStack() as ph:
            em_a = emit_A(ctx, io, ph)
            hooks = em_a["xhooks"]
            hooks.setdefault(24, []).append(mk_ag("sc_kTo", "sc_kag", 64, ctx.cs_k))
            hooks.setdefault(44, []).append(mk_ag("sc_vo", "sc_vag", 128, ctx.cs_v))
            em_p = emit_P0(ctx, io, ph, hooks)

            def gp(g):
                _g = g.memset(io["ident"][:], 0.0)
                ctx.chain(g, _g)
                _g = g.affine_select(out=io["ident"][:], in_=io["ident"][:],
                                compare_op=ALU.not_equal, fill=1.0, base=0,
                                pattern=[[-1, 128]], channel_multiplier=1)
                ctx.id_sem.inc(_g, 1)
                em_p["gpsimd"](g)

            run_phase(ctx, "ph_A", {
                "sync": em_a["sync"], "scalar": em_a["scalar"], "tensor": em_a["tensor"],
                "gpsimd": gp, "vector": em_p["vector"]})

        if upto == "A":
            return nc, io

        with contextlib.ExitStack() as ph:
            em = emit_B1(ctx, io, ph)
            run_phase(ctx, "ph_B1", em)
        if upto == "B1":
            return nc, io
        with contextlib.ExitStack() as oter:
            s_ot = oter.enter_context(nc.sbuf_tensor("g_ot", [128, 8 * 2048], BF16))
            with contextlib.ExitStack() as ph:
                em = emit_B2(ctx, io, ph, s_ot)
                run_phase(ctx, "ph_B2", em)
            if upto != "B2":
                with contextlib.ExitStack() as ph:
                    em = emit_B3(ctx, io, ph, s_ot)
                    run_phase(ctx, "ph_B3", em)
        if upto in ("B2", "B3"):
            return nc, io
        with contextlib.ExitStack() as ph:
            em = emit_C(ctx, io, ph)
            run_phase(ctx, "ph_C", em)
        if upto == "C":
            return nc, io
        with contextlib.ExitStack() as ph:
            em = emit_D(ctx, io, ph)
            run_phase(ctx, "ph_D", em)
    return nc, io


def _host_inputs(inputs):
    x = np.ascontiguousarray(np.asarray(inputs["L0_attn_x"], dtype=np.float32))
    wqkv = np.ascontiguousarray(np.asarray(inputs["L0_attn_Wqkv"], dtype=np.float32))
    bqkv = np.ascontiguousarray(np.asarray(inputs["L0_attn_bqkv"], dtype=np.float32))
    wo = np.ascontiguousarray(np.asarray(inputs["L0_attn_Wo"], dtype=np.float32))
    bo = np.ascontiguousarray(np.asarray(inputs["L0_attn_bo"], dtype=np.float32))

    perm = np.concatenate([np.arange(0, NI, 2), np.arange(1, NI, 2)])

    def pack_bytes(wq_i32):
        return np.asarray(wq_i32, dtype=np.int32).view(np.uint8)[:, 0::4]

    pg = np.ascontiguousarray(pack_bytes(inputs["L0_ffn_gate_Wq"])[perm].T)
    pu = np.ascontiguousarray(pack_bytes(inputs["L0_ffn_up_Wq"])[perm].T)
    pd = np.ascontiguousarray(pack_bytes(inputs["L0_ffn_down_Wq"]).T)

    def tp(a):
        return np.ascontiguousarray(np.asarray(a, dtype=np.float32).T)

    sbg = tp(np.asarray(inputs["L0_ffn_gate_scale"])[perm])
    zbg = tp(np.asarray(inputs["L0_ffn_gate_zero"])[perm])
    sbu = tp(np.asarray(inputs["L0_ffn_up_scale"])[perm])
    zbu = tp(np.asarray(inputs["L0_ffn_up_zero"])[perm])
    sbd = tp(np.asarray(inputs["L0_ffn_down_scale"]))
    zbd = tp(np.asarray(inputs["L0_ffn_down_zero"]))

    bqT = np.ascontiguousarray(bqkv[0:2048].reshape(16, 128).T)
    bvT = np.ascontiguousarray(bqkv[2048:3072].reshape(8, 128).T)

    in_maps = []
    for c in range(NCORE):
        b, r = c // 4, c % 4
        in_maps.append(dict(
            i_xqT=np.ascontiguousarray(x[b, r * TOWN:(r + 1) * TOWN].T),
            i_xo=np.ascontiguousarray(x[b, r * TOWN:(r + 1) * TOWN]),
            i_wqkv=wqkv, i_bqT=bqT, i_bo=bo, i_bvT=bvT, i_wo=wo,
            i_pg=pg, i_pu=pu, i_pd=pd,
            i_sbg=sbg, i_zbg=zbg, i_sbu=sbu, i_zbu=zbu, i_sbd=sbd, i_zbd=zbd,
        ))
    return in_maps


def kernel(**inputs):
    in_maps = _host_inputs(inputs)
    nc, io = build_nc()
    res = run_bass_kernel_spmd(nc, in_maps, core_ids=list(range(NCORE)))
    out = np.empty((B, S, H), dtype=np.float32)
    for c in range(NCORE):
        b, r = c // 4, c % 4
        out[b, r * TOWN:(r + 1) * TOWN] = res.results[c]["out"]
    return out


# version 21
# speedup vs baseline: 1.3384x; 1.0126x over previous
import contextlib
import numpy as np
import concourse.bass as bass
import concourse.mybir as mybir
from concourse.bass_utils import run_bass_kernel_spmd

F32 = mybir.dt.float32
BF16 = mybir.dt.bfloat16
U8 = mybir.dt.uint8
I32 = mybir.dt.int32
I16 = mybir.dt.int16
AF = mybir.ActivationFunctionType
ALU = mybir.AluOpType
AX = mybir.AxisListType

B, S, H, D, NI = 2, 8192, 4096, 1024, 11008
TOWN = 2048
NCORE = 8
SOFT_SCALE = 0.03125
EPS = 1e-5
DQ_SLABS_D = [(0, 2048), (2048, 2048)]
RG = [[0, 1, 2, 3], [4, 5, 6, 7]]


class SemT:
    def __init__(self, nc, es, name):
        self.h = es.enter_context(nc.semaphore(name))
        self.tot = 0

    def inc(self, instr, n=16):
        instr.then_inc(self.h, n)
        self.tot += n
        return instr


class Ctx:
    def __init__(self, nc, es):
        self.nc = nc
        self.es = es
        self.bar = SemT(nc, es, "bar")
        self.sems = {}
        self.chain_sems = {}

    def sem(self, name):
        s = SemT(self.nc, self.es, name)
        self.sems[name] = s
        return s

    def chain(self, eng, ins):
        key = ins.ins.engine if hasattr(ins, "ins") else None
        name = f"ch_{key}"
        if name not in self.chain_sems:
            self.chain_sems[name] = SemT(self.nc, self.es, name)
        s = self.chain_sems[name]
        s.inc(ins, 1)
        eng.wait_ge(s.h, s.tot)


ALL_ENGINES = ("tensor", "vector", "scalar", "gpsimd", "sync")


def run_phase(ctx, name, emitters):
    nc = ctx.nc
    base = ctx.bar.tot
    ctx.bar.tot += len(ALL_ENGINES)
    tgt = ctx.bar.tot

    with nc.Block(name) as blk:
        for ename in ALL_ENGINES:
            fn = emitters.get(ename)

            def body(eng, fn=fn):
                if fn is not None:
                    fn(eng)
                eng.drain().then_inc(ctx.bar.h, 1)
                eng.wait_ge(ctx.bar.h, tgt)
            getattr(blk, ename)(body)




def emit_P0(ctx, io, bufs_es, hooks=None):
    nc = ctx.nc
    E = bufs_es.enter_context
    W = 2048
    NBP = 4
    s_pi = [E(nc.sbuf_tensor(f"p_pi{i}", [128, W], I16)) for i in range(NBP)]
    s_sb = [E(nc.sbuf_tensor(f"p_sb{i}", [128, W], BF16)) for i in range(NBP)]
    s_zb = [E(nc.sbuf_tensor(f"p_zb{i}", [128, W], BF16)) for i in range(NBP)]
    s_th = E(nc.sbuf_tensor("p_th", [128, W], I16))
    s_tl = E(nc.sbuf_tensor("p_tl", [128, W], I16))
    NW = 3
    s_wh = [E(nc.sbuf_tensor(f"p_wh{i}", [128, W], BF16)) for i in range(NW)]
    s_wl = [E(nc.sbuf_tensor(f"p_wl{i}", [128, W], BF16)) for i in range(NW)]

    ld = [ctx.sem(f"p_ld{i}") for i in range(NBP)]
    dqd = ctx.sem("p_dqd")
    wo = ctx.sem("p_wo")

    jobs = [
        (io["i_pd"], io["i_sbd"], io["i_zbd"], io["sc_wd"], 43, 43, DQ_SLABS_D, True),
    ]
    slabs = []
    for (pk, sbv, zbv, outw, njt, lofs, slablist, is_down) in jobs:
        for jt in range(njt):
            for (c0, w) in slablist:
                slabs.append((pk, sbv, zbv, outw, jt, lofs, c0, w, is_down))

    def emit_dq(eng, si):
        (pk, sbv, zbv, outw, jt, lofs, c0, w, is_down) = slabs[si]
        b = si % NBP
        bw = si % NW
        eng.wait_ge(ld[b].h, (si // NBP + 1) * 48)
        if si >= NW:
            eng.wait_ge(wo.h, 32 * (si - (NW - 1)))
        eng.tensor_scalar(out=s_th[:, 0:w], in0=s_pi[b][:, 0:w], scalar1=4,
                          scalar2=None, op0=ALU.logical_shift_right)
        _v = eng.tensor_scalar(out=s_tl[:, 0:w], in0=s_pi[b][:, 0:w], scalar1=15,
                               scalar2=None, op0=ALU.bitwise_and)
        ctx.chain(eng, _v)
        eng.tensor_tensor(out=s_wh[bw][:, 0:w], in0=s_th[:, 0:w],
                          in1=s_zb[b][:, 0:w], op=ALU.subtract)
        _v = eng.tensor_tensor(out=s_wl[bw][:, 0:w], in0=s_tl[:, 0:w],
                               in1=s_zb[b][:, 0:w], op=ALU.subtract)
        ctx.chain(eng, _v)
        eng.tensor_tensor(out=s_wh[bw][:, 0:w], in0=s_wh[bw][:, 0:w],
                          in1=s_sb[b][:, 0:w], op=ALU.mult)
        dqd.inc(eng.tensor_tensor(out=s_wl[bw][:, 0:w], in0=s_wl[bw][:, 0:w],
                                  in1=s_sb[b][:, 0:w], op=ALU.mult), 1)

    def emit_wout(g, si):
        (pk, sbv, zbv, outw, jt, lofs, c0, w, is_down) = slabs[si]
        bw = si % NW
        nt, t0 = w // 128, c0 // 128
        g.wait_ge(dqd.h, si + 1)
        for src, rt in ((s_wh[bw], jt), (s_wl[bw], lofs + jt)):
            dst = outw[t0:t0 + nt, :, rt * 128:(rt + 1) * 128].rearrange("a p c -> p a c")
            wo.inc(g.dma_start(out=dst,
                               in_=src[:, 0:w].rearrange("p (a c) -> p a c", c=128)), 16)

    def body_gpsimd(g):
        for si, (pk, sbv, zbv, outw, jt, lofs, c0, w, is_down) in enumerate(slabs):
            for fn in (hooks or {}).get(si, []):
                fn(g)
            if si >= NBP:
                g.wait_ge(dqd.h, si - (NBP - 1))
            r0 = jt * 128
            g0 = jt * 4
            b = si % NBP
            ld[b].inc(g.dma_start(out=s_pi[b][:, 0:w], in_=pk[r0:r0 + 128, c0:c0 + w]), 16)
            ld[b].inc(g.dma_start(
                out=s_sb[b][:, 0:w],
                in_=sbv[g0:g0 + 4, c0:c0 + w].unsqueeze(1).broadcast_to([4, 32, w])), 16)
            ld[b].inc(g.dma_start(
                out=s_zb[b][:, 0:w],
                in_=zbv[g0:g0 + 4, c0:c0 + w].unsqueeze(1).broadcast_to([4, 32, w])), 16)
            if si >= 2:
                emit_wout(g, si - 2)
        emit_wout(g, len(slabs) - 2)
        emit_wout(g, len(slabs) - 1)
        g.wait_ge(wo.h, wo.tot)

    def body_vector(v):
        for si in range(len(slabs)):
            emit_dq(v, si)

    return {"gpsimd": body_gpsimd, "vector": body_vector}




def emit_A(ctx, io, bufs_es):
    nc = ctx.nc
    E = bufs_es.enter_context
    s_w = E(nc.sbuf_tensor("a_w", [128, 32 * 1024], BF16))
    s_wsl = [E(nc.sbuf_tensor(f"a_wsl{i}", [128, 2048], F32)) for i in range(2)]
    s_xb = [E(nc.sbuf_tensor(f"a_xb{i}", [128, 32 * 256], BF16)) for i in range(2)]
    s_st = [E(nc.sbuf_tensor(f"a_st{i}", [128, 2048], BF16)) for i in range(2)]
    s_bqT = E(nc.sbuf_tensor("a_bqT", [128, 16], F32))
    ps_mm = [E(nc.psum_tensor(f"a_ps{i}", [128, 2048], F32)) for i in range(2)]

    ld_b = ctx.sem("a_ldb")
    ld_w = [ctx.sem("a_ldw0"), ctx.sem("a_ldw1")]
    ld_x = ctx.sem("a_ldx")
    cv_w = ctx.sem("a_cvw")
    mm_d = ctx.sem("a_mmd")
    ev_d = ctx.sem("a_evd")
    st_o = [ctx.sem("a_sto0"), ctx.sem("a_sto1")]

    passes = [
        ("k", 1024, 8, io["i_xqT"], io["sc_kTo"], 8),
        ("v", 2048, 8, io["i_xqT"], None, None),
        ("q", 0, 8, io["i_xqT"], io["sc_qT"], 0),
    ]
    nsl_cum = [0, 8, 16, 24]
    ctx.a_sto = st_o

    def body_sync(sync):
        ld_b.inc(sync.dma_start(out=s_bqT[:, :], in_=io["i_bqT"]), 16)
        for pi, (pname, wofs, nsl, xsrc, outT, bcol) in enumerate(passes):
            for i in range(16):
                gsi = pi * 16 + i
                if gsi >= 2:
                    sync.wait_ge(cv_w.h, gsi - 1)
                ld_w[gsi % 2].inc(sync.dma_start(
                    out=s_wsl[gsi % 2][:, :].rearrange("p (a f) -> p a f", a=2),
                    in_=io["i_wqkv"].rearrange("(a p) f -> p a f", p=128)[
                        :, 2 * i:2 * i + 2, wofs:wofs + 1024]), 16)

    def body_scalar(sc):
        for pi, (pname, wofs, nsl, xsrc, outT, bcol) in enumerate(passes):
            if pi >= 1:
                sc.wait_ge(mm_d.h, nsl_cum[pi])
            for i in range(16):
                gsi = pi * 16 + i
                sc.wait_ge(ld_w[gsi % 2].h, (gsi // 2 + 1) * 16)
                cv_w.inc(sc.activation(out=s_w[:, i * 2048:(i + 1) * 2048],
                              in_=s_wsl[gsi % 2][:, :], func=AF.Copy), 1)
            def emit_evac(i):
                gxi = nsl_cum[pi] + i
                sc.wait_ge(mm_d.h, gxi + 1)
                if gxi >= 2:
                    sc.wait_ge(st_o[gxi % 2].h, (gxi // 2) * 16)
                if pname == "v":
                    ins = sc.activation(out=s_st[i % 2][:, :], in_=ps_mm[i % 2][:, :],
                                           func=AF.Copy)
                    ev_d.inc(ins, 1)
                else:
                    for dt in range(8):
                        ins = sc.activation(out=s_st[i % 2][:, dt * 256:(dt + 1) * 256],
                                      in_=ps_mm[i % 2][:, dt * 256:(dt + 1) * 256],
                                      func=AF.Identity,
                                      bias=s_bqT[:, bcol + dt: bcol + dt + 1])
                    ev_d.inc(ins, 1)
                sc.wait_ge(ev_d.h, ev_d.tot)
                c0 = i * 256
                if pname == "v":
                    st_o[gxi % 2].inc(sc.dma_start(
                        out=io["sc_vo"][c0:c0 + 256, :].rearrange("(a p) d -> p a d", p=128),
                        in_=s_st[i % 2][:, :].rearrange("p (a d) -> p a d", a=2)), 16)
                else:
                    st_o[gxi % 2].inc(sc.dma_start(
                        out=outT.rearrange("(a p) t -> p a t", p=128)[:, :, c0:c0 + 256],
                        in_=s_st[i % 2][:, :].rearrange("p (a t) -> p a t", a=8)), 16)

            for i in range(nsl):
                if i >= 1:
                    emit_evac(i - 1)
            emit_evac(nsl - 1)
        sc.wait_ge(st_o[0].h, st_o[0].tot)
        sc.wait_ge(st_o[1].h, st_o[1].tot)

    def body_tensor(t):
        t.wait_ge(ld_b.h, 16)
        for pi, (pname, wofs, nsl, xsrc, outT, bcol) in enumerate(passes):
            for i in range(nsl):
                gxi = nsl_cum[pi] + i
                t.wait_ge(cv_w.h, (pi + 1) * 16)
                t.wait_ge(ld_x.h, 16 * (gxi + 1))
                if gxi >= 2:
                    t.wait_ge(ev_d.h, gxi - 1)
                ps = ps_mm[i % 2]
                xb = s_xb[i % 2]
                if pname == "v":
                    for tsub in range(2):
                        for a in range(32):
                            for dc in range(2):
                                o0 = tsub * 1024 + dc * 512
                                _i = t.matmul(ps[:, o0:o0 + 512],
                                         xb[:, a * 256 + tsub * 128: a * 256 + tsub * 128 + 128],
                                         s_w[:, a * 1024 + dc * 512: a * 1024 + (dc + 1) * 512],
                                         start=(a == 0), stop=(a == 31))
                else:
                    for dt in range(8):
                        for a in range(32):
                            _i = t.matmul(ps[:, dt * 256:(dt + 1) * 256],
                                     s_w[:, a * 1024 + dt * 128: a * 1024 + dt * 128 + 128],
                                     xb[:, a * 256:(a + 1) * 256],
                                     start=(a == 0), stop=(a == 31))
                mm_d.inc(_i, 1)

    def mk_xload(gxi):
        pi, i = gxi // 8, gxi % 8
        xsrc = passes[pi][3]
        c0 = i * 256

        def f(g):
            if gxi >= 2:
                g.wait_ge(mm_d.h, gxi - 1)
            ld_x.inc(g.dma_start(
                out=s_xb[gxi % 2][:, :].rearrange("p (a t) -> p a t", t=256),
                in_=xsrc[:, c0:c0 + 256].rearrange("(a p) t -> p a t", p=128)), 16)
        return f

    xhooks = {}
    for gxi in range(24):
        xhooks.setdefault(min(2 * gxi, 84), []).append(mk_xload(gxi))

    return {"sync": body_sync, "scalar": body_scalar, "tensor": body_tensor,
            "xhooks": xhooks}





def emit_B1(ctx, io, bufs_es):
    nc = ctx.nc
    E = bufs_es.enter_context
    s_qt = [None, None]
    s_qt[0] = E(nc.sbuf_tensor("b_qt0", [128, 8 * 128], BF16))
    s_kT = E(nc.sbuf_tensor("b_kT", [128, 8 * 8192], BF16))
    s_qt[1] = E(nc.sbuf_tensor("b_qt1", [128, 8 * 128], BF16))
    s_sc = E(nc.sbuf_tensor("b_sc", [128, 8192], F32))
    s_pb = [E(nc.sbuf_tensor(f"b_pb{i}", [128, 8192], BF16)) for i in range(2)]
    s_mx = E(nc.sbuf_tensor("b_mx", [128, 16], F32))
    s_nb = E(nc.sbuf_tensor("b_nb", [128, 16], F32))
    s_l = E(nc.sbuf_tensor("b_l", [128, 16], F32))
    ps_s = [E(nc.psum_tensor(f"b_pss{i}", [128, 2048], F32)) for i in range(2)]

    b_lk = ctx.sem("b_lk")
    b_lq = [ctx.sem("b_lq0"), ctx.sem("b_lq1")]
    b_mm = ctx.sem("b_mm")
    b_ev = ctx.sem("b_ev")
    b_mx = ctx.sem("b_mx")
    b_ex = ctx.sem("b_ex")
    b_po = [ctx.sem("b_po0"), ctx.sem("b_po1")]
    b_rl = ctx.sem("b_rl")

    def body_sync(sync):
        sync.wait_ge(ctx.cs_k.h, 1)
        for r in range(4):
            b_lk.inc(sync.dma_start(
                out=s_kT[:, :].rearrange("p (a t) -> p a t", t=8192)[:, :, r * 2048:(r + 1) * 2048],
                in_=io["sc_kag"].rearrange("(r a p) t -> r p a t", r=4, p=128)[r]), 16)
        for qt in range(16):
            if qt >= 2:
                sync.wait_ge(b_mm.h, 4 * qt - 4)
            b_lq[qt % 2].inc(sync.dma_start(
                out=s_qt[qt % 2][:, :],
                in_=io["sc_qT"].rearrange("(a p) t -> p a t", p=128)[:, :, qt * 128:(qt + 1) * 128]), 16)

    def body_tensor(t):
        t.wait_ge(b_lk.h, 64)
        t.wait_ge(ctx.id_sem.h, 1)
        for qt in range(16):
            t.wait_ge(b_lq[qt % 2].h, (qt // 2 + 1) * 16)
            for su in range(4):
                gi = 4 * qt + su
                if gi >= 2:
                    t.wait_ge(b_ev.h, gi - 1)
                for dt in range(8):
                    for kc in range(4):
                        _i = t.matmul(ps_s[gi % 2][:, kc * 512:(kc + 1) * 512],
                                 s_qt[qt % 2][:, dt * 128:(dt + 1) * 128],
                                 s_kT[:, dt * 8192 + su * 2048 + kc * 512:
                                      dt * 8192 + su * 2048 + (kc + 1) * 512],
                                 start=(dt == 0), stop=(dt == 7))
                b_mm.inc(_i, 1)

    def body_scalar(sc):
        for qt in range(16):
            sc.wait_ge(b_mx.h, qt + 1)
            if qt >= 2:
                sc.wait_ge(b_po[qt % 2].h, (qt // 2) * 16)
            b_ex.inc(sc.activation(out=s_pb[qt % 2][:, :], in_=s_sc[:, :], func=AF.Exp,
                                   bias=s_nb[:, qt:qt + 1], scale=SOFT_SCALE,
                                   accum_out=s_l[:, qt:qt + 1]), 1)

    def body_vector(v):
        for qt in range(16):
            if qt >= 1:
                v.wait_ge(b_ex.h, qt)
            for su in range(4):
                gi = 4 * qt + su
                v.wait_ge(b_mm.h, gi + 1)
                _v = v.reduce_max(s_mx[:, 4 * (qt % 2) + su:4 * (qt % 2) + su + 1],
                                  ps_s[gi % 2][:, :], axis=AX.X)
                if su == 3:
                    ctx.chain(v, _v)
                b_ev.inc(v.tensor_copy(out=s_sc[:, su * 2048:(su + 1) * 2048],
                                       in_=ps_s[gi % 2][:, :]), 1)
            _v = v.reduce_max(s_nb[:, qt:qt + 1],
                              s_mx[:, 4 * (qt % 2):4 * (qt % 2) + 4], axis=AX.X)
            ctx.chain(v, _v)
            b_mx.inc(v.tensor_scalar(out=s_nb[:, qt:qt + 1], in0=s_nb[:, qt:qt + 1],
                                     scalar1=-SOFT_SCALE, scalar2=None, op0=ALU.mult), 1)
            v.wait_ge(b_ex.h, qt + 1)
            b_rl.inc(v.reciprocal(io["rl"][:, qt:qt + 1], s_l[:, qt:qt + 1]), 1)

    def body_gpsimd(g):
        for qt in range(16):
            g.wait_ge(b_ex.h, qt + 1)
            b_po[qt % 2].inc(g.dma_start(
                out=io["sc_p"][qt * 128:(qt + 1) * 128, :], in_=s_pb[qt % 2][:, :]), 16)
        g.wait_ge(b_po[0].h, b_po[0].tot)
        g.wait_ge(b_po[1].h, b_po[1].tot)

    return {"sync": body_sync, "tensor": body_tensor, "scalar": body_scalar,
            "vector": body_vector, "gpsimd": body_gpsimd}




def emit_B2(ctx, io, bufs_es, s_ot):
    nc = ctx.nc
    E = bufs_es.enter_context
    s_pl = [None, None]
    s_pl[0] = E(nc.sbuf_tensor("c_pl0", [128, 4096], BF16))
    s_v = E(nc.sbuf_tensor("c_v", [128, 64 * 1024], BF16))
    s_pl[1] = E(nc.sbuf_tensor("c_pl1", [128, 4096], BF16))
    s_o = E(nc.sbuf_tensor("c_o", [128, 1024], BF16))
    s_ptl = E(nc.sbuf_tensor("c_ptl", [128, 64 * 128], BF16))
    ps_o = [E(nc.psum_tensor(f"c_pso{i}", [128, 1024], F32)) for i in range(2)]
    ps_t = [E(nc.psum_tensor(f"c_pst{i}", [128, 1024], BF16)) for i in range(2)]
    ps_ot = E(nc.psum_tensor("c_psot", [128, 1024], BF16))

    c_lv = ctx.sem("c_lv")
    c_lp = [ctx.sem("c_lp0"), ctx.sem("c_lp1")]
    c_tp = ctx.sem("c_tp")
    c_pe = ctx.sem("c_pe")
    c_mm = ctx.sem("c_mm")
    c_ev = ctx.sem("c_ev")
    c_tpo = ctx.sem("c_tpo")
    c_oe = ctx.sem("c_oe")

    def body_sync(sync):
        sync.wait_ge(ctx.cs_v.h, 1)
        c_lv.inc(sync.dma_start(out=s_v[:, :],
                                in_=io["sc_vag"].rearrange("(a p) d -> p a d", p=128)), 16)
        for hh in range(32):
            if hh >= 2:
                sync.wait_ge(c_tp.h, 4 * hh - 4)
            c_lp[hh % 2].inc(sync.dma_start(
                out=s_pl[hh % 2][:, :],
                in_=io["sc_p"][(hh // 2) * 128:(hh // 2 + 1) * 128,
                               (hh % 2) * 4096:(hh % 2 + 1) * 4096]), 16)

    def body_tensor(t):
        t.wait_ge(c_lv.h, 16)
        for qt in range(16):
            if qt >= 1:
                t.wait_ge(c_mm.h, qt)
            for grp in range(8):
                gg = 8 * qt + grp
                hh = 2 * qt + grp // 4
                t.wait_ge(c_lp[hh % 2].h, (hh // 2 + 1) * 16)
                if gg >= 2:
                    t.wait_ge(c_pe.h, gg - 1)
                for j in range(8):
                    _i = t.transpose(ps_t[gg % 2][:, j * 128:(j + 1) * 128],
                                s_pl[hh % 2][:, ((grp % 4) * 8 + j) * 128:
                                             ((grp % 4) * 8 + j + 1) * 128],
                                io["ident"][:])
                c_tp.inc(_i, 1)
            t.wait_ge(c_pe.h, 8 * qt + 8)
            if qt >= 2:
                t.wait_ge(c_ev.h, qt - 1)
            for kt in range(64):
                for dc in range(2):
                    _i = t.matmul(ps_o[qt % 2][:, dc * 512:(dc + 1) * 512],
                             s_ptl[:, kt * 128:(kt + 1) * 128],
                             s_v[:, kt * 1024 + dc * 512: kt * 1024 + (dc + 1) * 512],
                             start=(kt == 0), stop=(kt == 63))
            c_mm.inc(_i, 1)
            t.wait_ge(c_ev.h, qt + 1)
            t.wait_ge(c_oe.h, qt)
            for j in range(8):
                _i = t.transpose(ps_ot[:, j * 128:(j + 1) * 128],
                            s_o[:, j * 128:(j + 1) * 128], io["ident"][:])
            c_tpo.inc(_i, 1)

    def body_scalar(sc):
        for qt in range(16):
            for grp in range(8):
                gg = 8 * qt + grp
                sc.wait_ge(c_tp.h, gg + 1)
                c_pe.inc(sc.activation(out=s_ptl[:, grp * 1024:(grp + 1) * 1024],
                                       in_=ps_t[gg % 2][:, :], func=AF.Copy), 1)
            sc.wait_ge(c_mm.h, qt + 1)
            if qt >= 1:
                sc.wait_ge(c_tpo.h, qt)
            c_ev.inc(sc.activation(out=s_o[:, :], in_=ps_o[qt % 2][:, :],
                                   func=AF.Identity, bias=0.0,
                                   scale=io["rl"][:, qt:qt + 1]), 1)
            sc.wait_ge(c_tpo.h, qt + 1)
            c_oe.inc(sc.activation(
                out=s_ot[:, :].rearrange("p (a q) -> p a q", q=2048)[:, :, qt * 128:(qt + 1) * 128],
                in_=ps_ot[:, :], func=AF.Copy), 1)

    return {"sync": body_sync, "tensor": body_tensor, "scalar": body_scalar}




def emit_B3(ctx, io, bufs_es, s_ot):
    nc = ctx.nc
    E = bufs_es.enter_context
    s_wo = E(nc.sbuf_tensor("d_wo", [128, 8 * 4096], BF16))
    s_wsl = [E(nc.sbuf_tensor(f"d_wsl{i}", [128, 1024], F32)) for i in range(2)]
    s_bo128 = E(nc.sbuf_tensor("d_bo128", [128, 4096], F32))
    s_bvT = E(nc.sbuf_tensor("d_bvT", [128, 8], BF16))
    s_ones = E(nc.sbuf_tensor("d_ones", [1, 128], BF16))
    s_xo = [E(nc.sbuf_tensor(f"d_xo{i}", [128, 4096], F32)) for i in range(2)]
    s_pr = E(nc.sbuf_tensor("d_pr", [128, 4096], F32))
    s_xe = E(nc.sbuf_tensor("d_xe", [128, 2048], BF16))
    s_xoo = E(nc.sbuf_tensor("d_xoo", [128, 2048], BF16))
    s_xms = E(nc.sbuf_tensor("d_xms", [128, 32 * 128], BF16))
    s_bor = s_pr[0:1, 0:4096]
    s_bob = s_xms[0:1, 0:4096]
    s_bx = s_xo
    s_xm = s_pr
    s_bns = E(nc.sbuf_tensor("d_bns", [128, 48], F32))
    s_mv = E(nc.sbuf_tensor("d_mv", [128, 2], F32))
    s_sd = E(nc.sbuf_tensor("d_sd", [128, 1], F32))
    s_rs = E(nc.sbuf_tensor("d_rs", [128, 1], F32))
    s_nmr = E(nc.sbuf_tensor("d_nmr", [128, 1], F32))
    ps_p = E(nc.psum_tensor("d_psp", [128, 2048], F32))
    ps_xt = [E(nc.psum_tensor(f"d_psxt{i}", [128, 512], BF16)) for i in range(2)]

    d_lb = ctx.sem("d_lb")
    d_lv = ctx.sem("d_lv")
    d_lw = [ctx.sem("d_lw0"), ctx.sem("d_lw1")]
    d_cw = ctx.sem("d_cw")
    d_on = ctx.sem("d_on")
    d_bm = ctx.sem("d_bm")
    d_bv = ctx.sem("d_bv")
    d_bc = ctx.sem("d_bc")
    d_bb = ctx.sem("d_bb")
    d_be = ctx.sem("d_be")
    d_lx = [ctx.sem("d_lx0"), ctx.sem("d_lx1")]
    d_bx = ctx.sem("d_bx")
    d_mm = ctx.sem("d_mm")
    d_pa = ctx.sem("d_pa")
    d_st = ctx.sem("d_st")
    d_sq = ctx.sem("d_sq")
    d_ln = ctx.sem("d_ln")
    d_xw = ctx.sem("d_xw")
    d_sp = ctx.sem("d_sp")
    d_tp = ctx.sem("d_tp")
    d_se = ctx.sem("d_se")
    d_so = ctx.sem("d_so")

    def body_sync(sync):
        d_lb.inc(sync.dma_start(out=s_bor[:, :], in_=io["i_bo"][None, :]), 16)
        for i in range(32):
            if i >= 2:
                sync.wait_ge(d_cw.h, i - 1)
            d_lw[i % 2].inc(sync.dma_start(
                out=s_wsl[i % 2][:, :],
                in_=io["i_wo"].rearrange("(a p) f -> a p f", p=128)[
                    i // 4, :, (i % 4) * 1024:(i % 4 + 1) * 1024]), 16)
        for tt in range(16):
            if tt >= 2:
                sync.wait_ge(d_bx.h, tt - 1)
            d_lx[tt % 2].inc(sync.dma_start(
                out=s_xo[tt % 2][:, :],
                in_=io["i_xo"][tt * 128:(tt + 1) * 128, :]), 16)
            sync.wait_ge(d_ln.h, tt + 1)
            d_xw.inc(sync.dma_start(
                out=io["sc_xmid"][tt * 128:(tt + 1) * 128, :], in_=s_pr[:, :]), 16)
        sync.wait_ge(d_xw.h, d_xw.tot)

    def body_scalar(sc):
        for i in range(32):
            sc.wait_ge(d_lw[i % 2].h, (i // 2 + 1) * 16)
            d_cw.inc(sc.activation(out=s_wo[:, i * 1024:(i + 1) * 1024],
                                   in_=s_wsl[i % 2][:, :], func=AF.Copy), 1)
        sc.wait_ge(d_bv.h, 2)
        d_bc.inc(sc.activation(out=s_bob[:, :], in_=s_bo128[0:1, :], func=AF.Copy), 1)
        for r in range(2):
            sc.wait_ge(d_bb.h, r + 1)
            d_be.inc(sc.activation(out=s_bo128[:, r * 2048:(r + 1) * 2048],
                                   in_=ps_p[:, :], func=AF.Copy), 1)
        for tt in range(16):
            sc.wait_ge(d_st.h, tt + 1)
            d_sq.inc(sc.activation(out=s_sd[:, :], in_=s_mv[:, 1:2],
                                   func=AF.Sqrt), 1)
            for grp in range(8):
                gg = 8 * tt + grp
                sc.wait_ge(d_tp.h, gg + 1)
                if grp == 0:
                    sc.wait_ge(d_so.h, 16 * tt)
                d_se.inc(sc.activation(out=s_xms[:, grp * 512:(grp + 1) * 512],
                                       in_=ps_xt[grp % 2][:, :], func=AF.Copy), 1)

    def body_tensor(t):
        t.wait_ge(d_cw.h, 32)
        t.wait_ge(d_lb.h, 16)
        t.wait_ge(d_lv.h, 16)
        for half in range(2):
            if half == 1:
                t.wait_ge(d_bv.h, 1)
            for cc in range(4):
                for dt in range(8):
                    _i = t.matmul(ps_p[0:1, cc * 512:(cc + 1) * 512],
                             s_bvT[:, dt:dt + 1],
                             s_wo[:, dt * 4096 + half * 2048 + cc * 512:
                                  dt * 4096 + half * 2048 + (cc + 1) * 512],
                             start=(dt == 0), stop=(dt == 7))
            d_bm.inc(_i, 1)
        t.wait_ge(d_bc.h, 1)
        t.wait_ge(d_on.h, 1)
        t.wait_ge(d_bv.h, 2)
        for r in range(2):
            if r == 1:
                t.wait_ge(d_be.h, 1)
            for cc in range(4):
                _i = t.matmul(ps_p[:, cc * 512:(cc + 1) * 512], s_ones[0:1, :],
                         s_bob[0:1, r * 2048 + cc * 512: r * 2048 + (cc + 1) * 512],
                         start=True, stop=True)
            d_bb.inc(_i, 1)
        for tt in range(16):
            for half in range(2):
                gi = 2 * tt + half
                t.wait_ge(d_be.h, 2) if gi == 0 else None
                if gi >= 1:
                    t.wait_ge(d_pa.h, gi)
                for dt in range(8):
                    for cc in range(4):
                        _i = t.matmul(ps_p[:, cc * 512:(cc + 1) * 512],
                                 s_ot[:, dt * 2048 + tt * 128: dt * 2048 + (tt + 1) * 128],
                                 s_wo[:, dt * 4096 + half * 2048 + cc * 512:
                                      dt * 4096 + half * 2048 + (cc + 1) * 512],
                                 start=(dt == 0), stop=(dt == 7))
                d_mm.inc(_i, 1)
            t.wait_ge(d_sp.h, tt + 1)
            for half in range(2):
                src = s_xe if half == 0 else s_xoo
                for grp in range(4):
                    gg = 8 * tt + half * 4 + grp
                    if gg >= 2:
                        t.wait_ge(d_se.h, gg - 1)
                    for j in range(4):
                        _i = t.transpose(ps_xt[gg % 2][:, j * 128:(j + 1) * 128],
                                    src[:, (grp * 4 + j) * 128:(grp * 4 + j + 1) * 128],
                                    io["ident"][:])
                    d_tp.inc(_i, 1)

    def body_vector(v):
        for half in range(2):
            v.wait_ge(d_bm.h, half + 1)
            v.wait_ge(d_lb.h, 16)
            d_bv.inc(v.tensor_tensor(out=s_bo128[0:1, half * 2048:(half + 1) * 2048],
                                     in0=ps_p[0:1, 0:2048],
                                     in1=s_bor[0:1, half * 2048:(half + 1) * 2048],
                                     op=ALU.add), 1)
        for tt in range(16):
            v.wait_ge(d_lx[tt % 2].h, (tt // 2 + 1) * 16)
            v.wait_ge(d_be.h, 2)
            if tt >= 1:
                v.wait_ge(d_pa.h, 2 * tt)
            d_bx.inc(v.tensor_tensor(out=s_xo[tt % 2][:, :], in0=s_bo128[:, :],
                                     in1=s_xo[tt % 2][:, :], op=ALU.add), 1)
            v.wait_ge(d_bx.h, d_bx.tot)
            for half in range(2):
                gi = 2 * tt + half
                v.wait_ge(d_mm.h, gi + 1)
                if half == 0 and tt >= 1:
                    v.wait_ge(d_xw.h, 16 * tt)
                d_pa.inc(v.tensor_tensor(out=s_pr[:, half * 2048:(half + 1) * 2048],
                                         in0=ps_p[:, :],
                                         in1=s_xo[tt % 2][:, half * 2048:(half + 1) * 2048],
                                         op=ALU.add), 1)
            v.wait_ge(d_pa.h, d_pa.tot)
            for ch in range(8):
                _v = v.bn_stats(s_bns[:, ch * 6:(ch + 1) * 6], s_pr[:, ch * 512:(ch + 1) * 512])
            ctx.chain(v, _v)
            _v = v.bn_aggr(s_mv[:, :], s_bns[:, :])
            ctx.chain(v, _v)
            d_st.inc(v.tensor_scalar(out=s_mv[:, 1:2], in0=s_mv[:, 1:2], scalar1=EPS,
                                     scalar2=None, op0=ALU.add), 1)
            v.wait_ge(d_sq.h, tt + 1)
            _v = v.reciprocal(s_rs[:, :], s_sd[:, :])
            ctx.chain(v, _v)
            _v = v.tensor_scalar(out=s_nmr[:, :], in0=s_mv[:, 0:1], scalar1=s_rs[:, 0:1],
                            scalar2=-1.0, op0=ALU.mult, op1=ALU.mult)
            ctx.chain(v, _v)
            d_ln.inc(v.tensor_scalar(out=s_pr[:, :], in0=s_pr[:, :], scalar1=s_rs[:, 0:1],
                                     scalar2=s_nmr[:, 0:1], op0=ALU.mult, op1=ALU.add), 1)
            v.wait_ge(d_ln.h, d_ln.tot)
            if tt >= 1:
                v.wait_ge(d_tp.h, 8 * tt)
            v.tensor_copy(out=s_xe[:, :], in_=s_pr[:, 0:4096:2])
            d_sp.inc(v.tensor_copy(out=s_xoo[:, :], in_=s_pr[:, 1:4096:2]), 1)

    def body_gpsimd(g):
        d_lv.inc(g.dma_start(out=s_bvT[:, :], in_=io["i_bvT"]), 16)
        d_on.inc(g.memset(s_ones[:, :], 1.0), 1)
        for tt in range(16):
            g.wait_ge(d_se.h, 8 * tt + 8)
            d_so.inc(g.dma_start(
                out=io["sc_xmT"][:, :, tt * 128:(tt + 1) * 128].rearrange("a p t -> p a t"),
                in_=s_xms[:, :].rearrange("p (a t) -> p a t", t=128)), 16)
        g.wait_ge(d_so.h, d_so.tot)

    return {"sync": body_sync, "scalar": body_scalar, "tensor": body_tensor,
            "vector": body_vector, "gpsimd": body_gpsimd}




def emit_C(ctx, io, bufs_es):
    nc = ctx.nc
    E = bufs_es.enter_context
    NB = 4
    s_w00 = E(nc.sbuf_tensor("e_w00", [128, 32 * 128], BF16))
    s_w10 = E(nc.sbuf_tensor("e_w10", [128, 32 * 128], BF16))
    s_xm = E(nc.sbuf_tensor("e_xm", [128, 32 * 1024], BF16))
    s_w01 = E(nc.sbuf_tensor("e_w01", [128, 32 * 128], BF16))
    s_w11 = E(nc.sbuf_tensor("e_w11", [128, 32 * 128], BF16))
    s_wt = [[s_w00, s_w01], [s_w10, s_w11]]
    s_pi = [E(nc.sbuf_tensor(f"e_pi{i}", [128, 2048], I16)) for i in range(NB)]
    s_th = E(nc.sbuf_tensor("e_th", [128, 2048], I16))
    s_tl = E(nc.sbuf_tensor("e_tl", [128, 2048], I16))
    s_sb = [E(nc.sbuf_tensor(f"e_sb{i}", [128, 2048], BF16)) for i in range(NB)]
    s_zb = [E(nc.sbuf_tensor(f"e_zb{i}", [128, 2048], BF16)) for i in range(NB)]
    s_sg = E(nc.sbuf_tensor("e_sg", [128, 1024], F32))
    s_ht = [E(nc.sbuf_tensor(f"e_ht{i}", [128, 1024], BF16)) for i in range(2)]
    ps_g = [E(nc.psum_tensor(f"e_psg{i}", [128, 1024], F32)) for i in range(2)]
    ps_u = [E(nc.psum_tensor(f"e_psu{i}", [128, 1024], F32)) for i in range(2)]

    e_lx = ctx.sem("e_lx")
    e_ld = [ctx.sem(f"e_ld{i}") for i in range(NB)]
    e_dq = ctx.sem("e_dq")
    e_mm = ctx.sem("e_mm")
    e_sl = ctx.sem("e_sl")
    e_hv = ctx.sem("e_hv")
    e_ho = ctx.sem("e_ho")

    MATIN = [(io["i_pg"], io["i_sbg"], io["i_zbg"]),
             (io["i_pu"], io["i_sbu"], io["i_zbu"])]

    def body_gpsimd2(g):
        for dqg in range(344):
            it, mat = (dqg % 172) // 2, dqg % 2
            b = dqg % NB
            if dqg >= NB:
                g.wait_ge(e_dq.h, dqg - (NB - 1))
            pk, sbv, zbv = MATIN[mat]
            e_ld[b].inc(g.dma_start(
                out=s_pi[b][:, :],
                in_=pk.rearrange("(a p) n -> p a n", p=128)[:, :, it * 128:(it + 1) * 128]), 16)
            for src, dst in ((sbv, s_sb[b]), (zbv, s_zb[b])):
                for g0 in range(4):
                    e_ld[b].inc(g.dma_start(
                        out=dst[g0 * 32:(g0 + 1) * 32, :],
                        in_=src.rearrange("(j g) n -> g j n", g=4)[g0, :, it * 128:(it + 1) * 128]
                            .unsqueeze(0).broadcast_to([32, 16, 128])), 16)

    def body_sync(sync):
        for pss in range(2):
            if pss == 1:
                sync.wait_ge(e_mm.h, 172)
            e_lx.inc(sync.dma_start(
                out=s_xm[:, :],
                in_=io["sc_xmT"][:, :, pss * 1024:(pss + 1) * 1024]
                    .rearrange("a p t -> p a t")), 16)
            for it in range(86):
                git = pss * 86 + it
                sync.wait_ge(e_hv.h, git + 1)
                e_ho.inc(sync.dma_start(
                    out=io["sc_hT"][it][:, pss * 1024:(pss + 1) * 1024],
                    in_=s_ht[git % 2][:, :]), 16)
        sync.wait_ge(e_ho.h, e_ho.tot)

    def body_vector(v):
        for pss in range(2):
            for it in range(86):
                git = pss * 86 + it
                for mat in range(2):
                    dqg = 2 * git + mat
                    b = dqg % NB
                    wb = git % 2
                    v.wait_ge(e_ld[b].h, (dqg // NB + 1) * 144)
                    if dqg >= 4:
                        v.wait_ge(e_mm.h, dqg - 3)
                    hi = s_wt[mat][wb][:, 0:2048]
                    lo = s_wt[mat][wb][:, 2048:4096]
                    v.tensor_scalar(out=s_th[:, :], in0=s_pi[b][:, :], scalar1=4,
                                    scalar2=None, op0=ALU.logical_shift_right)
                    _v = v.tensor_scalar(out=s_tl[:, :], in0=s_pi[b][:, :], scalar1=15,
                                         scalar2=None, op0=ALU.bitwise_and)
                    ctx.chain(v, _v)
                    v.tensor_tensor(out=hi, in0=s_th[:, :], in1=s_zb[b][:, :],
                                    op=ALU.subtract)
                    _v = v.tensor_tensor(out=lo, in0=s_tl[:, :], in1=s_zb[b][:, :],
                                         op=ALU.subtract)
                    ctx.chain(v, _v)
                    v.tensor_tensor(out=hi, in0=hi, in1=s_sb[b][:, :], op=ALU.mult)
                    e_dq.inc(v.tensor_tensor(out=lo, in0=lo, in1=s_sb[b][:, :],
                                             op=ALU.mult), 1)
                if git >= 1:
                    emit_hmult(v, git - 1)
            if pss == 1:
                emit_hmult(v, 171)

    def emit_hmult(v, git):
        v.wait_ge(e_mm.h, 2 * git + 2)
        v.wait_ge(e_sl.h, git + 1)
        if git >= 2:
            v.wait_ge(e_ho.h, 16 * (git - 1))
        e_hv.inc(v.tensor_tensor(out=s_ht[git % 2][:, :], in0=s_sg[:, :],
                                 in1=ps_u[git % 2][:, :], op=ALU.mult), 1)

    def body_tensor(t):
        for pss in range(2):
            t.wait_ge(e_lx.h, 16 * (pss + 1))
            for it in range(86):
                git = pss * 86 + it
                wb = git % 2
                for mat in range(2):
                    dqg = 2 * git + mat
                    t.wait_ge(e_dq.h, dqg + 1)
                    if git >= 2:
                        t.wait_ge((e_sl if mat == 0 else e_hv).h, git - 1)
                    ps = (ps_g if mat == 0 else ps_u)[git % 2]
                    for rt in range(32):
                        for tc in range(2):
                            _i = t.matmul(ps[:, tc * 512:(tc + 1) * 512],
                                     s_wt[mat][wb][:, rt * 128:(rt + 1) * 128],
                                     s_xm[:, rt * 1024 + tc * 512:
                                          rt * 1024 + (tc + 1) * 512],
                                     start=(rt == 0), stop=(rt == 31))
                    e_mm.inc(_i, 1)

    def body_scalar(sc):
        for git in range(172):
            sc.wait_ge(e_mm.h, 2 * git + 1)
            if git >= 1:
                sc.wait_ge(e_hv.h, git)
            e_sl.inc(sc.activation(out=s_sg[:, :], in_=ps_g[git % 2][:, :],
                                   func=AF.Silu), 1)

    return {"sync": body_sync, "tensor": body_tensor, "scalar": body_scalar,
            "vector": body_vector, "gpsimd": body_gpsimd2}




def emit_D(ctx, io, bufs_es):
    nc = ctx.nc
    E = bufs_es.enter_context
    s_hq = E(nc.sbuf_tensor("f_hq", [128, 86 * 512], BF16))
    s_wdh = [None] * 4
    s_wdh[0] = E(nc.sbuf_tensor("f_wd0", [128, 43 * 128], BF16))
    s_xo2 = [E(nc.sbuf_tensor("f_xo0", [128, 4096], F32))] * 2
    s_wdh[1] = E(nc.sbuf_tensor("f_wd1", [128, 43 * 128], BF16))
    s_row = E(nc.sbuf_tensor("f_row", [128, 4096], F32))
    s_wdh[2] = E(nc.sbuf_tensor("f_wd2", [128, 43 * 128], BF16))
    s_ff = E(nc.sbuf_tensor("f_ff", [128, 4 * 4096], BF16))
    s_wdh[3] = E(nc.sbuf_tensor("f_wd3", [128, 43 * 128], BF16))
    s_bns = E(nc.sbuf_tensor("f_bns", [128, 48], F32))
    s_mv = E(nc.sbuf_tensor("f_mv", [128, 2], F32))
    s_sd = E(nc.sbuf_tensor("f_sd", [128, 1], F32))
    s_rs = E(nc.sbuf_tensor("f_rs", [128, 1], F32))
    s_nmr = E(nc.sbuf_tensor("f_nmr", [128, 1], F32))
    s_tmp = [E(nc.sbuf_tensor(f"f_tmp{i}", [128, 512], BF16)) for i in range(2)]
    ps_d = [E(nc.psum_tensor(f"f_psd{i}", [128, 512], F32)) for i in range(2)]
    ps_t2 = [E(nc.psum_tensor(f"f_pst{i}", [128, 512], BF16)) for i in range(2)]

    f_lh = ctx.sem("f_lh")
    f_lw = [ctx.sem(f"f_lw{i}") for i in range(4)]
    f_mm = ctx.sem("f_mm")
    f_ev = ctx.sem("f_ev")
    f_tp = ctx.sem("f_tp")
    f_fe = ctx.sem("f_fe")
    f_lx = [ctx.sem("f_lx0"), ctx.sem("f_lx1")]
    f_ad = ctx.sem("f_ad")
    f_st = ctx.sem("f_st")
    f_sq = ctx.sem("f_sq")
    f_ln = ctx.sem("f_ln")
    f_oo = ctx.sem("f_oo")

    def emit_wload(sync, hgi):
        ht, half = (hgi % 64) // 2, hgi % 2
        if hgi >= 4:
            sync.wait_ge(f_mm.h, hgi - 3)
        f_lw[hgi % 4].inc(sync.dma_start(
            out=s_wdh[hgi % 4][:, :],
            in_=io["sc_wd"][ht][:, half * 5504:(half + 1) * 5504]), 16)

    def body_sync(sync):
        for tb in range(4):
            if tb == 0:
                for hgi in range(3):
                    emit_wload(sync, hgi)
            if tb >= 1:
                sync.wait_ge(f_mm.h, 64 * tb)
            for hf in range(2):
                for aq in range(2):
                    r0, r1 = (0, 21) if aq == 0 else (21, 43)
                    f_lh.inc(sync.dma_start(
                        out=s_hq[:, (hf * 43 + r0) * 512:(hf * 43 + r1) * 512]
                            .rearrange("p (a t) -> p a t", t=512),
                        in_=io["sc_hT"][hf * 43 + r0:hf * 43 + r1, :, tb * 512:(tb + 1) * 512]
                            .rearrange("a p t -> p a t")), 16)
            for hgi in range(64 * tb + 3, 64 * (tb + 1) + (3 if tb < 3 else 0)):
                emit_wload(sync, hgi)

    def body_gpsimd(g):
        for tb in range(4):
            for tt in range(4):
                gti = 4 * tb + tt
                if gti >= 1:
                    g.wait_ge(f_ad.h, gti)
                f_lx[gti % 2].inc(g.dma_start(
                    out=s_xo2[gti % 2][:, :],
                    in_=io["sc_xmid"][gti * 128:(gti + 1) * 128, :]), 16)
                g.wait_ge(f_ln.h, gti + 1)
                f_oo.inc(g.dma_start(
                    out=io["out"][gti * 128:(gti + 1) * 128, :], in_=s_row[:, :]), 16)
        g.wait_ge(f_oo.h, f_oo.tot)

    def emit_tr(t, ghi):
        t.wait_ge(f_ev.h, ghi + 1)
        if ghi >= 2:
            t.wait_ge(f_fe.h, ghi - 1)
        for j in range(4):
            _i = t.transpose(ps_t2[ghi % 2][:, j * 128:(j + 1) * 128],
                        s_tmp[ghi % 2][:, j * 128:(j + 1) * 128], io["ident"][:])
        f_tp.inc(_i, 1)

    def body_tensor(t):
        for tb in range(4):
            t.wait_ge(f_lh.h, 64 * (tb + 1))
            for ht in range(32):
                ghi = 32 * tb + ht
                for half in range(2):
                    hgi = 2 * ghi + half
                    t.wait_ge(f_lw[hgi % 4].h, (hgi // 4 + 1) * 16)
                    if half == 0 and ghi >= 2:
                        t.wait_ge(f_ev.h, ghi - 1)
                    for rt in range(43):
                        _i = t.matmul(ps_d[ghi % 2][:, :],
                                 s_wdh[hgi % 4][:, rt * 128:(rt + 1) * 128],
                                 s_hq[:, (half * 43 + rt) * 512:(half * 43 + rt + 1) * 512],
                                 start=(half == 0 and rt == 0), stop=(half == 1 and rt == 42))
                    f_mm.inc(_i, 1)
                if ghi >= 1:
                    emit_tr(t, ghi - 1)
        emit_tr(t, 127)

    def body_scalar(sc):
        for tb in range(4):
            for ht in range(32):
                ghi = 32 * tb + ht
                sc.wait_ge(f_mm.h, 2 * ghi + 2)
                if ghi >= 2:
                    sc.wait_ge(f_tp.h, ghi - 1)
                f_ev.inc(sc.activation(out=s_tmp[ghi % 2][:, :], in_=ps_d[ghi % 2][:, :],
                                       func=AF.Copy), 1)
                sc.wait_ge(f_tp.h, ghi + 1)
                if ht == 0:
                    sc.wait_ge(f_ad.h, 4 * tb)
                f_fe.inc(sc.activation(
                    out=s_ff[:, :].rearrange("p (a c) -> p a c", c=4096)[:, :, ht * 128:(ht + 1) * 128],
                    in_=ps_t2[ghi % 2][:, :], func=AF.Copy), 1)
            for tt in range(4):
                gti = 4 * tb + tt
                sc.wait_ge(f_st.h, gti + 1)
                f_sq.inc(sc.activation(out=s_sd[:, :], in_=s_mv[:, 1:2],
                                       func=AF.Sqrt), 1)

    def body_vector(v):
        for tb in range(4):
            for tt in range(4):
                gti = 4 * tb + tt
                v.wait_ge(f_fe.h, 32 * (tb + 1))
                v.wait_ge(f_lx[gti % 2].h, (gti // 2 + 1) * 16)
                if gti >= 1:
                    v.wait_ge(f_oo.h, 16 * gti)
                f_ad.inc(v.tensor_tensor(out=s_row[:, :],
                                         in0=s_ff[:, tt * 4096:(tt + 1) * 4096],
                                         in1=s_xo2[gti % 2][:, :], op=ALU.add), 1)
                v.wait_ge(f_ad.h, f_ad.tot)
                for ch in range(8):
                    _v = v.bn_stats(s_bns[:, ch * 6:(ch + 1) * 6],
                               s_row[:, ch * 512:(ch + 1) * 512])
                ctx.chain(v, _v)
                _v = v.bn_aggr(s_mv[:, :], s_bns[:, :])
                ctx.chain(v, _v)
                f_st.inc(v.tensor_scalar(out=s_mv[:, 1:2], in0=s_mv[:, 1:2], scalar1=EPS,
                                         scalar2=None, op0=ALU.add), 1)
                v.wait_ge(f_sq.h, gti + 1)
                _v = v.reciprocal(s_rs[:, :], s_sd[:, :])
                ctx.chain(v, _v)
                _v = v.tensor_scalar(out=s_nmr[:, :], in0=s_mv[:, 0:1], scalar1=s_rs[:, 0:1],
                                scalar2=-1.0, op0=ALU.mult, op1=ALU.mult)
                ctx.chain(v, _v)
                f_ln.inc(v.tensor_scalar(out=s_row[:, :], in0=s_row[:, :],
                                         scalar1=s_rs[:, 0:1], scalar2=s_nmr[:, 0:1],
                                         op0=ALU.mult, op1=ALU.add), 1)

    return {"sync": body_sync, "tensor": body_tensor, "scalar": body_scalar,
            "vector": body_vector, "gpsimd": body_gpsimd}



PHASES_ORDER = ["A", "B1", "B2", "B3", "C", "D"]


def build_nc(upto="D", dev_outputs=()):
    nc = bass.Bass("TRN2", target_bir_lowering=False)

    def scratch(name, shape, dtype):
        kind = "ExternalOutput" if name in dev_outputs else "Internal"
        return nc.dram_tensor(name, shape, dtype, kind=kind).ap()

    io = {}
    io["i_xqT"] = nc.dram_tensor("i_xqT", [H, TOWN], F32, kind="ExternalInput").ap()
    io["i_xo"] = nc.dram_tensor("i_xo", [TOWN, H], F32, kind="ExternalInput").ap()
    io["i_wqkv"] = nc.dram_tensor("i_wqkv", [H, 3 * D], F32, kind="ExternalInput").ap()
    io["i_bqT"] = nc.dram_tensor("i_bqT", [128, 16], F32, kind="ExternalInput").ap()
    io["i_bo"] = nc.dram_tensor("i_bo", [H], F32, kind="ExternalInput").ap()
    io["i_bvT"] = nc.dram_tensor("i_bvT", [128, 8], F32, kind="ExternalInput").ap()
    io["i_wo"] = nc.dram_tensor("i_wo", [D, H], F32, kind="ExternalInput").ap()
    io["i_pg"] = nc.dram_tensor("i_pg", [2048, NI], U8, kind="ExternalInput").ap()
    io["i_pu"] = nc.dram_tensor("i_pu", [2048, NI], U8, kind="ExternalInput").ap()
    io["i_pd"] = nc.dram_tensor("i_pd", [NI // 2, H], U8, kind="ExternalInput").ap()
    io["i_sbg"] = nc.dram_tensor("i_sbg", [64, NI], F32, kind="ExternalInput").ap()
    io["i_zbg"] = nc.dram_tensor("i_zbg", [64, NI], F32, kind="ExternalInput").ap()
    io["i_sbu"] = nc.dram_tensor("i_sbu", [64, NI], F32, kind="ExternalInput").ap()
    io["i_zbu"] = nc.dram_tensor("i_zbu", [64, NI], F32, kind="ExternalInput").ap()
    io["i_sbd"] = nc.dram_tensor("i_sbd", [172, H], F32, kind="ExternalInput").ap()
    io["i_zbd"] = nc.dram_tensor("i_zbd", [172, H], F32, kind="ExternalInput").ap()

    io["sc_qT"] = scratch("sc_qT", [D, TOWN], BF16)
    io["sc_kTo"] = scratch("sc_kTo", [D, TOWN], BF16)
    io["sc_kag"] = scratch("sc_kag", [4 * D, TOWN], BF16)
    io["sc_vo"] = scratch("sc_vo", [TOWN, D], BF16)
    io["sc_vag"] = scratch("sc_vag", [S, D], BF16)
    io["sc_p"] = scratch("sc_p", [TOWN, S], BF16)
    io["sc_xmid"] = scratch("sc_xmid", [TOWN, H], F32)
    io["sc_xmT"] = scratch("sc_xmT", [32, 128, TOWN], BF16)
    io["sc_wd"] = scratch("sc_wd", [32, 128, 86 * 128], BF16)
    io["sc_hT"] = scratch("sc_hT", [86, 128, TOWN], BF16)
    io["out"] = nc.dram_tensor("out", [TOWN, H], F32, kind="ExternalOutput").ap()

    with contextlib.ExitStack() as ges:
        ctx = Ctx(nc, ges)
        io["ident"] = ges.enter_context(nc.sbuf_tensor("g_ident", [128, 128], BF16))
        io["rl"] = ges.enter_context(nc.sbuf_tensor("g_rl", [128, 16], F32))
        ctx.id_sem = ctx.sem("g_id")
        ctx.cs_k = ctx.sem("cs_k")
        ctx.cs_v = ctx.sem("cs_v")

        def mk_ag(src_key, dst_key, sto_need, cs):
            def hook(g):
                g.wait_ge(ctx.a_sto[0].h, sto_need)
                g.wait_ge(ctx.a_sto[1].h, sto_need)
                cs.inc(g.collective_compute(
                    "AllGather", ALU.bypass, replica_groups=RG,
                    ins=[io[src_key].opt()], outs=[io[dst_key].opt()]), 1)
            return hook

        with contextlib.ExitStack() as ph:
            em_a = emit_A(ctx, io, ph)
            hooks = em_a["xhooks"]
            hooks.setdefault(24, []).append(mk_ag("sc_kTo", "sc_kag", 64, ctx.cs_k))
            hooks.setdefault(44, []).append(mk_ag("sc_vo", "sc_vag", 128, ctx.cs_v))
            em_p = emit_P0(ctx, io, ph, hooks)

            def gp(g):
                _g = g.memset(io["ident"][:], 0.0)
                ctx.chain(g, _g)
                _g = g.affine_select(out=io["ident"][:], in_=io["ident"][:],
                                compare_op=ALU.not_equal, fill=1.0, base=0,
                                pattern=[[-1, 128]], channel_multiplier=1)
                ctx.id_sem.inc(_g, 1)
                em_p["gpsimd"](g)

            run_phase(ctx, "ph_A", {
                "sync": em_a["sync"], "scalar": em_a["scalar"], "tensor": em_a["tensor"],
                "gpsimd": gp, "vector": em_p["vector"]})

        if upto == "A":
            return nc, io

        with contextlib.ExitStack() as ph:
            em = emit_B1(ctx, io, ph)
            run_phase(ctx, "ph_B1", em)
        if upto == "B1":
            return nc, io
        with contextlib.ExitStack() as oter:
            s_ot = oter.enter_context(nc.sbuf_tensor("g_ot", [128, 8 * 2048], BF16))
            with contextlib.ExitStack() as ph:
                em = emit_B2(ctx, io, ph, s_ot)
                run_phase(ctx, "ph_B2", em)
            if upto != "B2":
                with contextlib.ExitStack() as ph:
                    em = emit_B3(ctx, io, ph, s_ot)
                    run_phase(ctx, "ph_B3", em)
        if upto in ("B2", "B3"):
            return nc, io
        with contextlib.ExitStack() as ph:
            em = emit_C(ctx, io, ph)
            run_phase(ctx, "ph_C", em)
        if upto == "C":
            return nc, io
        with contextlib.ExitStack() as ph:
            em = emit_D(ctx, io, ph)
            run_phase(ctx, "ph_D", em)
    return nc, io


def _host_inputs(inputs):
    x = np.ascontiguousarray(np.asarray(inputs["L0_attn_x"], dtype=np.float32))
    wqkv = np.ascontiguousarray(np.asarray(inputs["L0_attn_Wqkv"], dtype=np.float32))
    bqkv = np.ascontiguousarray(np.asarray(inputs["L0_attn_bqkv"], dtype=np.float32))
    wo = np.ascontiguousarray(np.asarray(inputs["L0_attn_Wo"], dtype=np.float32))
    bo = np.ascontiguousarray(np.asarray(inputs["L0_attn_bo"], dtype=np.float32))

    perm = np.concatenate([np.arange(0, NI, 2), np.arange(1, NI, 2)])

    def pack_bytes(wq_i32):
        return np.asarray(wq_i32, dtype=np.int32).view(np.uint8)[:, 0::4]

    pg = np.ascontiguousarray(pack_bytes(inputs["L0_ffn_gate_Wq"])[perm].T)
    pu = np.ascontiguousarray(pack_bytes(inputs["L0_ffn_up_Wq"])[perm].T)
    pd = np.ascontiguousarray(pack_bytes(inputs["L0_ffn_down_Wq"]).T)

    def tp(a):
        return np.ascontiguousarray(np.asarray(a, dtype=np.float32).T)

    sbg = tp(np.asarray(inputs["L0_ffn_gate_scale"])[perm])
    zbg = tp(np.asarray(inputs["L0_ffn_gate_zero"])[perm])
    sbu = tp(np.asarray(inputs["L0_ffn_up_scale"])[perm])
    zbu = tp(np.asarray(inputs["L0_ffn_up_zero"])[perm])
    sbd = tp(np.asarray(inputs["L0_ffn_down_scale"]))
    zbd = tp(np.asarray(inputs["L0_ffn_down_zero"]))

    bqT = np.ascontiguousarray(bqkv[0:2048].reshape(16, 128).T)
    bvT = np.ascontiguousarray(bqkv[2048:3072].reshape(8, 128).T)

    in_maps = []
    for c in range(NCORE):
        b, r = c // 4, c % 4
        in_maps.append(dict(
            i_xqT=np.ascontiguousarray(x[b, r * TOWN:(r + 1) * TOWN].T),
            i_xo=np.ascontiguousarray(x[b, r * TOWN:(r + 1) * TOWN]),
            i_wqkv=wqkv, i_bqT=bqT, i_bo=bo, i_bvT=bvT, i_wo=wo,
            i_pg=pg, i_pu=pu, i_pd=pd,
            i_sbg=sbg, i_zbg=zbg, i_sbu=sbu, i_zbu=zbu, i_sbd=sbd, i_zbd=zbd,
        ))
    return in_maps


def kernel(**inputs):
    in_maps = _host_inputs(inputs)
    nc, io = build_nc()
    res = run_bass_kernel_spmd(nc, in_maps, core_ids=list(range(NCORE)))
    out = np.empty((B, S, H), dtype=np.float32)
    for c in range(NCORE):
        b, r = c // 4, c % 4
        out[b, r * TOWN:(r + 1) * TOWN] = res.results[c]["out"]
    return out


# version 23 (final)
# speedup vs baseline: 1.3453x; 1.0052x over previous
import contextlib
import numpy as np
import concourse.bass as bass
import concourse.mybir as mybir
from concourse.bass_utils import run_bass_kernel_spmd

F32 = mybir.dt.float32
BF16 = mybir.dt.bfloat16
U8 = mybir.dt.uint8
I32 = mybir.dt.int32
I16 = mybir.dt.int16
AF = mybir.ActivationFunctionType
ALU = mybir.AluOpType
AX = mybir.AxisListType

B, S, H, D, NI = 2, 8192, 4096, 1024, 11008
TOWN = 2048
NCORE = 8
SOFT_SCALE = 0.03125
EPS = 1e-5
DQ_SLABS_D = [(0, 2048), (2048, 2048)]
RG = [[0, 1, 2, 3], [4, 5, 6, 7]]


class SemT:
    def __init__(self, nc, es, name):
        self.h = es.enter_context(nc.semaphore(name))
        self.tot = 0

    def inc(self, instr, n=16):
        instr.then_inc(self.h, n)
        self.tot += n
        return instr


class Ctx:
    def __init__(self, nc, es):
        self.nc = nc
        self.es = es
        self.bar = SemT(nc, es, "bar")
        self.sems = {}
        self.chain_sems = {}

    def sem(self, name):
        s = SemT(self.nc, self.es, name)
        self.sems[name] = s
        return s

    def chain(self, eng, ins):
        key = ins.ins.engine if hasattr(ins, "ins") else None
        name = f"ch_{key}"
        if name not in self.chain_sems:
            self.chain_sems[name] = SemT(self.nc, self.es, name)
        s = self.chain_sems[name]
        s.inc(ins, 1)
        eng.wait_ge(s.h, s.tot)


ALL_ENGINES = ("tensor", "vector", "scalar", "gpsimd", "sync")


def run_phase(ctx, name, emitters):
    nc = ctx.nc
    base = ctx.bar.tot
    ctx.bar.tot += len(ALL_ENGINES)
    tgt = ctx.bar.tot

    with nc.Block(name) as blk:
        for ename in ALL_ENGINES:
            fn = emitters.get(ename)

            def body(eng, fn=fn):
                if fn is not None:
                    fn(eng)
                eng.drain().then_inc(ctx.bar.h, 1)
                eng.wait_ge(ctx.bar.h, tgt)
            getattr(blk, ename)(body)




def emit_P0(ctx, io, bufs_es, hooks=None):
    nc = ctx.nc
    E = bufs_es.enter_context
    W = 2048
    NBP = 4
    s_pi = [E(nc.sbuf_tensor(f"p_pi{i}", [128, W], I16)) for i in range(NBP)]
    s_sb = [E(nc.sbuf_tensor(f"p_sb{i}", [128, W], BF16)) for i in range(NBP)]
    s_zb = [E(nc.sbuf_tensor(f"p_zb{i}", [128, W], BF16)) for i in range(NBP)]
    s_th = E(nc.sbuf_tensor("p_th", [128, W], I16))
    s_tl = E(nc.sbuf_tensor("p_tl", [128, W], I16))
    NW = 3
    s_wh = [E(nc.sbuf_tensor(f"p_wh{i}", [128, W], BF16)) for i in range(NW)]
    s_wl = [E(nc.sbuf_tensor(f"p_wl{i}", [128, W], BF16)) for i in range(NW)]

    ld = [ctx.sem(f"p_ld{i}") for i in range(NBP)]
    dqd = ctx.sem("p_dqd")
    wo = ctx.sem("p_wo")
    wos = ctx.sem("p_wos")
    WSPLIT = 86

    jobs = [
        (io["i_pd"], io["i_sbd"], io["i_zbd"], io["sc_wd"], 43, 43, DQ_SLABS_D, True),
    ]
    slabs = []
    for (pk, sbv, zbv, outw, njt, lofs, slablist, is_down) in jobs:
        for jt in range(njt):
            for (c0, w) in slablist:
                slabs.append((pk, sbv, zbv, outw, jt, lofs, c0, w, is_down))

    def emit_dq(eng, si):
        (pk, sbv, zbv, outw, jt, lofs, c0, w, is_down) = slabs[si]
        b = si % NBP
        bw = si % NW
        eng.wait_ge(ld[b].h, (si // NBP + 1) * 48)
        if si >= NW:
            k = si - NW
            if k < WSPLIT:
                eng.wait_ge(wo.h, 32 * (k + 1))
            else:
                eng.wait_ge(wos.h, 32 * (k - WSPLIT + 1))
        eng.tensor_scalar(out=s_th[:, 0:w], in0=s_pi[b][:, 0:w], scalar1=4,
                          scalar2=None, op0=ALU.logical_shift_right)
        _v = eng.tensor_scalar(out=s_tl[:, 0:w], in0=s_pi[b][:, 0:w], scalar1=15,
                               scalar2=None, op0=ALU.bitwise_and)
        ctx.chain(eng, _v)
        eng.tensor_tensor(out=s_wh[bw][:, 0:w], in0=s_th[:, 0:w],
                          in1=s_zb[b][:, 0:w], op=ALU.subtract)
        _v = eng.tensor_tensor(out=s_wl[bw][:, 0:w], in0=s_tl[:, 0:w],
                               in1=s_zb[b][:, 0:w], op=ALU.subtract)
        ctx.chain(eng, _v)
        eng.tensor_tensor(out=s_wh[bw][:, 0:w], in0=s_wh[bw][:, 0:w],
                          in1=s_sb[b][:, 0:w], op=ALU.mult)
        dqd.inc(eng.tensor_tensor(out=s_wl[bw][:, 0:w], in0=s_wl[bw][:, 0:w],
                                  in1=s_sb[b][:, 0:w], op=ALU.mult), 1)

    def emit_wout(g, si, sem):
        (pk, sbv, zbv, outw, jt, lofs, c0, w, is_down) = slabs[si]
        bw = si % NW
        nt, t0 = w // 128, c0 // 128
        g.wait_ge(dqd.h, si + 1)
        for src, rt in ((s_wh[bw], jt), (s_wl[bw], lofs + jt)):
            dst = outw[t0:t0 + nt, :, rt * 128:(rt + 1) * 128].rearrange("a p c -> p a c")
            sem.inc(g.dma_start(out=dst,
                                in_=src[:, 0:w].rearrange("p (a c) -> p a c", c=128)), 16)

    def body_gpsimd(g):
        for si, (pk, sbv, zbv, outw, jt, lofs, c0, w, is_down) in enumerate(slabs):
            for fn in (hooks or {}).get(si, []):
                fn(g)
            if si >= NBP:
                g.wait_ge(dqd.h, si - (NBP - 1))
            r0 = jt * 128
            g0 = jt * 4
            b = si % NBP
            ld[b].inc(g.dma_start(out=s_pi[b][:, 0:w], in_=pk[r0:r0 + 128, c0:c0 + w]), 16)
            ld[b].inc(g.dma_start(
                out=s_sb[b][:, 0:w],
                in_=sbv[g0:g0 + 4, c0:c0 + w].unsqueeze(1).broadcast_to([4, 32, w])), 16)
            ld[b].inc(g.dma_start(
                out=s_zb[b][:, 0:w],
                in_=zbv[g0:g0 + 4, c0:c0 + w].unsqueeze(1).broadcast_to([4, 32, w])), 16)
            if 2 <= si < WSPLIT + 2:
                emit_wout(g, si - 2, wo)
        for si in range(min(WSPLIT, len(slabs) - 2), min(WSPLIT, len(slabs))):
            emit_wout(g, si, wo)
        g.wait_ge(wo.h, wo.tot)

    def body_scalar_tail(sc):
        for si in range(WSPLIT, len(slabs)):
            emit_wout(sc, si, wos)
        sc.wait_ge(wos.h, wos.tot)

    def body_vector(v):
        for si in range(len(slabs)):
            emit_dq(v, si)

    return {"gpsimd": body_gpsimd, "vector": body_vector,
            "scalar_tail": body_scalar_tail}




def emit_A(ctx, io, bufs_es):
    nc = ctx.nc
    E = bufs_es.enter_context
    s_w = E(nc.sbuf_tensor("a_w", [128, 32 * 1024], BF16))
    s_wsl = [E(nc.sbuf_tensor(f"a_wsl{i}", [128, 2048], F32)) for i in range(2)]
    s_xb = [E(nc.sbuf_tensor(f"a_xb{i}", [128, 32 * 256], BF16)) for i in range(2)]
    s_st = [E(nc.sbuf_tensor(f"a_st{i}", [128, 2048], BF16)) for i in range(2)]
    s_bqT = E(nc.sbuf_tensor("a_bqT", [128, 16], F32))
    ps_mm = [E(nc.psum_tensor(f"a_ps{i}", [128, 2048], F32)) for i in range(2)]

    ld_b = ctx.sem("a_ldb")
    ld_w = [ctx.sem("a_ldw0"), ctx.sem("a_ldw1")]
    ld_x = ctx.sem("a_ldx")
    cv_w = ctx.sem("a_cvw")
    mm_d = ctx.sem("a_mmd")
    ev_d = ctx.sem("a_evd")
    st_o = [ctx.sem("a_sto0"), ctx.sem("a_sto1")]

    passes = [
        ("k", 1024, 8, None, io["sc_kTo"], 8),
        ("v", 2048, 8, None, None, None),
        ("q", 0, 8, None, io["sc_qT"], 0),
    ]
    nsl_cum = [0, 8, 16, 24]
    ctx.a_sto = st_o

    def body_sync(sync):
        ld_b.inc(sync.dma_start(out=s_bqT[:, :], in_=io["i_bqT"]), 16)
        for pi, (pname, wofs, nsl, xsrc, outT, bcol) in enumerate(passes):
            for i in range(16):
                gsi = pi * 16 + i
                if gsi >= 2:
                    sync.wait_ge(cv_w.h, gsi - 1)
                ld_w[gsi % 2].inc(sync.dma_start(
                    out=s_wsl[gsi % 2][:, :].rearrange("p (a f) -> p a f", a=2),
                    in_=io["i_wqkv"].rearrange("(a p) f -> p a f", p=128)[
                        :, 2 * i:2 * i + 2, wofs:wofs + 1024]), 16)

    def body_scalar(sc):
        for pi, (pname, wofs, nsl, xsrc, outT, bcol) in enumerate(passes):
            if pi >= 1:
                sc.wait_ge(mm_d.h, nsl_cum[pi])
            for i in range(16):
                gsi = pi * 16 + i
                sc.wait_ge(ld_w[gsi % 2].h, (gsi // 2 + 1) * 16)
                cv_w.inc(sc.activation(out=s_w[:, i * 2048:(i + 1) * 2048],
                              in_=s_wsl[gsi % 2][:, :], func=AF.Copy), 1)
            def emit_evac(i):
                gxi = nsl_cum[pi] + i
                sc.wait_ge(mm_d.h, gxi + 1)
                if gxi >= 2:
                    sc.wait_ge(st_o[gxi % 2].h, (gxi // 2) * 16)
                if pname == "v":
                    ins = sc.activation(out=s_st[i % 2][:, :], in_=ps_mm[i % 2][:, :],
                                           func=AF.Copy)
                    ev_d.inc(ins, 1)
                else:
                    for dt in range(8):
                        ins = sc.activation(out=s_st[i % 2][:, dt * 256:(dt + 1) * 256],
                                      in_=ps_mm[i % 2][:, dt * 256:(dt + 1) * 256],
                                      func=AF.Identity,
                                      bias=s_bqT[:, bcol + dt: bcol + dt + 1])
                    ev_d.inc(ins, 1)
                sc.wait_ge(ev_d.h, ev_d.tot)
                c0 = i * 256
                if pname == "v":
                    st_o[gxi % 2].inc(sc.dma_start(
                        out=io["sc_vo"][c0:c0 + 256, :].rearrange("(a p) d -> p a d", p=128),
                        in_=s_st[i % 2][:, :].rearrange("p (a d) -> p a d", a=2)), 16)
                else:
                    st_o[gxi % 2].inc(sc.dma_start(
                        out=outT.rearrange("(a p) t -> p a t", p=128)[:, :, c0:c0 + 256],
                        in_=s_st[i % 2][:, :].rearrange("p (a t) -> p a t", a=8)), 16)

            for i in range(nsl):
                if i >= 1:
                    emit_evac(i - 1)
            emit_evac(nsl - 1)
        sc.wait_ge(st_o[0].h, st_o[0].tot)
        sc.wait_ge(st_o[1].h, st_o[1].tot)

    def body_tensor(t):
        t.wait_ge(ld_b.h, 16)
        for pi, (pname, wofs, nsl, xsrc, outT, bcol) in enumerate(passes):
            for i in range(nsl):
                gxi = nsl_cum[pi] + i
                t.wait_ge(cv_w.h, (pi + 1) * 16)
                t.wait_ge(ld_x.h, 16 * (gxi + 1))
                if gxi >= 2:
                    t.wait_ge(ev_d.h, gxi - 1)
                ps = ps_mm[i % 2]
                xb = s_xb[i % 2]
                if pname == "v":
                    for tsub in range(2):
                        for a in range(32):
                            for dc in range(2):
                                o0 = tsub * 1024 + dc * 512
                                _i = t.matmul(ps[:, o0:o0 + 512],
                                         xb[:, a * 256 + tsub * 128: a * 256 + tsub * 128 + 128],
                                         s_w[:, a * 1024 + dc * 512: a * 1024 + (dc + 1) * 512],
                                         start=(a == 0), stop=(a == 31))
                else:
                    for dt in range(8):
                        for a in range(32):
                            _i = t.matmul(ps[:, dt * 256:(dt + 1) * 256],
                                     s_w[:, a * 1024 + dt * 128: a * 1024 + dt * 128 + 128],
                                     xb[:, a * 256:(a + 1) * 256],
                                     start=(a == 0), stop=(a == 31))
                mm_d.inc(_i, 1)

    def mk_xload(gxi):
        i = gxi % 8

        def f(g):
            if gxi >= 2:
                g.wait_ge(mm_d.h, gxi - 1)
            ld_x.inc(g.dma_start(
                out=s_xb[gxi % 2][:, :],
                in_=io["i_xq3"][i]), 16)
        return f

    xhooks = {}
    for gxi in range(24):
        xhooks.setdefault(min(2 * gxi, 84), []).append(mk_xload(gxi))

    return {"sync": body_sync, "scalar": body_scalar, "tensor": body_tensor,
            "xhooks": xhooks}





def emit_B1(ctx, io, bufs_es):
    nc = ctx.nc
    E = bufs_es.enter_context
    s_qt = [None, None]
    s_qt[0] = E(nc.sbuf_tensor("b_qt0", [128, 8 * 128], BF16))
    s_kT = E(nc.sbuf_tensor("b_kT", [128, 8 * 8192], BF16))
    s_qt[1] = E(nc.sbuf_tensor("b_qt1", [128, 8 * 128], BF16))
    s_sc = E(nc.sbuf_tensor("b_sc", [128, 8192], F32))
    s_pb = [E(nc.sbuf_tensor(f"b_pb{i}", [128, 8192], BF16)) for i in range(2)]
    s_mx = E(nc.sbuf_tensor("b_mx", [128, 16], F32))
    s_nb = E(nc.sbuf_tensor("b_nb", [128, 16], F32))
    s_l = E(nc.sbuf_tensor("b_l", [128, 16], F32))
    ps_s = [E(nc.psum_tensor(f"b_pss{i}", [128, 2048], F32)) for i in range(2)]

    b_lk = ctx.sem("b_lk")
    b_lq = [ctx.sem("b_lq0"), ctx.sem("b_lq1")]
    b_mm = ctx.sem("b_mm")
    b_ev = ctx.sem("b_ev")
    b_mx = ctx.sem("b_mx")
    b_ex = ctx.sem("b_ex")
    b_po = [ctx.sem("b_po0"), ctx.sem("b_po1")]
    b_rl = ctx.sem("b_rl")

    def body_sync(sync):
        sync.wait_ge(ctx.cs_k.h, 1)
        for r in range(4):
            b_lk.inc(sync.dma_start(
                out=s_kT[:, :].rearrange("p (a t) -> p a t", t=8192)[:, :, r * 2048:(r + 1) * 2048],
                in_=io["sc_kag"].rearrange("(r a p) t -> r p a t", r=4, p=128)[r]), 16)
        for qt in range(16):
            if qt >= 2:
                sync.wait_ge(b_mm.h, 4 * qt - 4)
            b_lq[qt % 2].inc(sync.dma_start(
                out=s_qt[qt % 2][:, :],
                in_=io["sc_qT"].rearrange("(a p) t -> p a t", p=128)[:, :, qt * 128:(qt + 1) * 128]), 16)

    def body_tensor(t):
        t.wait_ge(b_lk.h, 64)
        t.wait_ge(ctx.id_sem.h, 1)
        for qt in range(16):
            t.wait_ge(b_lq[qt % 2].h, (qt // 2 + 1) * 16)
            for su in range(4):
                gi = 4 * qt + su
                if gi >= 2:
                    t.wait_ge(b_ev.h, gi - 1)
                for dt in range(8):
                    for kc in range(4):
                        _i = t.matmul(ps_s[gi % 2][:, kc * 512:(kc + 1) * 512],
                                 s_qt[qt % 2][:, dt * 128:(dt + 1) * 128],
                                 s_kT[:, dt * 8192 + su * 2048 + kc * 512:
                                      dt * 8192 + su * 2048 + (kc + 1) * 512],
                                 start=(dt == 0), stop=(dt == 7))
                b_mm.inc(_i, 1)

    def body_scalar(sc):
        for qt in range(16):
            sc.wait_ge(b_mx.h, qt + 1)
            if qt >= 2:
                sc.wait_ge(b_po[qt % 2].h, (qt // 2) * 16)
            b_ex.inc(sc.activation(out=s_pb[qt % 2][:, :], in_=s_sc[:, :], func=AF.Exp,
                                   bias=s_nb[:, qt:qt + 1], scale=SOFT_SCALE,
                                   accum_out=s_l[:, qt:qt + 1]), 1)

    def body_vector(v):
        for qt in range(16):
            if qt >= 1:
                v.wait_ge(b_ex.h, qt)
            for su in range(4):
                gi = 4 * qt + su
                v.wait_ge(b_mm.h, gi + 1)
                _v = v.reduce_max(s_mx[:, 4 * (qt % 2) + su:4 * (qt % 2) + su + 1],
                                  ps_s[gi % 2][:, :], axis=AX.X)
                if su == 3:
                    ctx.chain(v, _v)
                b_ev.inc(v.tensor_copy(out=s_sc[:, su * 2048:(su + 1) * 2048],
                                       in_=ps_s[gi % 2][:, :]), 1)
            _v = v.reduce_max(s_nb[:, qt:qt + 1],
                              s_mx[:, 4 * (qt % 2):4 * (qt % 2) + 4], axis=AX.X)
            ctx.chain(v, _v)
            b_mx.inc(v.tensor_scalar(out=s_nb[:, qt:qt + 1], in0=s_nb[:, qt:qt + 1],
                                     scalar1=-SOFT_SCALE, scalar2=None, op0=ALU.mult), 1)
            v.wait_ge(b_ex.h, qt + 1)
            b_rl.inc(v.reciprocal(io["rl"][:, qt:qt + 1], s_l[:, qt:qt + 1]), 1)

    def body_gpsimd(g):
        for qt in range(16):
            g.wait_ge(b_ex.h, qt + 1)
            b_po[qt % 2].inc(g.dma_start(
                out=io["sc_p"][qt * 128:(qt + 1) * 128, :], in_=s_pb[qt % 2][:, :]), 16)
        g.wait_ge(b_po[0].h, b_po[0].tot)
        g.wait_ge(b_po[1].h, b_po[1].tot)

    return {"sync": body_sync, "tensor": body_tensor, "scalar": body_scalar,
            "vector": body_vector, "gpsimd": body_gpsimd}




def emit_B2(ctx, io, bufs_es, s_ot):
    nc = ctx.nc
    E = bufs_es.enter_context
    s_pl = [None, None]
    s_pl[0] = E(nc.sbuf_tensor("c_pl0", [128, 4096], BF16))
    s_v = E(nc.sbuf_tensor("c_v", [128, 64 * 1024], BF16))
    s_pl[1] = E(nc.sbuf_tensor("c_pl1", [128, 4096], BF16))
    s_o = E(nc.sbuf_tensor("c_o", [128, 1024], BF16))
    s_ptl = E(nc.sbuf_tensor("c_ptl", [128, 64 * 128], BF16))
    ps_o = [E(nc.psum_tensor(f"c_pso{i}", [128, 1024], F32)) for i in range(2)]
    ps_t = [E(nc.psum_tensor(f"c_pst{i}", [128, 1024], BF16)) for i in range(2)]
    ps_ot = E(nc.psum_tensor("c_psot", [128, 1024], BF16))

    c_lv = ctx.sem("c_lv")
    c_lp = [ctx.sem("c_lp0"), ctx.sem("c_lp1")]
    c_tp = ctx.sem("c_tp")
    c_pe = ctx.sem("c_pe")
    c_mm = ctx.sem("c_mm")
    c_ev = ctx.sem("c_ev")
    c_tpo = ctx.sem("c_tpo")
    c_oe = ctx.sem("c_oe")

    def body_sync(sync):
        sync.wait_ge(ctx.cs_v.h, 1)
        c_lv.inc(sync.dma_start(out=s_v[:, :],
                                in_=io["sc_vag"].rearrange("(a p) d -> p a d", p=128)), 16)
        for hh in range(32):
            if hh >= 2:
                sync.wait_ge(c_tp.h, 4 * hh - 4)
            c_lp[hh % 2].inc(sync.dma_start(
                out=s_pl[hh % 2][:, :],
                in_=io["sc_p"][(hh // 2) * 128:(hh // 2 + 1) * 128,
                               (hh % 2) * 4096:(hh % 2 + 1) * 4096]), 16)

    def body_tensor(t):
        t.wait_ge(c_lv.h, 16)
        for qt in range(16):
            if qt >= 1:
                t.wait_ge(c_mm.h, qt)
            for grp in range(8):
                gg = 8 * qt + grp
                hh = 2 * qt + grp // 4
                t.wait_ge(c_lp[hh % 2].h, (hh // 2 + 1) * 16)
                if gg >= 2:
                    t.wait_ge(c_pe.h, gg - 1)
                for j in range(8):
                    _i = t.transpose(ps_t[gg % 2][:, j * 128:(j + 1) * 128],
                                s_pl[hh % 2][:, ((grp % 4) * 8 + j) * 128:
                                             ((grp % 4) * 8 + j + 1) * 128],
                                io["ident"][:])
                c_tp.inc(_i, 1)
            t.wait_ge(c_pe.h, 8 * qt + 8)
            if qt >= 2:
                t.wait_ge(c_ev.h, qt - 1)
            for kt in range(64):
                for dc in range(2):
                    _i = t.matmul(ps_o[qt % 2][:, dc * 512:(dc + 1) * 512],
                             s_ptl[:, kt * 128:(kt + 1) * 128],
                             s_v[:, kt * 1024 + dc * 512: kt * 1024 + (dc + 1) * 512],
                             start=(kt == 0), stop=(kt == 63))
            c_mm.inc(_i, 1)
            t.wait_ge(c_ev.h, qt + 1)
            t.wait_ge(c_oe.h, qt)
            for j in range(8):
                _i = t.transpose(ps_ot[:, j * 128:(j + 1) * 128],
                            s_o[:, j * 128:(j + 1) * 128], io["ident"][:])
            c_tpo.inc(_i, 1)

    def body_scalar(sc):
        for qt in range(16):
            for grp in range(8):
                gg = 8 * qt + grp
                sc.wait_ge(c_tp.h, gg + 1)
                c_pe.inc(sc.activation(out=s_ptl[:, grp * 1024:(grp + 1) * 1024],
                                       in_=ps_t[gg % 2][:, :], func=AF.Copy), 1)
            sc.wait_ge(c_mm.h, qt + 1)
            if qt >= 1:
                sc.wait_ge(c_tpo.h, qt)
            c_ev.inc(sc.activation(out=s_o[:, :], in_=ps_o[qt % 2][:, :],
                                   func=AF.Identity, bias=0.0,
                                   scale=io["rl"][:, qt:qt + 1]), 1)
            sc.wait_ge(c_tpo.h, qt + 1)
            c_oe.inc(sc.activation(
                out=s_ot[:, :].rearrange("p (a q) -> p a q", q=2048)[:, :, qt * 128:(qt + 1) * 128],
                in_=ps_ot[:, :], func=AF.Copy), 1)

    return {"sync": body_sync, "tensor": body_tensor, "scalar": body_scalar}




def emit_B3(ctx, io, bufs_es, s_ot):
    nc = ctx.nc
    E = bufs_es.enter_context
    s_wo = E(nc.sbuf_tensor("d_wo", [128, 8 * 4096], BF16))
    s_wsl = [E(nc.sbuf_tensor(f"d_wsl{i}", [128, 1024], F32)) for i in range(2)]
    s_bo128 = E(nc.sbuf_tensor("d_bo128", [128, 4096], F32))
    s_bvT = E(nc.sbuf_tensor("d_bvT", [128, 8], BF16))
    s_ones = E(nc.sbuf_tensor("d_ones", [1, 128], BF16))
    s_xo = [E(nc.sbuf_tensor(f"d_xo{i}", [128, 4096], F32)) for i in range(2)]
    s_pr = E(nc.sbuf_tensor("d_pr", [128, 4096], F32))
    s_xe = E(nc.sbuf_tensor("d_xe", [128, 2048], BF16))
    s_xoo = E(nc.sbuf_tensor("d_xoo", [128, 2048], BF16))
    s_xms = E(nc.sbuf_tensor("d_xms", [128, 32 * 128], BF16))
    s_bor = s_pr[0:1, 0:4096]
    s_bob = s_xms[0:1, 0:4096]
    s_bx = s_xo
    s_xm = s_pr
    s_bns = E(nc.sbuf_tensor("d_bns", [128, 48], F32))
    s_mv = E(nc.sbuf_tensor("d_mv", [128, 2], F32))
    s_sd = E(nc.sbuf_tensor("d_sd", [128, 1], F32))
    s_rs = E(nc.sbuf_tensor("d_rs", [128, 1], F32))
    s_nmr = E(nc.sbuf_tensor("d_nmr", [128, 1], F32))
    ps_p = E(nc.psum_tensor("d_psp", [128, 2048], F32))
    ps_xt = [E(nc.psum_tensor(f"d_psxt{i}", [128, 512], BF16)) for i in range(2)]

    d_lb = ctx.sem("d_lb")
    d_lv = ctx.sem("d_lv")
    d_lw = [ctx.sem("d_lw0"), ctx.sem("d_lw1")]
    d_cw = ctx.sem("d_cw")
    d_on = ctx.sem("d_on")
    d_bm = ctx.sem("d_bm")
    d_bv = ctx.sem("d_bv")
    d_bc = ctx.sem("d_bc")
    d_bb = ctx.sem("d_bb")
    d_be = ctx.sem("d_be")
    d_lx = [ctx.sem("d_lx0"), ctx.sem("d_lx1")]
    d_bx = ctx.sem("d_bx")
    d_mm = ctx.sem("d_mm")
    d_pa = ctx.sem("d_pa")
    d_st = ctx.sem("d_st")
    d_sq = ctx.sem("d_sq")
    d_ln = ctx.sem("d_ln")
    d_xw = ctx.sem("d_xw")
    d_sp = ctx.sem("d_sp")
    d_tp = ctx.sem("d_tp")
    d_se = ctx.sem("d_se")
    d_so = ctx.sem("d_so")

    def body_sync(sync):
        d_lb.inc(sync.dma_start(out=s_bor[:, :], in_=io["i_bo"][None, :]), 16)
        for i in range(32):
            if i >= 2:
                sync.wait_ge(d_cw.h, i - 1)
            d_lw[i % 2].inc(sync.dma_start(
                out=s_wsl[i % 2][:, :],
                in_=io["i_wo"].rearrange("(a p) f -> a p f", p=128)[
                    i // 4, :, (i % 4) * 1024:(i % 4 + 1) * 1024]), 16)
        for tt in range(16):
            if tt >= 2:
                sync.wait_ge(d_bx.h, tt - 1)
            d_lx[tt % 2].inc(sync.dma_start(
                out=s_xo[tt % 2][:, :],
                in_=io["i_xo"][tt * 128:(tt + 1) * 128, :]), 16)
            sync.wait_ge(d_ln.h, tt + 1)
            d_xw.inc(sync.dma_start(
                out=io["sc_xmid"][tt * 128:(tt + 1) * 128, :], in_=s_pr[:, :]), 16)
        sync.wait_ge(d_xw.h, d_xw.tot)

    def body_scalar(sc):
        for i in range(32):
            sc.wait_ge(d_lw[i % 2].h, (i // 2 + 1) * 16)
            d_cw.inc(sc.activation(out=s_wo[:, i * 1024:(i + 1) * 1024],
                                   in_=s_wsl[i % 2][:, :], func=AF.Copy), 1)
        sc.wait_ge(d_bv.h, 2)
        d_bc.inc(sc.activation(out=s_bob[:, :], in_=s_bo128[0:1, :], func=AF.Copy), 1)
        for r in range(2):
            sc.wait_ge(d_bb.h, r + 1)
            d_be.inc(sc.activation(out=s_bo128[:, r * 2048:(r + 1) * 2048],
                                   in_=ps_p[:, :], func=AF.Copy), 1)
        for tt in range(16):
            sc.wait_ge(d_st.h, tt + 1)
            d_sq.inc(sc.activation(out=s_sd[:, :], in_=s_mv[:, 1:2],
                                   func=AF.Sqrt), 1)
            for grp in range(8):
                gg = 8 * tt + grp
                sc.wait_ge(d_tp.h, gg + 1)
                if grp == 0:
                    sc.wait_ge(d_so.h, 16 * tt)
                d_se.inc(sc.activation(out=s_xms[:, grp * 512:(grp + 1) * 512],
                                       in_=ps_xt[grp % 2][:, :], func=AF.Copy), 1)

    def body_tensor(t):
        t.wait_ge(d_cw.h, 32)
        t.wait_ge(d_lb.h, 16)
        t.wait_ge(d_lv.h, 16)
        for half in range(2):
            if half == 1:
                t.wait_ge(d_bv.h, 1)
            for cc in range(4):
                for dt in range(8):
                    _i = t.matmul(ps_p[0:1, cc * 512:(cc + 1) * 512],
                             s_bvT[:, dt:dt + 1],
                             s_wo[:, dt * 4096 + half * 2048 + cc * 512:
                                  dt * 4096 + half * 2048 + (cc + 1) * 512],
                             start=(dt == 0), stop=(dt == 7))
            d_bm.inc(_i, 1)
        t.wait_ge(d_bc.h, 1)
        t.wait_ge(d_on.h, 1)
        t.wait_ge(d_bv.h, 2)
        for r in range(2):
            if r == 1:
                t.wait_ge(d_be.h, 1)
            for cc in range(4):
                _i = t.matmul(ps_p[:, cc * 512:(cc + 1) * 512], s_ones[0:1, :],
                         s_bob[0:1, r * 2048 + cc * 512: r * 2048 + (cc + 1) * 512],
                         start=True, stop=True)
            d_bb.inc(_i, 1)
        for tt in range(16):
            for half in range(2):
                gi = 2 * tt + half
                t.wait_ge(d_be.h, 2) if gi == 0 else None
                if gi >= 1:
                    t.wait_ge(d_pa.h, gi)
                for dt in range(8):
                    for cc in range(4):
                        _i = t.matmul(ps_p[:, cc * 512:(cc + 1) * 512],
                                 s_ot[:, dt * 2048 + tt * 128: dt * 2048 + (tt + 1) * 128],
                                 s_wo[:, dt * 4096 + half * 2048 + cc * 512:
                                      dt * 4096 + half * 2048 + (cc + 1) * 512],
                                 start=(dt == 0), stop=(dt == 7))
                d_mm.inc(_i, 1)
            t.wait_ge(d_sp.h, tt + 1)
            for half in range(2):
                src = s_xe if half == 0 else s_xoo
                for grp in range(4):
                    gg = 8 * tt + half * 4 + grp
                    if gg >= 2:
                        t.wait_ge(d_se.h, gg - 1)
                    for j in range(4):
                        _i = t.transpose(ps_xt[gg % 2][:, j * 128:(j + 1) * 128],
                                    src[:, (grp * 4 + j) * 128:(grp * 4 + j + 1) * 128],
                                    io["ident"][:])
                    d_tp.inc(_i, 1)

    def body_vector(v):
        for half in range(2):
            v.wait_ge(d_bm.h, half + 1)
            v.wait_ge(d_lb.h, 16)
            d_bv.inc(v.tensor_tensor(out=s_bo128[0:1, half * 2048:(half + 1) * 2048],
                                     in0=ps_p[0:1, 0:2048],
                                     in1=s_bor[0:1, half * 2048:(half + 1) * 2048],
                                     op=ALU.add), 1)
        for tt in range(16):
            v.wait_ge(d_lx[tt % 2].h, (tt // 2 + 1) * 16)
            v.wait_ge(d_be.h, 2)
            if tt >= 1:
                v.wait_ge(d_pa.h, 2 * tt)
            d_bx.inc(v.tensor_tensor(out=s_xo[tt % 2][:, :], in0=s_bo128[:, :],
                                     in1=s_xo[tt % 2][:, :], op=ALU.add), 1)
            v.wait_ge(d_bx.h, d_bx.tot)
            for half in range(2):
                gi = 2 * tt + half
                v.wait_ge(d_mm.h, gi + 1)
                if half == 0 and tt >= 1:
                    v.wait_ge(d_xw.h, 16 * tt)
                d_pa.inc(v.tensor_tensor(out=s_pr[:, half * 2048:(half + 1) * 2048],
                                         in0=ps_p[:, :],
                                         in1=s_xo[tt % 2][:, half * 2048:(half + 1) * 2048],
                                         op=ALU.add), 1)
            v.wait_ge(d_pa.h, d_pa.tot)
            for ch in range(8):
                _v = v.bn_stats(s_bns[:, ch * 6:(ch + 1) * 6], s_pr[:, ch * 512:(ch + 1) * 512])
            ctx.chain(v, _v)
            _v = v.bn_aggr(s_mv[:, :], s_bns[:, :])
            ctx.chain(v, _v)
            d_st.inc(v.tensor_scalar(out=s_mv[:, 1:2], in0=s_mv[:, 1:2], scalar1=EPS,
                                     scalar2=None, op0=ALU.add), 1)
            v.wait_ge(d_sq.h, tt + 1)
            _v = v.reciprocal(s_rs[:, :], s_sd[:, :])
            ctx.chain(v, _v)
            _v = v.tensor_scalar(out=s_nmr[:, :], in0=s_mv[:, 0:1], scalar1=s_rs[:, 0:1],
                            scalar2=-1.0, op0=ALU.mult, op1=ALU.mult)
            ctx.chain(v, _v)
            d_ln.inc(v.tensor_scalar(out=s_pr[:, :], in0=s_pr[:, :], scalar1=s_rs[:, 0:1],
                                     scalar2=s_nmr[:, 0:1], op0=ALU.mult, op1=ALU.add), 1)
            v.wait_ge(d_ln.h, d_ln.tot)
            if tt >= 1:
                v.wait_ge(d_tp.h, 8 * tt)
            v.tensor_copy(out=s_xe[:, :], in_=s_pr[:, 0:4096:2])
            d_sp.inc(v.tensor_copy(out=s_xoo[:, :], in_=s_pr[:, 1:4096:2]), 1)

    def body_gpsimd(g):
        d_lv.inc(g.dma_start(out=s_bvT[:, :], in_=io["i_bvT"]), 16)
        d_on.inc(g.memset(s_ones[:, :], 1.0), 1)
        for tt in range(16):
            g.wait_ge(d_se.h, 8 * tt + 8)
            d_so.inc(g.dma_start(
                out=io["sc_xmT"][:, :, tt * 128:(tt + 1) * 128].rearrange("a p t -> p a t"),
                in_=s_xms[:, :].rearrange("p (a t) -> p a t", t=128)), 16)
        g.wait_ge(d_so.h, d_so.tot)

    return {"sync": body_sync, "scalar": body_scalar, "tensor": body_tensor,
            "vector": body_vector, "gpsimd": body_gpsimd}




def emit_C(ctx, io, bufs_es):
    nc = ctx.nc
    E = bufs_es.enter_context
    NB = 4
    s_w00 = E(nc.sbuf_tensor("e_w00", [128, 32 * 128], BF16))
    s_w10 = E(nc.sbuf_tensor("e_w10", [128, 32 * 128], BF16))
    s_xm = E(nc.sbuf_tensor("e_xm", [128, 32 * 1024], BF16))
    s_w01 = E(nc.sbuf_tensor("e_w01", [128, 32 * 128], BF16))
    s_w11 = E(nc.sbuf_tensor("e_w11", [128, 32 * 128], BF16))
    s_wt = [[s_w00, s_w01], [s_w10, s_w11]]
    s_pi = [E(nc.sbuf_tensor(f"e_pi{i}", [128, 2048], I16)) for i in range(NB)]
    s_th = E(nc.sbuf_tensor("e_th", [128, 2048], I16))
    s_tl = E(nc.sbuf_tensor("e_tl", [128, 2048], I16))
    s_sb = [E(nc.sbuf_tensor(f"e_sb{i}", [128, 2048], BF16)) for i in range(NB)]
    s_zb = [E(nc.sbuf_tensor(f"e_zb{i}", [128, 2048], BF16)) for i in range(NB)]
    s_sg = E(nc.sbuf_tensor("e_sg", [128, 1024], F32))
    s_ht = [E(nc.sbuf_tensor(f"e_ht{i}", [128, 1024], BF16)) for i in range(2)]
    ps_g = [E(nc.psum_tensor(f"e_psg{i}", [128, 1024], F32)) for i in range(2)]
    ps_u = [E(nc.psum_tensor(f"e_psu{i}", [128, 1024], F32)) for i in range(2)]

    e_lx = ctx.sem("e_lx")
    e_ld = [ctx.sem(f"e_ld{i}") for i in range(NB)]
    e_dq = ctx.sem("e_dq")
    e_mm = ctx.sem("e_mm")
    e_sl = ctx.sem("e_sl")
    e_hv = ctx.sem("e_hv")
    e_ho = ctx.sem("e_ho")

    MATIN = [(io["i_pg"], io["i_sbg"], io["i_zbg"]),
             (io["i_pu"], io["i_sbu"], io["i_zbu"])]

    def body_gpsimd2(g):
        for dqg in range(344):
            it, mat = (dqg % 172) // 2, dqg % 2
            b = dqg % NB
            if dqg >= NB:
                g.wait_ge(e_dq.h, dqg - (NB - 1))
            pk, sbv, zbv = MATIN[mat]
            e_ld[b].inc(g.dma_start(
                out=s_pi[b][:, :],
                in_=pk.rearrange("(a p) n -> p a n", p=128)[:, :, it * 128:(it + 1) * 128]), 16)
            for src, dst in ((sbv, s_sb[b]), (zbv, s_zb[b])):
                for g0 in range(4):
                    e_ld[b].inc(g.dma_start(
                        out=dst[g0 * 32:(g0 + 1) * 32, :],
                        in_=src.rearrange("(j g) n -> g j n", g=4)[g0, :, it * 128:(it + 1) * 128]
                            .unsqueeze(0).broadcast_to([32, 16, 128])), 16)

    def body_sync(sync):
        for pss in range(2):
            if pss == 1:
                sync.wait_ge(e_mm.h, 172)
            e_lx.inc(sync.dma_start(
                out=s_xm[:, :],
                in_=io["sc_xmT"][:, :, pss * 1024:(pss + 1) * 1024]
                    .rearrange("a p t -> p a t")), 16)
            for it in range(86):
                git = pss * 86 + it
                sync.wait_ge(e_hv.h, git + 1)
                e_ho.inc(sync.dma_start(
                    out=io["sc_hT"][it][:, pss * 1024:(pss + 1) * 1024],
                    in_=s_ht[git % 2][:, :]), 16)
        sync.wait_ge(e_ho.h, e_ho.tot)

    def body_vector(v):
        for pss in range(2):
            for it in range(86):
                git = pss * 86 + it
                for mat in range(2):
                    dqg = 2 * git + mat
                    b = dqg % NB
                    wb = git % 2
                    v.wait_ge(e_ld[b].h, (dqg // NB + 1) * 144)
                    if dqg >= 4:
                        v.wait_ge(e_mm.h, dqg - 3)
                    hi = s_wt[mat][wb][:, 0:2048]
                    lo = s_wt[mat][wb][:, 2048:4096]
                    v.tensor_scalar(out=s_th[:, :], in0=s_pi[b][:, :], scalar1=4,
                                    scalar2=None, op0=ALU.logical_shift_right)
                    _v = v.tensor_scalar(out=s_tl[:, :], in0=s_pi[b][:, :], scalar1=15,
                                         scalar2=None, op0=ALU.bitwise_and)
                    ctx.chain(v, _v)
                    v.tensor_tensor(out=hi, in0=s_th[:, :], in1=s_zb[b][:, :],
                                    op=ALU.subtract)
                    _v = v.tensor_tensor(out=lo, in0=s_tl[:, :], in1=s_zb[b][:, :],
                                         op=ALU.subtract)
                    ctx.chain(v, _v)
                    v.tensor_tensor(out=hi, in0=hi, in1=s_sb[b][:, :], op=ALU.mult)
                    e_dq.inc(v.tensor_tensor(out=lo, in0=lo, in1=s_sb[b][:, :],
                                             op=ALU.mult), 1)
                if git >= 1:
                    emit_hmult(v, git - 1)
            if pss == 1:
                emit_hmult(v, 171)

    def emit_hmult(v, git):
        v.wait_ge(e_mm.h, 2 * git + 2)
        v.wait_ge(e_sl.h, git + 1)
        if git >= 2:
            v.wait_ge(e_ho.h, 16 * (git - 1))
        e_hv.inc(v.tensor_tensor(out=s_ht[git % 2][:, :], in0=s_sg[:, :],
                                 in1=ps_u[git % 2][:, :], op=ALU.mult), 1)

    def body_tensor(t):
        for pss in range(2):
            t.wait_ge(e_lx.h, 16 * (pss + 1))
            for it in range(86):
                git = pss * 86 + it
                wb = git % 2
                for mat in range(2):
                    dqg = 2 * git + mat
                    t.wait_ge(e_dq.h, dqg + 1)
                    if git >= 2:
                        t.wait_ge((e_sl if mat == 0 else e_hv).h, git - 1)
                    ps = (ps_g if mat == 0 else ps_u)[git % 2]
                    for rt in range(32):
                        for tc in range(2):
                            _i = t.matmul(ps[:, tc * 512:(tc + 1) * 512],
                                     s_wt[mat][wb][:, rt * 128:(rt + 1) * 128],
                                     s_xm[:, rt * 1024 + tc * 512:
                                          rt * 1024 + (tc + 1) * 512],
                                     start=(rt == 0), stop=(rt == 31))
                    e_mm.inc(_i, 1)

    def body_scalar(sc):
        for git in range(172):
            sc.wait_ge(e_mm.h, 2 * git + 1)
            if git >= 1:
                sc.wait_ge(e_hv.h, git)
            e_sl.inc(sc.activation(out=s_sg[:, :], in_=ps_g[git % 2][:, :],
                                   func=AF.Silu), 1)

    return {"sync": body_sync, "tensor": body_tensor, "scalar": body_scalar,
            "vector": body_vector, "gpsimd": body_gpsimd2}




def emit_D(ctx, io, bufs_es):
    nc = ctx.nc
    E = bufs_es.enter_context
    s_hq = E(nc.sbuf_tensor("f_hq", [128, 86 * 512], BF16))
    s_wdh = [None] * 4
    s_wdh[0] = E(nc.sbuf_tensor("f_wd0", [128, 43 * 128], BF16))
    s_xo2 = [E(nc.sbuf_tensor("f_xo0", [128, 4096], F32))] * 2
    s_wdh[1] = E(nc.sbuf_tensor("f_wd1", [128, 43 * 128], BF16))
    s_row = E(nc.sbuf_tensor("f_row", [128, 4096], F32))
    s_wdh[2] = E(nc.sbuf_tensor("f_wd2", [128, 43 * 128], BF16))
    s_ff = E(nc.sbuf_tensor("f_ff", [128, 4 * 4096], BF16))
    s_wdh[3] = E(nc.sbuf_tensor("f_wd3", [128, 43 * 128], BF16))
    s_bns = E(nc.sbuf_tensor("f_bns", [128, 48], F32))
    s_mv = E(nc.sbuf_tensor("f_mv", [128, 2], F32))
    s_sd = E(nc.sbuf_tensor("f_sd", [128, 1], F32))
    s_rs = E(nc.sbuf_tensor("f_rs", [128, 1], F32))
    s_nmr = E(nc.sbuf_tensor("f_nmr", [128, 1], F32))
    s_tmp = [E(nc.sbuf_tensor(f"f_tmp{i}", [128, 512], BF16)) for i in range(2)]
    ps_d = [E(nc.psum_tensor(f"f_psd{i}", [128, 512], F32)) for i in range(2)]
    ps_t2 = [E(nc.psum_tensor(f"f_pst{i}", [128, 512], BF16)) for i in range(2)]

    f_lh = ctx.sem("f_lh")
    f_lw = [ctx.sem(f"f_lw{i}") for i in range(4)]
    f_mm = ctx.sem("f_mm")
    f_ev = ctx.sem("f_ev")
    f_tp = ctx.sem("f_tp")
    f_fe = ctx.sem("f_fe")
    f_lx = [ctx.sem("f_lx0"), ctx.sem("f_lx1")]
    f_ad = ctx.sem("f_ad")
    f_st = ctx.sem("f_st")
    f_sq = ctx.sem("f_sq")
    f_ln = ctx.sem("f_ln")
    f_oo = ctx.sem("f_oo")

    def emit_wload(sync, hgi):
        ht, half = (hgi % 64) // 2, hgi % 2
        if hgi >= 4:
            sync.wait_ge(f_mm.h, hgi - 3)
        f_lw[hgi % 4].inc(sync.dma_start(
            out=s_wdh[hgi % 4][:, :],
            in_=io["sc_wd"][ht][:, half * 5504:(half + 1) * 5504]), 16)

    def body_sync(sync):
        for tb in range(4):
            if tb == 0:
                for hgi in range(3):
                    emit_wload(sync, hgi)
            if tb >= 1:
                sync.wait_ge(f_mm.h, 64 * tb)
            for hf in range(2):
                for aq in range(2):
                    r0, r1 = (0, 21) if aq == 0 else (21, 43)
                    f_lh.inc(sync.dma_start(
                        out=s_hq[:, (hf * 43 + r0) * 512:(hf * 43 + r1) * 512]
                            .rearrange("p (a t) -> p a t", t=512),
                        in_=io["sc_hT"][hf * 43 + r0:hf * 43 + r1, :, tb * 512:(tb + 1) * 512]
                            .rearrange("a p t -> p a t")), 16)
            for hgi in range(64 * tb + 3, 64 * (tb + 1) + (3 if tb < 3 else 0)):
                emit_wload(sync, hgi)

    def body_gpsimd(g):
        for tb in range(4):
            for tt in range(4):
                gti = 4 * tb + tt
                if gti >= 1:
                    g.wait_ge(f_ad.h, gti)
                f_lx[gti % 2].inc(g.dma_start(
                    out=s_xo2[gti % 2][:, :],
                    in_=io["sc_xmid"][gti * 128:(gti + 1) * 128, :]), 16)
                g.wait_ge(f_ln.h, gti + 1)
                f_oo.inc(g.dma_start(
                    out=io["out"][gti * 128:(gti + 1) * 128, :], in_=s_row[:, :]), 16)
        g.wait_ge(f_oo.h, f_oo.tot)

    def emit_tr(t, ghi):
        t.wait_ge(f_ev.h, ghi + 1)
        if ghi >= 2:
            t.wait_ge(f_fe.h, ghi - 1)
        for j in range(4):
            _i = t.transpose(ps_t2[ghi % 2][:, j * 128:(j + 1) * 128],
                        s_tmp[ghi % 2][:, j * 128:(j + 1) * 128], io["ident"][:])
        f_tp.inc(_i, 1)

    def body_tensor(t):
        for tb in range(4):
            t.wait_ge(f_lh.h, 64 * (tb + 1))
            for ht in range(32):
                ghi = 32 * tb + ht
                for half in range(2):
                    hgi = 2 * ghi + half
                    t.wait_ge(f_lw[hgi % 4].h, (hgi // 4 + 1) * 16)
                    if half == 0 and ghi >= 2:
                        t.wait_ge(f_ev.h, ghi - 1)
                    for rt in range(43):
                        _i = t.matmul(ps_d[ghi % 2][:, :],
                                 s_wdh[hgi % 4][:, rt * 128:(rt + 1) * 128],
                                 s_hq[:, (half * 43 + rt) * 512:(half * 43 + rt + 1) * 512],
                                 start=(half == 0 and rt == 0), stop=(half == 1 and rt == 42))
                    f_mm.inc(_i, 1)
                if ghi >= 1:
                    emit_tr(t, ghi - 1)
        emit_tr(t, 127)

    def body_scalar(sc):
        for tb in range(4):
            for ht in range(32):
                ghi = 32 * tb + ht
                sc.wait_ge(f_mm.h, 2 * ghi + 2)
                if ghi >= 2:
                    sc.wait_ge(f_tp.h, ghi - 1)
                f_ev.inc(sc.activation(out=s_tmp[ghi % 2][:, :], in_=ps_d[ghi % 2][:, :],
                                       func=AF.Copy), 1)
                sc.wait_ge(f_tp.h, ghi + 1)
                if ht == 0:
                    sc.wait_ge(f_ad.h, 4 * tb)
                f_fe.inc(sc.activation(
                    out=s_ff[:, :].rearrange("p (a c) -> p a c", c=4096)[:, :, ht * 128:(ht + 1) * 128],
                    in_=ps_t2[ghi % 2][:, :], func=AF.Copy), 1)
            for tt in range(4):
                gti = 4 * tb + tt
                sc.wait_ge(f_st.h, gti + 1)
                f_sq.inc(sc.activation(out=s_sd[:, :], in_=s_mv[:, 1:2],
                                       func=AF.Sqrt), 1)

    def body_vector(v):
        for tb in range(4):
            for tt in range(4):
                gti = 4 * tb + tt
                v.wait_ge(f_fe.h, 32 * (tb + 1))
                v.wait_ge(f_lx[gti % 2].h, (gti // 2 + 1) * 16)
                if gti >= 1:
                    v.wait_ge(f_oo.h, 16 * gti)
                f_ad.inc(v.tensor_tensor(out=s_row[:, :],
                                         in0=s_ff[:, tt * 4096:(tt + 1) * 4096],
                                         in1=s_xo2[gti % 2][:, :], op=ALU.add), 1)
                v.wait_ge(f_ad.h, f_ad.tot)
                for ch in range(8):
                    _v = v.bn_stats(s_bns[:, ch * 6:(ch + 1) * 6],
                               s_row[:, ch * 512:(ch + 1) * 512])
                ctx.chain(v, _v)
                _v = v.bn_aggr(s_mv[:, :], s_bns[:, :])
                ctx.chain(v, _v)
                f_st.inc(v.tensor_scalar(out=s_mv[:, 1:2], in0=s_mv[:, 1:2], scalar1=EPS,
                                         scalar2=None, op0=ALU.add), 1)
                v.wait_ge(f_sq.h, gti + 1)
                _v = v.reciprocal(s_rs[:, :], s_sd[:, :])
                ctx.chain(v, _v)
                _v = v.tensor_scalar(out=s_nmr[:, :], in0=s_mv[:, 0:1], scalar1=s_rs[:, 0:1],
                                scalar2=-1.0, op0=ALU.mult, op1=ALU.mult)
                ctx.chain(v, _v)
                f_ln.inc(v.tensor_scalar(out=s_row[:, :], in0=s_row[:, :],
                                         scalar1=s_rs[:, 0:1], scalar2=s_nmr[:, 0:1],
                                         op0=ALU.mult, op1=ALU.add), 1)

    return {"sync": body_sync, "tensor": body_tensor, "scalar": body_scalar,
            "vector": body_vector, "gpsimd": body_gpsimd}



PHASES_ORDER = ["A", "B1", "B2", "B3", "C", "D"]


def build_nc(upto="D", dev_outputs=()):
    nc = bass.Bass("TRN2", target_bir_lowering=False)

    def scratch(name, shape, dtype):
        kind = "ExternalOutput" if name in dev_outputs else "Internal"
        return nc.dram_tensor(name, shape, dtype, kind=kind).ap()

    io = {}
    io["i_xq3"] = nc.dram_tensor("i_xq3", [8, 128, 32 * 256], F32, kind="ExternalInput").ap()
    io["i_xo"] = nc.dram_tensor("i_xo", [TOWN, H], F32, kind="ExternalInput").ap()
    io["i_wqkv"] = nc.dram_tensor("i_wqkv", [H, 3 * D], F32, kind="ExternalInput").ap()
    io["i_bqT"] = nc.dram_tensor("i_bqT", [128, 16], F32, kind="ExternalInput").ap()
    io["i_bo"] = nc.dram_tensor("i_bo", [H], F32, kind="ExternalInput").ap()
    io["i_bvT"] = nc.dram_tensor("i_bvT", [128, 8], F32, kind="ExternalInput").ap()
    io["i_wo"] = nc.dram_tensor("i_wo", [D, H], F32, kind="ExternalInput").ap()
    io["i_pg"] = nc.dram_tensor("i_pg", [2048, NI], U8, kind="ExternalInput").ap()
    io["i_pu"] = nc.dram_tensor("i_pu", [2048, NI], U8, kind="ExternalInput").ap()
    io["i_pd"] = nc.dram_tensor("i_pd", [NI // 2, H], U8, kind="ExternalInput").ap()
    io["i_sbg"] = nc.dram_tensor("i_sbg", [64, NI], F32, kind="ExternalInput").ap()
    io["i_zbg"] = nc.dram_tensor("i_zbg", [64, NI], F32, kind="ExternalInput").ap()
    io["i_sbu"] = nc.dram_tensor("i_sbu", [64, NI], F32, kind="ExternalInput").ap()
    io["i_zbu"] = nc.dram_tensor("i_zbu", [64, NI], F32, kind="ExternalInput").ap()
    io["i_sbd"] = nc.dram_tensor("i_sbd", [172, H], F32, kind="ExternalInput").ap()
    io["i_zbd"] = nc.dram_tensor("i_zbd", [172, H], F32, kind="ExternalInput").ap()

    io["sc_qT"] = scratch("sc_qT", [D, TOWN], BF16)
    io["sc_kTo"] = scratch("sc_kTo", [D, TOWN], BF16)
    io["sc_kag"] = scratch("sc_kag", [4 * D, TOWN], BF16)
    io["sc_vo"] = scratch("sc_vo", [TOWN, D], BF16)
    io["sc_vag"] = scratch("sc_vag", [S, D], BF16)
    io["sc_p"] = scratch("sc_p", [TOWN, S], BF16)
    io["sc_xmid"] = scratch("sc_xmid", [TOWN, H], F32)
    io["sc_xmT"] = scratch("sc_xmT", [32, 128, TOWN], BF16)
    io["sc_wd"] = scratch("sc_wd", [32, 128, 86 * 128], BF16)
    io["sc_hT"] = scratch("sc_hT", [86, 128, TOWN], BF16)
    io["out"] = nc.dram_tensor("out", [TOWN, H], F32, kind="ExternalOutput").ap()

    with contextlib.ExitStack() as ges:
        ctx = Ctx(nc, ges)
        io["ident"] = ges.enter_context(nc.sbuf_tensor("g_ident", [128, 128], BF16))
        io["rl"] = ges.enter_context(nc.sbuf_tensor("g_rl", [128, 16], F32))
        ctx.id_sem = ctx.sem("g_id")
        ctx.cs_k = ctx.sem("cs_k")
        ctx.cs_v = ctx.sem("cs_v")

        def mk_ag(src_key, dst_key, sto_need, cs):
            def hook(g):
                g.wait_ge(ctx.a_sto[0].h, sto_need)
                g.wait_ge(ctx.a_sto[1].h, sto_need)
                cs.inc(g.collective_compute(
                    "AllGather", ALU.bypass, replica_groups=RG,
                    ins=[io[src_key].opt()], outs=[io[dst_key].opt()]), 1)
            return hook

        with contextlib.ExitStack() as ph:
            em_a = emit_A(ctx, io, ph)
            hooks = em_a["xhooks"]
            hooks.setdefault(24, []).append(mk_ag("sc_kTo", "sc_kag", 64, ctx.cs_k))
            hooks.setdefault(44, []).append(mk_ag("sc_vo", "sc_vag", 128, ctx.cs_v))
            em_p = emit_P0(ctx, io, ph, hooks)

            def gp(g):
                _g = g.memset(io["ident"][:], 0.0)
                ctx.chain(g, _g)
                _g = g.affine_select(out=io["ident"][:], in_=io["ident"][:],
                                compare_op=ALU.not_equal, fill=1.0, base=0,
                                pattern=[[-1, 128]], channel_multiplier=1)
                ctx.id_sem.inc(_g, 1)
                em_p["gpsimd"](g)

            def sc_both(sc):
                em_a["scalar"](sc)
                em_p["scalar_tail"](sc)

            run_phase(ctx, "ph_A", {
                "sync": em_a["sync"], "scalar": sc_both, "tensor": em_a["tensor"],
                "gpsimd": gp, "vector": em_p["vector"]})

        if upto == "A":
            return nc, io

        with contextlib.ExitStack() as ph:
            em = emit_B1(ctx, io, ph)
            run_phase(ctx, "ph_B1", em)
        if upto == "B1":
            return nc, io
        with contextlib.ExitStack() as oter:
            s_ot = oter.enter_context(nc.sbuf_tensor("g_ot", [128, 8 * 2048], BF16))
            with contextlib.ExitStack() as ph:
                em = emit_B2(ctx, io, ph, s_ot)
                run_phase(ctx, "ph_B2", em)
            if upto != "B2":
                with contextlib.ExitStack() as ph:
                    em = emit_B3(ctx, io, ph, s_ot)
                    run_phase(ctx, "ph_B3", em)
        if upto in ("B2", "B3"):
            return nc, io
        with contextlib.ExitStack() as ph:
            em = emit_C(ctx, io, ph)
            run_phase(ctx, "ph_C", em)
        if upto == "C":
            return nc, io
        with contextlib.ExitStack() as ph:
            em = emit_D(ctx, io, ph)
            run_phase(ctx, "ph_D", em)
    return nc, io


def _host_inputs(inputs):
    x = np.ascontiguousarray(np.asarray(inputs["L0_attn_x"], dtype=np.float32))
    wqkv = np.ascontiguousarray(np.asarray(inputs["L0_attn_Wqkv"], dtype=np.float32))
    bqkv = np.ascontiguousarray(np.asarray(inputs["L0_attn_bqkv"], dtype=np.float32))
    wo = np.ascontiguousarray(np.asarray(inputs["L0_attn_Wo"], dtype=np.float32))
    bo = np.ascontiguousarray(np.asarray(inputs["L0_attn_bo"], dtype=np.float32))

    perm = np.concatenate([np.arange(0, NI, 2), np.arange(1, NI, 2)])

    def pack_bytes(wq_i32):
        return np.asarray(wq_i32, dtype=np.int32).view(np.uint8)[:, 0::4]

    pg = np.ascontiguousarray(pack_bytes(inputs["L0_ffn_gate_Wq"])[perm].T)
    pu = np.ascontiguousarray(pack_bytes(inputs["L0_ffn_up_Wq"])[perm].T)
    pd = np.ascontiguousarray(pack_bytes(inputs["L0_ffn_down_Wq"]).T)

    def tp(a):
        return np.ascontiguousarray(np.asarray(a, dtype=np.float32).T)

    sbg = tp(np.asarray(inputs["L0_ffn_gate_scale"])[perm])
    zbg = tp(np.asarray(inputs["L0_ffn_gate_zero"])[perm])
    sbu = tp(np.asarray(inputs["L0_ffn_up_scale"])[perm])
    zbu = tp(np.asarray(inputs["L0_ffn_up_zero"])[perm])
    sbd = tp(np.asarray(inputs["L0_ffn_down_scale"]))
    zbd = tp(np.asarray(inputs["L0_ffn_down_zero"]))

    bqT = np.ascontiguousarray(bqkv[0:2048].reshape(16, 128).T)
    bvT = np.ascontiguousarray(bqkv[2048:3072].reshape(8, 128).T)

    in_maps = []
    for c in range(NCORE):
        b, r = c // 4, c % 4
        in_maps.append(dict(
            i_xq3=np.ascontiguousarray(
                x[b, r * TOWN:(r + 1) * TOWN].T.reshape(32, 128, 8, 256)
                .transpose(2, 1, 0, 3).reshape(8, 128, 8192)),
            i_xo=np.ascontiguousarray(x[b, r * TOWN:(r + 1) * TOWN]),
            i_wqkv=wqkv, i_bqT=bqT, i_bo=bo, i_bvT=bvT, i_wo=wo,
            i_pg=pg, i_pu=pu, i_pd=pd,
            i_sbg=sbg, i_zbg=zbg, i_sbu=sbu, i_zbu=zbu, i_sbd=sbd, i_zbd=zbd,
        ))
    return in_maps


def kernel(**inputs):
    in_maps = _host_inputs(inputs)
    nc, io = build_nc()
    res = run_bass_kernel_spmd(nc, in_maps, core_ids=list(range(NCORE)))
    out = np.empty((B, S, H), dtype=np.float32)
    for c in range(NCORE):
        b, r = c // 4, c % 4
        out[b, r * TOWN:(r + 1) * TOWN] = res.results[c]["out"]
    return out
